# version 14
# baseline (speedup 1.0000x reference)
import numpy as np
import ml_dtypes
from contextlib import ExitStack

import concourse.bass as bass
import concourse.mybir as mybir
from concourse.bass_utils import run_bass_kernel_spmd
from concourse.alu_op_type import AluOpType

F32 = mybir.dt.float32
BF16 = mybir.dt.bfloat16
U32 = mybir.dt.uint32
I32 = mybir.dt.int32

def _act_set_id(name):
    from concourse.hw_specs import get_activation_tables
    import concourse.bass as _b
    _nc = _b.Bass("TRN2", target_bir_lowering=False)
    return list(get_activation_tables(_nc.m.arch)).index(name)

LNEXP_SET = None
SILU_SET = None
AF = mybir.ActivationFunctionType
AX = mybir.AxisListType.X

H = 8
DH = 128
DM = 1024
DC = 8
HF = 1408
HC = 11
NE = 9
EPS_LN = 1e-5
EPS_RMS = 1e-6
NEG_BIG = -1.0e30

W13_SL = [(h, min(h + 2, HC)) for h in range(0, HC, 2)]
W2_GR = [(0, 3), (3, 6), (6, 9)]


class Ctr:

    def __init__(self, ctx, nc, name):
        self.h = ctx.enter_context(nc.semaphore(name))
        self.n = 0

    def inc(self, instr, k=1):
        instr.then_inc(self.h, k)
        self.n += k
        return instr

    def wait(self, eng, n=None):
        eng.wait_ge(self.h, self.n if n is None else n)


def build_nc(T, debug=False):
    from concourse.hw_specs import get_activation_tables
    NT = T // 128
    NQ = T // 512
    nc = bass.Bass("TRN2", target_bir_lowering=False, debug=debug)
    tab_names = list(get_activation_tables(nc.m.arch))
    LNEXP_SET = tab_names.index("natural_log_exp_and_others")
    SILU_SET = tab_names.index("silu_and_others")

    x_in = nc.dram_tensor("x", [T, DM], F32, kind="ExternalInput").ap()
    wq_in = nc.dram_tensor("wq", [DM, DM], F32, kind="ExternalInput").ap()
    wk_in = nc.dram_tensor("wk", [DM, DM], F32, kind="ExternalInput").ap()
    wv_in = nc.dram_tensor("wv", [DM, DM], F32, kind="ExternalInput").ap()
    wo_in = nc.dram_tensor("wo", [H, DH, DM], F32, kind="ExternalInput").ap()
    w13_in = nc.dram_tensor("w13", [NE, HC, 128, 2, DC, 128], F32, kind="ExternalInput").ap()
    w2_in = nc.dram_tensor("w2", [NE, 128, HC, DC, 128], F32, kind="ExternalInput").ap()
    b13_in = nc.dram_tensor("b13", [128, NE, HC, 2], F32, kind="ExternalInput").ap()
    b2_in = nc.dram_tensor("b2", [8, DM], F32, kind="ExternalInput").ap()
    sb2_in = nc.dram_tensor("sb2c", [128, DC], F32, kind="ExternalInput").ap()
    gwt_in = nc.dram_tensor("gwt", [128, DC, 8], F32, kind="ExternalInput").ap()
    gbb_in = nc.dram_tensor("gbb", [128, 8], F32, kind="ExternalInput").ap()
    cos_in = nc.dram_tensor("cosT", [NT, 128, 64], BF16, kind="ExternalInput").ap()
    sin_in = nc.dram_tensor("sinT", [NT, 128, 64], BF16, kind="ExternalInput").ap()
    identb_in = nc.dram_tensor("identb", [128, 128], BF16, kind="ExternalInput").ap()
    identf_in = nc.dram_tensor("identf", [128, 128], F32, kind="ExternalInput").ap()
    ones1b_in = nc.dram_tensor("ones1b", [1, 128], BF16, kind="ExternalInput").ap()
    onescb_in = nc.dram_tensor("onescb", [128, 1], BF16, kind="ExternalInput").ap()
    ones1f_in = nc.dram_tensor("ones1f", [1, 128], F32, kind="ExternalInput").ap()
    masks_in = nc.dram_tensor("masksb", [128, 4, 512], BF16, kind="ExternalInput").ap()
    triu_in = nc.dram_tensor("triu", [128, 128], BF16, kind="ExternalInput").ap()
    pidx_in = nc.dram_tensor("pidx", [128, 1], BF16, kind="ExternalInput").ap()
    bigc_in = nc.dram_tensor("bigc", [128, 8], F32, kind="ExternalInput").ap()
    eofs_in = nc.dram_tensor("eofs", [128, 8], F32, kind="ExternalInput").ap()
    ccol_in = nc.dram_tensor("ccols", [128, 6], F32, kind="ExternalInput").ap()
    gzt_in = nc.dram_tensor("gzt", [512, 1040], BF16, kind="ExternalInput").ap()
    zrow_in = nc.dram_tensor("zrow", [128, DM], F32, kind="ExternalInput").ap()

    out_d = nc.dram_tensor("out", [T, DM], F32, kind="ExternalOutput").ap()

    w13s = nc.dram_tensor("w13s", [NE, HC, 128, 2, DC, 128], BF16).ap()
    w2s2 = nc.dram_tensor("w2s", [NE, 128, HC, DC, 128], BF16).ap()
    qs = nc.dram_tensor("qscr", [NT, 128, H, 128], BF16).ap()
    attnscr = nc.dram_tensor("attnscr", [NQ, DC, 128, 512], F32).ap()
    x1s = nc.dram_tensor("x1scr", [T + 512, DM], F32).ap()
    xn2s = nc.dram_tensor("xn2scr", [NQ, 128, DC, 512], BF16).ap()
    rws = nc.dram_tensor("rwscr", [8, T], BF16).ap()
    GR = T + 512
    gathd = nc.dram_tensor("gathd", [8 * (T + 512), 1040], BF16).ap()

    with ExitStack() as top:
        def sbuf(name, shape, dt, stack):
            return stack.enter_context(nc.sbuf_tensor(name, shape, dt))

        def psum(name, shape, dt, stack):
            return stack.enter_context(nc.psum_tensor(name, shape, dt))

        s_identb = sbuf("s_identb", [128, 128], BF16, top)
        s_identf = sbuf("s_identf", [128, 128], F32, top)
        s_ones1b = sbuf("s_ones1b", [1, 128], BF16, top)
        s_onescb = sbuf("s_onescb", [128, 1], BF16, top)
        s_ccol = sbuf("s_ccol", [128, 6], F32, top)
        s_b13 = sbuf("s_b13", [128, NE, HC, 2], F32, top)
        s_sb2 = sbuf("s_sb2", [128, DC], F32, top)
        s_gwt = sbuf("s_gwt", [128, DC, 8], BF16, top)
        s_gbb = sbuf("s_gbb", [128, 8], F32, top)
        s_pidx = sbuf("s_pidx", [128, 1], BF16, top)
        s_triu = sbuf("s_triu", [128, 128], BF16, top)
        s_bigc = sbuf("s_bigc", [128, 8], F32, top)
        s_eofs = sbuf("s_eofs", [128, 8], F32, top)
        s_toti = sbuf("s_toti", [1, 8], I32, top)

        c_epsln = s_ccol[:, 0:1]
        c_epsrms = s_ccol[:, 1:2]
        c_neg1 = s_ccol[:, 2:3]
        c_negbig = s_ccol[:, 3:4]
        c_128 = s_ccol[:, 4:5]

        cast_sems = {}
        for cname in (["wqkv", "wo", "gate", "b2", "gz"] + [f"w13_{e}" for e in range(NE)]
                      + [f"w2_{e}" for e in range(NE)]):
            cast_sems[cname] = Ctr(top, nc, "cast_" + cname)

        def cast_wait(eng, cname):
            s = cast_sems[cname]
            eng.wait_ge(s.h, s.n)

        sems = {}
        for name in ["c0",
                     "v1", "s1", "t1",
                     "v2", "s2", "t2",
                     "v25", "s25", "t25",
                     "v3", "s3", "t3",
                     "yLd0", "yLd1", "yQs",
                     "yQq", "yAt0", "yAt1",
                     "yL25a", "yL25b", "yX1", "yXn2",
                     "yW13a", "yW13b", "yW13c", "yW2a", "yW2b", "yW2c",
                     "yXn2f", "yX1t", "yOut", "yRw", "yRwF", "yGs", "gsc3"]:
            sems[name] = Ctr(top, nc, "sem_" + name)
        c0 = sems["c0"]

        with ExitStack() as p012:
            k_sb = sbuf("k_sb", [128, H, NT, 128], BF16, p012)
            s_ones1f = sbuf("s_ones1f", [1, 128], F32, p012)
            v_sb = sbuf("v_sb", [128, H, NT, 128], BF16, p012)

            v1, s1, t1 = sems["v1"], sems["s1"], sems["t1"]
            yLd = [sems["yLd0"], sems["yLd1"]]
            yQs = sems["yQs"]
            with ExitStack() as p1:
                wq_sb = sbuf("wq_sb", [128, DC, DM], BF16, p1)
                wk_sb = sbuf("wk_sb", [128, DC, DM], BF16, p1)
                wv_sb = sbuf("wv_sb", [128, DC, DM], BF16, p1)
                xb = [sbuf(f"xb{i}", [128, DM], F32, p1) for i in range(2)]
                cb = [sbuf(f"cb{i}", [128, 64], BF16, p1) for i in range(2)]
                sb_ = [sbuf(f"sb{i}", [128, 64], BF16, p1) for i in range(2)]
                mv = sbuf("mv", [128, 14], F32, p1)
                xn_tm = sbuf("xn_tm", [128, DM], BF16, p1)
                xn_fm = sbuf("xn_fm", [128, DM], BF16, p1)
                sqb = sbuf("sqb", [128, DM], F32, p1)
                qkb_q = sbuf("qkb_q", [128, DM], BF16, p1)
                qkb_k = sbuf("qkb_k", [128, DM], BF16, p1)
                rt_q = sbuf("rt_q", [128, DM], BF16, p1)
                rt_k = sbuf("rt_k", [128, DM], BF16, p1)
                q_rope = sbuf("q_rope", [128, DM], BF16, p1)
                k_rope = sbuf("k_rope", [128, DM], BF16, p1)
                qfm_st = rt_q
                msq = sbuf("msq", [128, 8], F32, p1)
                rsq = sbuf("rsq", [128, 8], F32, p1)
                msq2 = sbuf("msq2", [128, 8], F32, p1)
                rsq2 = sbuf("rsq2", [128, 8], F32, p1)
                psQ = psum("psQ", [128, DM], F32, p1)
                psK = psum("psK", [128, DM], F32, p1)
                psV = psum("psV", [128, DM], F32, p1)
                psXP = psum("psXP", [128, DM], BF16, p1)
                psT2 = psum("psT2", [128, DM], BF16, p1)

                ck = {}

                with nc.Block("p1") as blk:
                    @blk.scalar
                    def _(s):
                        s.add_instruction(mybir.InstLoadActFuncSet(
                            name=nc.get_next_instruction_name(),
                            act_func_set_id=LNEXP_SET, ins=[], outs=[]))
                        c0.wait(s, 192)

                    @blk.sync
                    def _(sy):
                        for dst, srcap in [
                            (s_identb[:], identb_in[:]), (s_identf[:], identf_in[:]),
                            (s_ones1b[:], ones1b_in[:]), (s_onescb[:], onescb_in[:]),
                            (s_ccol[:], ccol_in[:]), (s_b13[:], b13_in[:]),
                            (s_sb2[:], sb2_in[:]), (s_gbb[:], gbb_in[:]),
                            (s_pidx[:], pidx_in[:]), (s_triu[:], triu_in[:]),
                            (s_bigc[:], bigc_in[:]), (s_eofs[:], eofs_in[:]),
                        ]:
                            c0.inc(sy.dma_start(dst, srcap), 16)

                    @blk.gpsimd
                    def _(g):
                        gz = cast_sems["gz"]
                        nblk = 8 * (T + 512) // 512
                        for b_ in range(nblk):
                            gz.inc(g.dma_start(
                                gathd[b_ * 512:(b_ + 1) * 512, :], gzt_in[:]), 16)
                        for sub in range(4):
                            gz.inc(g.dma_start(
                                x1s[T + sub * 128:T + (sub + 1) * 128, :],
                                zrow_in[:]), 16)
                        cq = cast_sems["wqkv"]
                        cq.inc(g.dma_start(wq_sb[:], wq_in.rearrange("(c p) n -> p c n", p=128)), 16)
                        cq.inc(g.dma_start(wk_sb[:], wk_in.rearrange("(c p) n -> p c n", p=128)), 16)
                        cq.inc(g.dma_start(wv_sb[:], wv_in.rearrange("(c p) n -> p c n", p=128)), 16)
                        cast_sems["gate"].inc(g.dma_start(s_gwt[:], gwt_in[:]), 16)
                        for e in range(NE):
                            cast_sems[f"w13_{e}"].inc(g.dma_start(w13s[e], w13_in[e]), 16)
                        for e in range(NE):
                            cast_sems[f"w2_{e}"].inc(g.dma_start(w2s2[e], w2_in[e]), 16)

                    @blk.vector
                    def _(v):
                        c0.wait(v, 192)

                    @blk.tensor
                    def _(tt):
                        c0.wait(tt, 192)

                    def emit_qk_T(t):
                        for which, ro in [("q", q_rope), ("k", k_rope)]:
                            @blk.tensor
                            def _(tt, t=t, which=which, ro=ro):
                                v1.wait(tt, ck[("rope", t, which)])
                                if not (t == 0 and which == "q"):
                                    v1.wait(tt, ck[("psT2",)])
                                for h in range(H):
                                    mm = tt.transpose(psT2[:, h * 128:(h + 1) * 128],
                                                      ro[:, h * 128:(h + 1) * 128], s_identb[:])
                                t1.inc(mm)
                                ck[("qT" if which == "q" else "kT", t)] = t1.n

                            @blk.vector
                            def _(v, t=t, which=which):
                                t1.wait(v)
                                if which == "q":
                                    if t >= 1:
                                        yQs.wait(v, 16 * t)
                                    v1.inc(v.tensor_copy(qfm_st[:], psT2[:]))
                                else:
                                    v1.inc(v.tensor_copy(
                                        k_sb[:, :, t, :],
                                        psT2[:].rearrange("p (h d) -> p h d", h=H)))
                                ck[("psT2",)] = v1.n
                                if which == "q":
                                    ck[("qfm", t)] = v1.n

                        @blk.sync
                        def _(sy, t=t):
                            v1.wait(sy, ck[("qfm", t)])
                            yQs.inc(sy.dma_start(qs[t].rearrange("p h d -> p (h d)"), qfm_st[:]), 16)

                    def emit_loads(t):
                        p = t % 2

                        @blk.sync
                        def _(sy, t=t, p=p):
                            if t >= 2:
                                v1.wait(sy, ck[("ropek", t - 2)])
                                s1.wait(sy, ck[("apply", t - 2)])
                            yLd[p].inc(sy.dma_start(xb[p][:], x_in[t * 128:(t + 1) * 128, :]), 16)
                            yLd[p].inc(sy.dma_start(cb[p][:], cos_in[t]), 16)
                            yLd[p].inc(sy.dma_start(sb_[p][:], sin_in[t]), 16)
                            ck[("xld", t)] = yLd[p].n

                    def emit_ln(t):
                        p = t % 2

                        @blk.vector
                        def _(v, t=t, p=p):
                            yLd[p].wait(v, ck[("xld", t)])
                            v.bn_stats(mv[:, 0:6], xb[p][:, 0:512])
                            v.bn_stats(mv[:, 6:12], xb[p][:, 512:1024])
                            v.drain()
                            v.bn_aggr(mv[:, 12:14], mv[:, 0:12])
                            v.drain()
                            v.tensor_scalar(mv[:, 13:14], mv[:, 13:14], c_epsln,
                                            None, AluOpType.add)
                            v.drain()
                            v1.inc(v.tensor_scalar(mv[:, 11:12], mv[:, 12:13], c_neg1,
                                                   None, AluOpType.mult))
                            ck[("stats", t)] = v1.n

                        @blk.scalar
                        def _(s, t=t, p=p):
                            v1.wait(s, ck[("stats", t)])
                            s.activation(mv[:, 13:14], mv[:, 13:14], AF.Ln)
                            s.drain()
                            s.activation(mv[:, 13:14], mv[:, 13:14], AF.Exp, scale=-0.5)
                            s.drain()
                            s.activation(mv[:, 12:13], mv[:, 11:12], AF.Identity,
                                         scale=mv[:, 13:14])
                            s.drain()
                            if t >= 1:
                                t1.wait(s, ck[("xnT", t - 1)])
                            s1.inc(s.activation(xn_tm[:], xb[p][:], AF.Identity,
                                                bias=mv[:, 12:13], scale=mv[:, 13:14]))
                            ck[("apply", t)] = s1.n

                    emit_loads(0)
                    emit_ln(0)

                    def emit_xnT(t):
                        @blk.tensor
                        def _(tt, t=t):
                            s1.wait(tt, ck[("apply", t)])
                            if t >= 1:
                                v1.wait(tt, ck[("xnfm", t - 1)])
                            for i in range(8):
                                mmt = tt.transpose(psXP[:, i * 128:(i + 1) * 128],
                                                   xn_tm[:, i * 128:(i + 1) * 128], s_identb[:])
                            t1.inc(mmt)
                            ck[("xnT", t)] = t1.n

                        @blk.vector
                        def _(v, t=t):
                            t1.wait(v, ck[("xnT", t)])
                            if t >= 1:
                                t1.wait(v, ck[("qkv", t - 1)])
                            v1.inc(v.tensor_copy(xn_fm[:], psXP[:]))
                            ck[("xnfm", t)] = v1.n

                    emit_xnT(0)

                    for t in range(NT):
                        p = t % 2

                        if t + 1 < NT:
                            emit_loads(t + 1)

                        @blk.tensor
                        def _(tt, t=t):
                            if t == 0:
                                cast_wait(tt, "wqkv")
                            v1.wait(tt, ck[("xnfm", t)])
                            if t >= 1:
                                s1.wait(tt, ck[("kscale", t - 1)])
                                v1.wait(tt, ck[("vev", t - 1)])
                            for ps, w in [(psQ, wq_sb), (psK, wk_sb), (psV, wv_sb)]:
                                for hlf in (0, 512):
                                    for dc in range(DC):
                                        mm = tt.matmul(ps[:, hlf:hlf + 512],
                                                       xn_fm[:, dc * 128:(dc + 1) * 128],
                                                       w[:, dc, hlf:hlf + 512],
                                                       start=(dc == 0), stop=(dc == DC - 1))
                            t1.inc(mm)
                            ck[("qkv", t)] = t1.n

                        if t >= 1:
                            emit_qk_T(t - 1)

                        if t + 1 < NT:
                            emit_ln(t + 1)
                            emit_xnT(t + 1)

                        @blk.vector
                        def _(v, t=t):
                            t1.wait(v, ck[("qkv", t)])
                            v1.inc(v.tensor_copy(
                                v_sb[:, :, t, :],
                                psV[:].rearrange("p (h d) -> p h d", h=H)))
                            ck[("vev", t)] = v1.n

                        @blk.scalar
                        def _(s, t=t):
                            t1.wait(s, ck[("qkv", t)])
                            if t >= 1:
                                v1.wait(s, ck[("kred", t - 1)])
                            s1.inc(s.activation(sqb[:], psQ[:], AF.Square))
                            ck[("qsq", t)] = s1.n

                        @blk.vector
                        def _(v, t=t):
                            s1.wait(v, ck[("qsq", t)])
                            v1.inc(v.reduce_sum(
                                msq[:], sqb[:].rearrange("p (h d) -> p h d", h=H), axis=AX))
                            ck[("qred", t)] = v1.n

                        @blk.scalar
                        def _(s, t=t):
                            v1.wait(s, ck[("qred", t)])
                            s1.inc(s.activation(sqb[:], psK[:], AF.Square))
                            ck[("ksq", t)] = s1.n

                        @blk.scalar
                        def _(s, t=t):
                            s.activation(rsq[:], msq[:], AF.Ln, bias=c_epsrms, scale=1.0 / DH)
                            s.drain()
                            s.activation(rsq[:], rsq[:], AF.Exp, scale=-0.5)
                            s.drain()
                            for h in range(H):
                                mm = s.activation(qkb_q[:, h * 128:(h + 1) * 128],
                                                  psQ[:, h * 128:(h + 1) * 128],
                                                  AF.Identity, scale=rsq[:, h:h + 1])
                            s1.inc(mm)
                            ck[("qscale", t)] = s1.n

                        @blk.vector
                        def _(v, t=t):
                            s1.wait(v, ck[("ksq", t)])
                            v1.inc(v.reduce_sum(
                                msq2[:], sqb[:].rearrange("p (h d) -> p h d", h=H), axis=AX))
                            ck[("kred", t)] = v1.n

                        def emit_rope(t, which, ro, qkb, scale_ck):
                            p = t % 2

                            @blk.vector
                            def _(v, t=t, which=which, ro=ro, qkb=qkb):
                                s1.wait(v, ck[(scale_ck, t)])
                                if t >= 1:
                                    t1.wait(v, ck[("qT" if which == "q" else "kT", t - 1)])
                                if which == "q" and t >= 1:
                                    yQs.wait(v, 16 * t)
                                rt = rt_q if which == "q" else rt_k
                                r3 = lambda ap: ap.rearrange("p (h d) -> p h d", h=H)
                                ev, od = r3(qkb[:])[:, :, 0:64], r3(qkb[:])[:, :, 64:128]
                                oe, oo = r3(ro[:])[:, :, 0:64], r3(ro[:])[:, :, 64:128]
                                te, to = r3(rt[:])[:, :, 0:64], r3(rt[:])[:, :, 64:128]
                                cosb = bass.AP(cb[p].ap().tensor, cb[p].ap().offset,
                                               [cb[p].ap().ap[0], [0, H], [1, 64]])
                                sinb = bass.AP(sb_[p].ap().tensor, sb_[p].ap().offset,
                                               [sb_[p].ap().ap[0], [0, H], [1, 64]])
                                v.tensor_tensor(te, ev, cosb, AluOpType.mult)
                                v.tensor_tensor(to, od, sinb, AluOpType.mult)
                                v.drain()
                                v.tensor_tensor(oe, te, to, AluOpType.subtract)
                                v.drain()
                                v.tensor_tensor(te, ev, sinb, AluOpType.mult)
                                v.tensor_tensor(to, od, cosb, AluOpType.mult)
                                v.drain()
                                v1.inc(v.tensor_tensor(oo, te, to, AluOpType.add))
                                ck[("rope", t, which)] = v1.n
                                if which == "k":
                                    ck[("ropek", t)] = v1.n

                        emit_rope(t, "q", q_rope, qkb_q, "qscale")

                        @blk.scalar
                        def _(s, t=t):
                            v1.wait(s, ck[("kred", t)])
                            s.activation(rsq2[:], msq2[:], AF.Ln, bias=c_epsrms, scale=1.0 / DH)
                            s.drain()
                            s.activation(rsq2[:], rsq2[:], AF.Exp, scale=-0.5)
                            s.drain()
                            for h in range(H):
                                mm = s.activation(qkb_k[:, h * 128:(h + 1) * 128],
                                                  psK[:, h * 128:(h + 1) * 128],
                                                  AF.Identity, scale=rsq2[:, h:h + 1])
                            s1.inc(mm)
                            ck[("kscale", t)] = s1.n

                        emit_rope(t, "k", k_rope, qkb_k, "kscale")

                    emit_qk_T(NT - 1)


            v2, s2, t2 = sems["v2"], sems["s2"], sems["t2"]
            yQq = sems["yQq"]
            yAt = [sems["yAt0"], sems["yAt1"]]
            with ExitStack() as p2:
                s_masks = sbuf("s_masks", [128, 4, 512], BF16, p2)
                wo_sb = sbuf("wo_sb", [128, H, DC, 128], BF16, p2)
                qq = sbuf("qq", [128, 4, H, 128], BF16, p2)
                pt_sl = [sbuf(f"pt{i}", [128, 1024], BF16, p2) for i in range(2)]
                ptm = sbuf("ptm", [128, 4, 512], BF16, p2)
                sden = [sbuf(f"sden{i}", [1, 512], F32, p2) for i in range(2)]
                srden = [sbuf(f"srden{i}", [1, 512], F32, p2) for i in range(2)]
                sdb = [sbuf(f"sdb{i}", [128, 512], F32, p2) for i in range(2)]
                o_all = sbuf("o_all", [128, H, 512], BF16, p2)
                sattn = [sbuf(f"sattn{i}", [128, 512], F32, p2) for i in range(2)]
                psS = [psum(f"psS{i}", [128, 1024], F32, p2) for i in range(2)]
                psO = [psum(f"psO{i}", [128, 512], F32, p2) for i in range(2)]
                psDD = [psum(f"psDD{i}", [128, 512], F32, p2) for i in range(2)]
                psA = psS[0][:, 0:512]

                ck2 = {"exp_last": {}, "pv_last": {}, "onorm_p": {}, "wodone": 0}

                with nc.Block("p2") as blk:
                    @blk.gpsimd
                    def _(g):
                        cast_sems["wo"].inc(g.dma_start(
                            wo_sb[:],
                            bass.AP(wo_in.tensor, 0,
                                    [[1024, 128], [131072, H], [128, DC], [1, 128]])), 16)

                    @blk.scalar
                    def _(s):
                        s.add_instruction(mybir.InstLoadActFuncSet(
                            name=nc.get_next_instruction_name(),
                            act_func_set_id=LNEXP_SET, ins=[], outs=[]))

                    @blk.sync
                    def _(sy):
                        yQs.wait(sy)
                        yLd[0].wait(sy)
                        yLd[1].wait(sy)
                        c0.inc(sy.dma_start(s_masks[:], masks_in[:]), 16)
                        c0.inc(sy.dma_start(s_ones1f[:], ones1f_in[:]), 16)

                    for qg in range(NQ):
                        JC = 4 * (qg + 1)
                        nbatch = (JC + 1) // 2

                        @blk.sync
                        def _(sy, qg=qg):
                            if qg >= 1:
                                t2.wait(sy, ck2[("qq_free", qg - 1)])
                            for i in range(4):
                                yQq.inc(sy.dma_start(qq[:, i, :, :], qs[4 * qg + i]), 16)

                        for h in range(H):
                            hp = h % 2
                            exp_ck, pv_ck = {}, {}
                            batches = [list(range(2 * b, min(2 * b + 2, JC)))
                                       for b in range(nbatch)]

                            def emit_st(b, qg=qg, h=h):
                                jcs = batches[b]

                                @blk.tensor
                                def _(tt):
                                    if b == 0 and h == 0:
                                        yQq.wait(tt)
                                    if (b - 2) in exp_ck:
                                        s2.wait(tt, exp_ck[b - 2])
                                    elif b < 2 and (b % 2) in ck2["exp_last"]:
                                        s2.wait(tt, ck2["exp_last"][b % 2])
                                    for i, jc in enumerate(jcs):
                                        mm = tt.matmul(psS[b % 2][:, i * 512:(i + 1) * 512],
                                                       k_sb[:, h, jc, :], qq[:, :, h, :],
                                                       start=True, stop=True)
                                    t2.inc(mm)
                                    ck2[("st", b)] = t2.n

                            def emit_exp(b, qg=qg, h=h):
                                jcs = batches[b]

                                @blk.scalar
                                def _(s):
                                    t2.wait(s, ck2[("st", b)])
                                    if (b - 2) in pv_ck:
                                        t2.wait(s, pv_ck[b - 2])
                                    elif b < 2 and (b % 2) in ck2["pv_last"]:
                                        t2.wait(s, ck2["pv_last"][b % 2])
                                    s2.inc(s.activation(pt_sl[b % 2][:, 0:len(jcs) * 512],
                                                        psS[b % 2][:, 0:len(jcs) * 512],
                                                        AF.Exp, scale=1.0 / DH))
                                    exp_ck[b] = s2.n
                                    ck2["exp_last"][b % 2] = s2.n

                                diag = [i for i, jc in enumerate(jcs) if jc >= 4 * qg]
                                if diag:
                                    @blk.vector
                                    def _(v):
                                        if qg == 0 and h == 0:
                                            c0.wait(v)
                                        s2.wait(v, exp_ck[b])
                                        for i in diag:
                                            kk = jcs[i] - 4 * qg
                                            mm = v.tensor_tensor(
                                                ptm[:, kk, :],
                                                pt_sl[b % 2][:, i * 512:(i + 1) * 512],
                                                s_masks[:, kk, :], AluOpType.mult)
                                        v2.inc(mm)
                                        ck2[("mask", b)] = v2.n

                            def emit_pv(b, qg=qg, h=h, hp=hp):
                                jcs = batches[b]

                                @blk.tensor
                                def _(tt):
                                    s2.wait(tt, exp_ck[b])
                                    if any(jc >= 4 * qg for jc in jcs):
                                        v2.wait(tt, ck2[("mask", b)])
                                    if b == 0 and (hp in ck2["onorm_p"]):
                                        v2.wait(tt, ck2["onorm_p"][hp])
                                    for i, jc in enumerate(jcs):
                                        src = (ptm[:, jc - 4 * qg, :] if jc >= 4 * qg
                                               else pt_sl[b % 2][:, i * 512:(i + 1) * 512])
                                        tt.matmul(psO[hp][:], v_sb[:, h, jc, :], src,
                                                  start=(jc == 0), stop=(jc == JC - 1))
                                        mm = tt.matmul(psDD[hp][0:1, :], s_onescb[:], src,
                                                       start=(jc == 0), stop=(jc == JC - 1))
                                    t2.inc(mm)
                                    pv_ck[b] = t2.n
                                    ck2["pv_last"][b % 2] = t2.n

                            emit_st(0)
                            emit_exp(0)
                            for b in range(1, nbatch):
                                emit_st(b)
                                emit_pv(b - 1)
                                emit_exp(b)
                            emit_pv(nbatch - 1)

                            @blk.vector
                            def _(v, qg=qg, h=h, hp=hp, pv_ck=pv_ck, nbatch=nbatch):
                                t2.wait(v, pv_ck[nbatch - 1])
                                if qg >= 1 and h == 0:
                                    t2.wait(v, ck2["wodone"])
                                v.tensor_copy(sden[hp][:], psDD[hp][0:1, :])
                                v.drain()
                                v2.inc(v.reciprocal(srden[hp][:], sden[hp][:]))

                            @blk.tensor
                            def _(tt, qg=qg, h=h, hp=hp):
                                if qg == 0 and h == 0:
                                    c0.wait(tt)
                                v2.wait(tt)
                                t2.inc(tt.matmul(psDD[hp][:], s_ones1f[:], srden[hp][:],
                                                 start=True, stop=True))

                            @blk.vector
                            def _(v, qg=qg, h=h, hp=hp):
                                t2.wait(v)
                                v.tensor_copy(sdb[hp][:], psDD[hp][:])
                                v.drain()
                                v2.inc(v.tensor_tensor(o_all[:, h, :], psO[hp][:], sdb[hp][:],
                                                       AluOpType.mult))
                                ck2["onorm_p"][hp] = v2.n
                                if h == H - 1:
                                    ck2[("heads_done", qg)] = v2.n

                        ck2[("qq_free", qg)] = t2.n

                        for mc in range(DC):
                            @blk.tensor
                            def _(tt, qg=qg, mc=mc):
                                if qg == 0 and mc == 0:
                                    cast_wait(tt, "wo")
                                v2.wait(tt, ck2[("heads_done", qg)])
                                if ("attn_ev", qg, mc - 1) in ck2:
                                    v2.wait(tt, ck2[("attn_ev", qg, mc - 1)])
                                elif qg >= 1 and mc == 0:
                                    v2.wait(tt, ck2[("attn_ev", qg - 1, DC - 1)])
                                for h in range(H):
                                    mm = tt.matmul(psA, wo_sb[:, h, mc, :], o_all[:, h, :],
                                                   start=(h == 0), stop=(h == H - 1))
                                t2.inc(mm)
                                if mc == DC - 1:
                                    ck2["wodone"] = t2.n

                            @blk.vector
                            def _(v, qg=qg, mc=mc):
                                t2.wait(v)
                                k_st = qg * DC + mc
                                if k_st >= 2:
                                    yAt[mc % 2].wait(v, 16 * (k_st // 2))
                                v2.inc(v.tensor_copy(sattn[mc % 2][:], psA))
                                ck2[("attn_ev", qg, mc)] = v2.n

                            @blk.sync
                            def _(sy, qg=qg, mc=mc):
                                v2.wait(sy, ck2[("attn_ev", qg, mc)])
                                yAt[mc % 2].inc(sy.dma_start(attnscr[qg, mc], sattn[mc % 2][:]), 16)

            v25, s25, t25 = sems["v25"], sems["s25"], sems["t25"]
            yL25 = [sems["yL25a"], sems["yL25b"]]
            yX1 = sems["yX1"]
            yXn2 = sems["yXn2"]
            yRw = sems["yRw"]
            yGs = sems["yGs"]
            with ExitStack() as p25:
                xb2 = [sbuf(f"xb2_{i}", [128, DM], F32, p25) for i in range(2)]
                afm = [sbuf(f"afm{i}", [128, DC, 128], F32, p25) for i in range(2)]
                x1tm = sbuf("x1tm", [128, DM], F32, p25)
                xn2tm = sbuf("xn2tm", [128, DM], BF16, p25)
                xn2st = [sbuf(f"xn2st{i}", [128, DC, 512], BF16, p25) for i in range(2)]
                mv2 = sbuf("mv2", [128, 14], F32, p25)
                sxp = sbuf("sxp", [128, 8], F32, p25)
                sume = sbuf("sume", [128, 1], F32, p25)
                rsum = sbuf("rsum", [128, 1], F32, p25)
                rsc = sbuf("rsc", [128, 8], F32, p25)
                bsd = sbuf("bsd", [128, 8], F32, p25)
                m1c = sbuf("m1c", [128, 1], F32, p25)
                mask1 = sbuf("mask1", [128, 8], F32, p25)
                b2d = sbuf("b2d", [128, 8], F32, p25)
                m2c = sbuf("m2c", [128, 1], F32, p25)
                mask2 = sbuf("mask2", [128, 8], F32, p25)
                mask1i = sbuf("mask1i", [128, 8], I32, p25)
                rwbf = sbuf("rwbf", [128, 8], BF16, p25)
                rw_st = sbuf("rw_st", [8, 128], BF16, p25)
                selb = sbuf("selb", [128, 8], BF16, p25)
                base_b = sbuf("base_b", [128, 8], F32, p25)
                a_s = sbuf("a_s", [128, 8], F32, p25)
                idxf = sbuf("idxf", [128, 8], F32, p25)
                idxu = sbuf("idxu", [128, 8], U32, p25)
                totrow = sbuf("totrow", [1, 8], F32, p25)
                ttile = sbuf("ttile", [1, 8], F32, p25)
                stage = sbuf("stage", [128, 1040], BF16, p25)
                psT25 = psum("psT25", [128, DM], F32, p25)
                psX2 = psum("psX2", [128, DM], BF16, p25)
                psG = psum("psG", [128, 512], F32, p25)
                psRw = psum("psRw", [8, 512], BF16, p25)
                psTot = psum("psTot", [1, 512], F32, p25)

                ck25 = {}

                with nc.Block("p25") as blk:
                    @blk.vector
                    def _(v):
                        v.memset(stage[:], 0.0)
                        v25.inc(v.memset(totrow[:], 0.0))
                        ck25[("tot_snap", -1)] = v25.n

                    @blk.sync
                    def _(sy):
                        yAt[0].wait(sy)
                        yAt[1].wait(sy)
                        yQq.wait(sy)

                    for t in range(NT):
                        p = t % 2
                        qg, off = t // 4, (t % 4) * 128

                        @blk.sync
                        def _(sy, t=t, p=p, qg=qg, off=off):
                            if t >= 2:
                                v25.wait(sy, ck25[("vfin", t - 2)])
                            yL25[p].inc(sy.dma_start(xb2[p][:], x_in[t * 128:(t + 1) * 128, :]), 16)
                            yL25[p].inc(sy.dma_start(
                                afm[p][:],
                                bass.AP(attnscr.tensor, qg * DC * 128 * 512 + off,
                                        [[512, 128], [128 * 512, DC], [1, 128]])), 16)
                            ck25[("ld", t)] = yL25[p].n

                        @blk.tensor
                        def _(tt, t=t, p=p):
                            yL25[p].wait(tt, ck25[("ld", t)])
                            if t >= 1:
                                v25.wait(tt, ck25[("x1add", t - 1)])
                            for mc in range(DC):
                                mm = tt.transpose(psT25[:, mc * 128:(mc + 1) * 128],
                                                  afm[p][:, mc, :], s_identf[:])
                            t25.inc(mm)

                        @blk.vector
                        def _(v, t=t, p=p):
                            t25.wait(v)
                            if t >= 1:
                                yX1.wait(v, 16 * t)
                            v25.inc(v.tensor_tensor(x1tm[:], psT25[:], xb2[p][:], AluOpType.add))
                            ck25[("x1add", t)] = v25.n

                        @blk.sync
                        def _(sy, t=t):
                            v25.wait(sy, ck25[("x1add", t)])
                            yX1.inc(sy.dma_start(x1s[t * 128:(t + 1) * 128, :], x1tm[:]), 16)

                        @blk.vector
                        def _(v, t=t):
                            v.drain()
                            v.bn_stats(mv2[:, 0:6], x1tm[:, 0:512])
                            v.bn_stats(mv2[:, 6:12], x1tm[:, 512:1024])
                            v.drain()
                            v.bn_aggr(mv2[:, 12:14], mv2[:, 0:12])
                            v.drain()
                            v.tensor_scalar(mv2[:, 13:14], mv2[:, 13:14],
                                            c_epsln, None, AluOpType.add)
                            v.drain()
                            v25.inc(v.tensor_scalar(mv2[:, 11:12], mv2[:, 12:13], c_neg1,
                                                    None, AluOpType.mult))

                        @blk.scalar
                        def _(s, t=t):
                            v25.wait(s)
                            s.activation(mv2[:, 13:14], mv2[:, 13:14], AF.Ln)
                            s.drain()
                            s.activation(mv2[:, 13:14], mv2[:, 13:14], AF.Exp, scale=-0.5)
                            s.drain()
                            s.activation(mv2[:, 12:13], mv2[:, 11:12], AF.Identity,
                                         scale=mv2[:, 13:14])
                            s.drain()
                            if t >= 1:
                                t25.wait(s, ck25[("x2t", t - 1)])
                            s25.inc(s.activation(xn2tm[:], x1tm[:], AF.Identity,
                                                 bias=mv2[:, 12:13], scale=mv2[:, 13:14]))
                            ck25[("apply2", t)] = s25.n

                        @blk.tensor
                        def _(tt, t=t):
                            s25.wait(tt, ck25[("apply2", t)])
                            if t >= 1:
                                v25.wait(tt, ck25[("x2ev", t - 1)])
                            for i in range(DC):
                                mm = tt.transpose(psX2[:, i * 128:(i + 1) * 128],
                                                  xn2tm[:, i * 128:(i + 1) * 128], s_identb[:])
                            t25.inc(mm)
                            ck25[("x2t", t)] = t25.n

                        @blk.vector
                        def _(v, t=t, qg=qg, off=off):
                            t25.wait(v, ck25[("x2t", t)])
                            if t % 4 == 0 and qg >= 2:
                                yXn2.wait(v, 16 * qg)
                                t25.wait(v, ck25[("gmm", 4 * (qg - 2) + 3)])
                            v25.inc(v.tensor_copy(
                                xn2st[qg % 2][:, :, off:off + 128],
                                psX2[:].rearrange("p (c d) -> p c d", c=DC)))
                            ck25[("x2ev", t)] = v25.n

                        @blk.sync
                        def _(sy, t=t, qg=qg):
                            if t % 4 == 3:
                                v25.wait(sy, ck25[("x2ev", t)])
                                yXn2.inc(sy.dma_start(xn2s[qg], xn2st[qg % 2][:]), 16)

                        @blk.tensor
                        def _(tt, t=t, qg=qg, off=off):
                            if t == 0:
                                cast_wait(tt, "gate")
                            v25.wait(tt, ck25[("x2ev", t)])
                            if t >= 1:
                                s25.wait(tt, ck25[("gexp", t - 1)])
                            for dc in range(DC):
                                mm = tt.matmul(psG[:, 0:8],
                                               xn2st[qg % 2][:, dc, off:off + 128],
                                               s_gwt[:, dc, :],
                                               start=(dc == 0), stop=(dc == DC - 1))
                            t25.inc(mm)
                            ck25[("gmm", t)] = t25.n

                        @blk.vector
                        def _(v, t=t):
                            t25.wait(v, ck25[("gmm", t)])
                            v25.inc(v.reduce_max(m1c[:], psG[:, 0:8], axis=AX, negate=True))

                        @blk.scalar
                        def _(s, t=t):
                            v25.wait(s)
                            s25.inc(s.activation(sxp[:], psG[:, 0:8], AF.Exp,
                                                 bias=m1c[:], accum_out=sume[:]))
                            ck25[("gexp", t)] = s25.n

                        @blk.vector
                        def _(v, t=t):
                            s25.wait(v, ck25[("gexp", t)])
                            v.reciprocal(rsum[:], sume[:])
                            v.drain()
                            v.tensor_scalar(rsc[:], sxp[:], rsum[:], None, AluOpType.mult)
                            v.drain()
                            v.tensor_tensor(bsd[:], rsc[:], s_gbb[:], AluOpType.add)
                            v.drain()
                            v.reduce_max(m1c[:], bsd[:], axis=AX)
                            v.drain()
                            v.tensor_scalar(mask1[:], bsd[:], m1c[:], None, AluOpType.is_ge)
                            v.drain()
                            v.scalar_tensor_tensor(b2d[:], mask1[:], c_negbig, bsd[:],
                                                   AluOpType.mult, AluOpType.add)
                            v.drain()
                            v.reduce_max(m2c[:], b2d[:], axis=AX)
                            v.drain()
                            v.tensor_scalar(mask2[:], b2d[:], m2c[:], None, AluOpType.is_ge)
                            v.drain()
                            v.tensor_tensor(mask1[:], mask1[:], mask2[:], AluOpType.add)
                            v.drain()
                            v25.inc(v.tensor_tensor(rwbf[:], rsc[:], mask1[:], AluOpType.mult))

                        @blk.tensor
                        def _(tt, t=t):
                            v25.wait(tt)
                            if t >= 1:
                                v25.wait(tt, ck25[("rwev", t - 1)])
                            t25.inc(tt.transpose(psRw[:, 0:128], rwbf[:], s_identb[:]))

                        @blk.vector
                        def _(v, t=t):
                            t25.wait(v)
                            if t >= 1:
                                yRw.wait(v, 16 * t)
                            v25.inc(v.tensor_copy(rw_st[:], psRw[:, 0:128]))
                            ck25[("rwev", t)] = v25.n

                        @blk.sync
                        def _(sy, t=t):
                            v25.wait(sy, ck25[("rwev", t)])
                            yRw.inc(sy.dma_start(
                                bass.AP(rws.tensor, t * 128, [[T, 8], [1, 128]]),
                                rw_st[:]), 16)

                        @blk.vector
                        def _(v, t=t):
                            v.drain()
                            v25.inc(v.tensor_copy(selb[:], mask1[:]))
                            ck25[("selb", t)] = v25.n

                        @blk.tensor
                        def _(tt, t=t):
                            v25.wait(tt, ck25[("selb", t)])
                            v25.wait(tt, ck25[("tot_snap", t - 1)])
                            tt.matmul(psG[:, 16:24], s_ones1f[:], totrow[:],
                                      start=True, stop=True)
                            tt.matmul(psG[:, 8:16], s_triu[:], selb[:],
                                      start=True, stop=True)
                            mmt = tt.matmul(psTot[0:1, 0:8], s_onescb[:], selb[:],
                                            start=True, stop=True)
                            t25.inc(mmt)
                            ck25[("preMM", t)] = t25.n

                        @blk.vector
                        def _(v, t=t):
                            t25.wait(v, ck25[("preMM", t)])
                            if t >= 1:
                                yGs.wait(v, 128 * t)
                            v.tensor_copy(base_b[:], psG[:, 16:24])
                            v.drain()
                            v.tensor_tensor(a_s[:], psG[:, 8:16], base_b[:], AluOpType.add)
                            v.drain()
                            v.tensor_copy(mask1i[:], mask1[:])
                            v.drain()
                            v.select(idxf[:], mask1i[:], a_s[:], s_bigc[:], add_drain=True)
                            v.drain()
                            v.tensor_tensor(idxf[:], idxf[:], mask1[:], AluOpType.subtract)
                            v.drain()
                            v.tensor_tensor(idxf[:], idxf[:], s_eofs[:], AluOpType.add)
                            v.drain()
                            v.tensor_copy(idxu[:], idxf[:])
                            v.drain()
                            v.tensor_copy(ttile[:], psTot[0:1, 0:8])
                            v.drain()
                            v.tensor_tensor(totrow[:], totrow[:], ttile[:], AluOpType.add)
                            v.drain()
                            v.tensor_copy(stage[:, 0:1024], xn2tm[:])
                            v.tensor_copy(stage[:, 1024:1032], rwbf[:])
                            v.memset(stage[:, 1032:1033], float(t))
                            v.tensor_copy(stage[:, 1033:1034], s_pidx[:])
                            v25.inc(v.drain())
                            ck25[("tot_snap", t)] = v25.n
                            ck25[("stage", t)] = v25.n

                        @blk.gpsimd
                        def _(g, t=t):
                            if t == 0:
                                gz_ = cast_sems["gz"]
                                g.wait_ge(gz_.h, gz_.n)
                            v25.wait(g, ck25[("stage", t)])
                            with g.register(f"bnd{t}") as rb:
                                g.reg_mov(rb, 8 * GR - 1)
                                for e in range(8):
                                    yGs.inc(g.indirect_dma_start(
                                        gathd[:],
                                        bass.IndirectOffsetOnAxis(ap=idxu[:, e:e + 1], axis=0),
                                        stage[:], None,
                                        bounds_check=rb, oob_is_err=False), 16)

                        ck25[("vfin", t)] = v25.n

                    @blk.vector
                    def _(v):
                        v.drain()
                        v25.inc(v.tensor_copy(s_toti[:], totrow[:]))

        v3, s3, t3 = sems["v3"], sems["s3"], sems["t3"]
        yW13 = [sems["yW13a"], sems["yW13b"]]
        yW2 = sems["yW2a"]
        yGb = [sems["yW2b"], sems["yW2c"]]
        yXn2f = sems["yXn2f"]
        yX1t = sems["yX1t"]
        yOut = sems["yOut"]
        yRwF = sems["yRwF"]
        gsc3 = sems["gsc3"]
        GR_ = T + 512
        NCH = T // 512
        with ExitStack() as p3:
            w13e = [sbuf(f"w13e{i}", [128, 6 * 2048], BF16, p3) for i in range(2)]
            w2e = sbuf("w2e", [128, HC, DC, 128], BF16, p3)
            gbuf = [sbuf(f"gbuf{i}", [128, 4, 1040], BF16, p3) for i in range(2)]
            gfm = [sbuf(f"gfm{i}", [128, DC, 512], BF16, p3) for i in range(2)]
            hgb = [sbuf(f"hgb{i}", [128, HC, 512], BF16, p3) for i in range(2)]
            ydst = sbuf("ydst", [128, 4, DM], F32, p3)
            rwfc = sbuf("rwfc", [128, 4], F32, p3)
            yidxf = sbuf("yidxf", [128, 4], F32, p3)
            yidxu = sbuf("yidxu", [128, 4], U32, p3)
            ddum = sbuf("ddum", [128, 8 * NCH + 8], BF16, p3)
            ddumg = sbuf("ddumg", [128, 4 * (8 * NCH + 8)], BF16, p3)
            xn2f = sbuf("xn2f", [128, DC, 512], BF16, p3)
            x1t = sbuf("x1t", [128, 4, DM], F32, p3)
            rwT_f = sbuf("rwT_f", [8, 512], BF16, p3)
            s_b2 = sbuf("s_b2", [8, DM], BF16, p3)
            sA = [sbuf(f"sA{i}", [128, 512], F32, p3) for i in range(2)]
            sProd = sbuf("sProd", [128, 4, 512], F32, p3)
            yf = [sbuf(f"yf{i}", [128, 512], F32, p3) for i in range(2)]
            ost = sbuf("ost", [128, 4, 128], F32, p3)
            psA3 = [psum(f"psA3{i}", [128, 512], F32, p3) for i in range(2)]
            psB3 = [psum(f"psB3{i}", [128, 512], F32, p3) for i in range(2)]
            psX3 = psum("psX3", [128, 1024], BF16, p3)
            psY = [psum(f"psY{i}", [128, 512], F32, p3) for i in range(2)]
            psYT = psum("psYT", [128, 512], F32, p3)

            ck3 = {}
            SGRP = [(0, 4), (4, 8), (8, HC)]

            def w13_ap(e, half):
                h0, h1 = (0, 6) if half == 0 else (6, HC)
                base = (e * HC + h0) * (128 * 2048)
                return bass.AP(w13s.tensor, base,
                               [[2048, 128], [128 * 2048, h1 - h0], [1, 2048]])

            with nc.Block("p3") as blk:
                @blk.gpsimd
                def _(g):
                    yGs.wait(g)
                    cast_sems["b2"].inc(g.dma_start(s_b2[:], b2_in[:]), 16)

                @blk.scalar
                def _(s):
                    s.add_instruction(mybir.InstLoadActFuncSet(
                        name=nc.get_next_instruction_name(),
                        act_func_set_id=SILU_SET, ins=[], outs=[]))
                    yGs.wait(s)
                    yX1.wait(s)
                    yXn2.wait(s)
                    yRw.wait(s)
                    yL25[0].wait(s)
                    yL25[1].wait(s)

                @blk.sync
                def _(sy):
                    yX1.wait(sy)
                    yXn2.wait(sy)
                    yRw.wait(sy)
                    yGs.wait(sy)
                    yL25[0].wait(sy)
                    yL25[1].wait(sy)

                _cnd = [0]

                def cond(dec, e, c, fn, fn_else=None):
                    _cnd[0] += 1
                    nm = f"cnd{_cnd[0]}"

                    @dec
                    def _(eng, e=e, c=c, fn=fn, fn_else=fn_else, nm=nm):
                        snap = {s_: s_.n for s_ in sems.values()}
                        with eng.register(nm + "a") as rt, eng.register(nm + "b") as rc:
                            eng.reg_load(rt, s_toti[0:1, e:e + 1])
                            eng.reg_mov(rc, 512 * c)
                            with eng.If_lt(rc, rt):
                                fn(eng)
                                eng.drain()
                            with eng.Else():
                                if fn_else is not None:
                                    fn_else(eng)
                                else:
                                    for s_ in sems.values():
                                        d = s_.n - snap[s_]
                                        for _ in range(d):
                                            eng.sem_inc(s_.h, 1)
                                eng.drain()

                rix = [0]
                for e in range(8):
                    for half in range(2):
                        @blk.sync
                        def _(sy, e=e, half=half):
                            if e >= 1:
                                t3.wait(sy, ck3[("w13use", e - 1, half)])
                            cast_wait(sy, f"w13_{e}")
                            hn = 6 if half == 0 else 5
                            yW13[half].inc(sy.dma_start(
                                w13e[half][:, 0:hn * 2048], w13_ap(e, half)), 16)
                            ck3[("w13ld", e, half)] = yW13[half].n

                    @blk.scalar
                    def _(s, e=e):
                        if e >= 1:
                            t3.wait(s, ck3[("w2use", e - 1)])
                        cast_wait(s, f"w2_{e}")
                        yW2.inc(s.dma_start(w2e[:], w2s2[e]), 16)
                        ck3[("w2ld", e)] = yW2.n

                    for c in range(NCH):
                        r = rix[0]
                        rix[0] += 1
                        cp = c % 2

                        def f_sync(eng, e=e, c=c, cp=cp):
                            if ("gT", e, c - 2) in ck3:
                                t3.wait(eng, ck3[("gT", e, c - 2)])
                                v3.wait(eng, ck3[("gV", e, c - 2)])
                            elif ("gT_last", cp) in ck3:
                                t3.wait(eng, ck3[("gT_last", cp)])
                                v3.wait(eng, ck3[("gV_last", cp)])
                            yGb[cp].inc(eng.dma_start(
                                gbuf[cp][:],
                                bass.AP(gathd.tensor,
                                        (e * GR_ + c * 512) * 1040,
                                        [[1040, 128], [128 * 1040, 4], [1, 1040]])), 16)
                        def f_sync_else(eng, e=e, c=c, cp=cp, r_=r, prev=dict(ck3)):
                            if ("gT", e, c - 2) in prev:
                                t3.wait(eng, prev[("gT", e, c - 2)])
                                v3.wait(eng, prev[("gV", e, c - 2)])
                            elif ("gT_last", cp) in prev:
                                t3.wait(eng, prev[("gT_last", cp)])
                                v3.wait(eng, prev[("gV_last", cp)])
                            eng.dma_start(ddum[0:1, r_:r_ + 1], gathd[0:1, 0:1]
                                          ).then_inc(yGb[cp].h, 16)
                        cond(blk.sync, e, c, f_sync, fn_else=f_sync_else)
                        ck3[("gld", e, c)] = yGb[cp].n

                        for sub in range(4):
                            def f_Tt(eng, e=e, c=c, cp=cp, sub=sub):
                                if sub == 0:
                                    yGb[cp].wait(eng, ck3[("gld", e, c)])
                                if ("x3ev",) in ck3:
                                    v3.wait(eng, ck3[("x3ev",)])
                                for i in range(8):
                                    mm = eng.transpose(
                                        psX3[:, i * 128:(i + 1) * 128],
                                        gbuf[cp][:, sub, i * 128:(i + 1) * 128],
                                        s_identb[:])
                                t3.inc(mm)
                                ck3[("x3t",)] = t3.n
                            cond(blk.tensor, e, c, f_Tt)

                            def f_Vt(eng, e=e, c=c, cp=cp, sub=sub):
                                t3.wait(eng, ck3[("x3t",)])
                                if sub == 0 and ("gfmfree", cp) in ck3:
                                    t3.wait(eng, ck3[("gfmfree", cp)])
                                mm = eng.tensor_copy(
                                    bass.AP(gfm[cp].ap().tensor,
                                            gfm[cp].ap().offset + sub * 128,
                                            [gfm[cp].ap().ap[0], [512, DC], [1, 128]]),
                                    psX3[:].rearrange("p (d t) -> p d t", d=DC))
                                v3.inc(mm)
                                ck3[("x3ev",)] = v3.n
                            cond(blk.vector, e, c, f_Vt)

                        def f_Vidx(eng, e=e, c=c, cp=cp):
                            if ("sca",) in ck3:
                                gsc3.wait(eng, ck3[("sca",)])
                            eng.tensor_copy(rwfc[:], gbuf[cp][:, :, 1024 + e])
                            eng.drain()
                            eng.scalar_tensor_tensor(
                                yidxf[:], gbuf[cp][:, :, 1032], c_128,
                                gbuf[cp][:, :, 1033], AluOpType.mult, AluOpType.add)
                            eng.drain()
                            mm = eng.tensor_copy(yidxu[:], yidxf[:])
                            v3.inc(mm)
                            ck3[("yidx", e, c)] = v3.n
                        cond(blk.vector, e, c, f_Vidx)
                        ck3[("gV", e, c)] = v3.n
                        ck3[("gV_last", cp)] = v3.n

                        def f_T2a(eng, e=e, c=c, cp=cp, vbase=None):
                            vbase = v3.n
                            v3.wait(eng, ck3[("x3ev",)])
                            if c == 0:
                                yW13[0].wait(eng, ck3[("w13ld", e, 0)])
                                yW13[1].wait(eng, ck3[("w13ld", e, 1)])
                            for hc in range(HC):
                                hp = hc % 2
                                half = 0 if hc < 6 else 1
                                h0 = 0 if hc < 6 else 6
                                if hc >= 2:
                                    eng.wait_ge(v3.h, vbase + hc - 1)
                                elif ("ab_ev", hp) in ck3:
                                    v3.wait(eng, ck3[("ab_ev", hp)])
                                base = (hc - h0) * 2048
                                for dc in range(DC):
                                    eng.matmul(psA3[hp][:],
                                               w13e[half][:, base + dc * 128:base + dc * 128 + 128],
                                               gfm[cp][:, dc, :],
                                               start=(dc == 0), stop=(dc == DC - 1))
                                for dc in range(DC):
                                    mm = eng.matmul(psB3[hp][:],
                                                    w13e[half][:, base + 1024 + dc * 128:base + 1024 + dc * 128 + 128],
                                                    gfm[cp][:, dc, :],
                                                    start=(dc == 0), stop=(dc == DC - 1))
                                t3.inc(mm)
                                ck3[("ab", hp)] = t3.n
                        cond(blk.tensor, e, c, f_T2a)
                        ck3[("gfmfree", cp)] = t3.n
                        ck3[("gT", e, c)] = t3.n
                        ck3[("gT_last", cp)] = t3.n

                        def f_V2(eng, e=e, c=c, cp=cp):
                            sbase = s3.n
                            for hc in range(HC):
                                hp = hc % 2
                                slot = hc % 4
                                grp = hc // 4
                                t3.wait(eng, ck3[("abck", e, c, hc)])
                                if grp >= 1:
                                    eng.wait_ge(s3.h, sbase + grp)
                                elif ("sgrp_last",) in ck3:
                                    s3.wait(eng, ck3[("sgrp_last",)])
                                eng.tensor_scalar(sA[hp][:], psA3[hp][:],
                                                  s_b13[:, e, hc, 0:1], None, AluOpType.add)
                                eng.drain()
                                mm = eng.scalar_tensor_tensor(
                                    sProd[:, slot, :], psB3[hp][:], s_b13[:, e, hc, 1:2],
                                    sA[hp][:], AluOpType.add, AluOpType.mult)
                                v3.inc(mm)
                                ck3[("ab_ev", hp)] = v3.n
                                ck3[("pr", hc)] = v3.n
                        base_t = t3.n - HC
                        for hc in range(HC):
                            ck3[("abck", e, c, hc)] = base_t + hc + 1
                        cond(blk.vector, e, c, f_V2)

                        def f_S1(eng, e=e, c=c, cp=cp):
                            for gi, (g0_, g1_) in enumerate(SGRP):
                                v3.wait(eng, ck3[("prck", e, c, g1_ - 1)])
                                if ("hgbfree", cp) in ck3 and gi == 0:
                                    t3.wait(eng, ck3[("hgbfree", cp)])
                                ng = g1_ - g0_
                                mm = eng.activation(
                                    hgb[cp][:, g0_:g1_, :].rearrange("p g d -> p (g d)"),
                                    sProd[:, 0:ng, :].rearrange("p g d -> p (g d)"),
                                    AF.Silu)
                                s3.inc(mm)
                                ck3[("sgrp_p", gi)] = s3.n
                                ck3[("sgrp_last",)] = s3.n
                        base_v = v3.n - HC
                        for hc in range(HC):
                            ck3[("prck", e, c, hc)] = base_v + hc + 1
                        cond(blk.scalar, e, c, f_S1)
                        ck3[("hrdy", e, c)] = s3.n

                        def mk_TB(mcs, first, e=e, c=c, cp=cp):
                            def f(eng):
                                if first:
                                    s3.wait(eng, ck3[("hrdy", e, c)])
                                    yW2.wait(eng, ck3[("w2ld", e)])
                                for kind, mc in mcs:
                                    mp = mc % 2
                                    if kind == "mm":
                                        if ("yevck", mp) in ck3:
                                            s3.wait(eng, ck3[("yevck", mp)])
                                        for hc in range(HC):
                                            mm = eng.matmul(psY[mp][:],
                                                            w2e[:, hc, mc, :],
                                                            hgb[cp][:, hc, :],
                                                            start=(hc == 0), stop=(hc == HC - 1))
                                        t3.inc(mm)
                                        ck3[("ymm", mp)] = t3.n
                                    else:
                                        s3.wait(eng, ck3[("yfck", mc % 2)])
                                        if ("ytevck",) in ck3:
                                            v3.wait(eng, ck3[("ytevck",)])
                                        for sub in range(4):
                                            mm = eng.transpose(
                                                psYT[:, sub * 128:(sub + 1) * 128],
                                                yf[mc % 2][:, sub * 128:(sub + 1) * 128],
                                                s_identf[:])
                                        t3.inc(mm)
                                        ck3[("ytck",)] = t3.n
                            return f

                        def mk_SB(mcs, e=e, c=c):
                            def f(eng):
                                for mc in mcs:
                                    mp = mc % 2
                                    t3.wait(eng, ck3[("ymmck", mp)])
                                    if ("ytck2", mp) in ck3:
                                        t3.wait(eng, ck3[("ytck2", mp)])
                                    mm = eng.copy(yf[mp][:], psY[mp][:])
                                    s3.inc(mm)
                                    ck3[("yevck", mp)] = s3.n
                                    ck3[("yfck", mp)] = s3.n
                            return f

                        def mk_VB(mcs, e=e, c=c):
                            def f(eng):
                                for mc in mcs:
                                    t3.wait(eng, ck3[("ytck3", mc)])
                                    mm = eng.tensor_copy(
                                        bass.AP(ydst.ap().tensor,
                                                ydst.ap().offset + mc * 128,
                                                [ydst.ap().ap[0], [DM, 4], [1, 128]]),
                                        psYT[:].rearrange("p (s d) -> p s d", s=4))
                                    v3.inc(mm)
                                    ck3[("ytevck",)] = v3.n
                            return f

                        for mc in range(DC):
                            mp = mc % 2
                            cond(blk.tensor, e, c, mk_TB([("mm", mc)], first=(mc == 0)))
                            ck3[("ymmck", mp)] = t3.n
                            cond(blk.scalar, e, c, mk_SB([mc]))
                            cond(blk.tensor, e, c, mk_TB([("yt", mc)], first=False))
                            ck3[("ytck2", mp)] = t3.n
                            ck3[("ytck3", mc)] = t3.n
                            cond(blk.vector, e, c, mk_VB([mc]))
                        ck3[("hgbfree", cp)] = t3.n

                        def f_Vsc(eng, e=e, c=c):
                            v3.wait(eng, ck3[("yidx", e, c)])
                            if ("sca",) in ck3:
                                gsc3.wait(eng, ck3[("sca",)])
                            eng.drain()
                            for sub in range(4):
                                mm = eng.tensor_scalar(ydst[:, sub, :], ydst[:, sub, :],
                                                       rwfc[:, sub:sub + 1], None,
                                                       AluOpType.mult)
                            eng.drain()
                            v3.inc(mm)
                            ck3[("ysc", e, c)] = v3.n
                        cond(blk.vector, e, c, f_Vsc)

                        def f_G1(eng, e=e, c=c, r=r):
                            v3.wait(eng, ck3[("ysc", e, c)])
                            gsc3.wait(eng, 64 * r)
                            for sub in range(4):
                                gsc3.inc(eng.indirect_dma_start(
                                    x1s[:],
                                    bass.IndirectOffsetOnAxis(ap=yidxu[:, sub:sub + 1], axis=0),
                                    ydst[:, sub, :], None,
                                    compute_op=AluOpType.add), 16)
                        def f_G1_else(eng, r=r):
                            gsc3.wait(eng, 64 * r)
                            for sub in range(4):
                                eng.dma_start(ddumg[0:1, 4 * r + sub:4 * r + sub + 1],
                                              gathd[0:1, 0:1]).then_inc(gsc3.h, 16)
                        cond(blk.gpsimd, e, c, f_G1, fn_else=f_G1_else)
                        ck3[("sca",)] = gsc3.n

                    ck3[("w13use", e, 0)] = t3.n
                    ck3[("w13use", e, 1)] = t3.n
                    ck3[("w2use", e)] = t3.n


                for half in range(2):
                    @blk.sync
                    def _(sy, half=half):
                        t3.wait(sy, ck3[("w13use", 7, half)])
                        cast_wait(sy, "w13_8")
                        hn = 6 if half == 0 else 5
                        yW13[half].inc(sy.dma_start(
                            w13e[half][:, 0:hn * 2048], w13_ap(8, half)), 16)
                        ck3[("w13ld", 8, half)] = yW13[half].n

                @blk.scalar
                def _(s):
                    t3.wait(s, ck3[("w2use", 7)])
                    cast_wait(s, "w2_8")
                    yW2.inc(s.dma_start(w2e[:], w2s2[8]), 16)
                    ck3[("w2ld", 8)] = yW2.n

                for f in range(NQ):
                    @blk.sync
                    def _(sy, f=f):
                        if f >= 1:
                            t3.wait(sy, ck3[("ymmB", f - 1, DC - 1)])
                        yRwF.inc(sy.dma_start(
                            rwT_f[:], bass.AP(rws.tensor, f * 512, [[T, 8], [1, 512]])), 16)
                        ck3[("rwfld", f)] = yRwF.n
                        if f >= 1:
                            t3.wait(sy, ck3[("sAdoneB", f - 1)])
                        yXn2f.inc(sy.dma_start(xn2f[:], xn2s[f]), 16)
                        if f == 0:
                            gsc3.wait(sy)
                        if f >= 1:
                            v3.wait(sy, ck3[("ost", f - 1)])
                        yX1t.inc(sy.dma_start(
                            x1t[:],
                            bass.AP(x1s.tensor, f * 512 * DM,
                                    [[DM, 128], [128 * DM, 4], [1, DM]])), 16)

                    for hc in range(HC):
                        hp = hc % 2
                        slot = hc % 4
                        grp = hc // 4
                        half = 0 if hc < 6 else 1
                        h0 = 0 if hc < 6 else 6

                        @blk.tensor
                        def _(tt, f=f, hc=hc, hp=hp, half=half, h0=h0):
                            if hc == 0:
                                if f == 0:
                                    yW13[0].wait(tt, ck3[("w13ld", 8, 0)])
                                    yW13[1].wait(tt, ck3[("w13ld", 8, 1)])
                                yXn2f.wait(tt)
                            if ("ab_evB", hp) in ck3:
                                v3.wait(tt, ck3[("ab_evB", hp)])
                            elif f == 0 and hc < 2 and ("ab_ev", hp) in ck3:
                                v3.wait(tt, ck3[("ab_ev", hp)])
                            base = (hc - h0) * 2048
                            for dc in range(DC):
                                tt.matmul(psA3[hp][:],
                                          w13e[half][:, base + dc * 128:base + dc * 128 + 128],
                                          xn2f[:, dc, :],
                                          start=(dc == 0), stop=(dc == DC - 1))
                            for dc in range(DC):
                                mm = tt.matmul(psB3[hp][:],
                                               w13e[half][:, base + 1024 + dc * 128:base + 1024 + dc * 128 + 128],
                                               xn2f[:, dc, :],
                                               start=(dc == 0), stop=(dc == DC - 1))
                            t3.inc(mm)
                            ck3[("abB", hp)] = t3.n
                            if hc == HC - 1:
                                ck3[("sAdoneB", f)] = t3.n

                        @blk.vector
                        def _(v, f=f, hc=hc, hp=hp, slot=slot, grp=grp):
                            t3.wait(v, ck3[("abB", hp)])
                            if ("sgrpB", grp - 1) in ck3:
                                s3.wait(v, ck3[("sgrpB", grp - 1)])
                            elif grp == 0 and ("sgrpB_last",) in ck3:
                                s3.wait(v, ck3[("sgrpB_last",)])
                            elif grp == 0 and ("sgrp_last",) in ck3 and f == 0:
                                s3.wait(v, ck3[("sgrp_last",)])
                            v.tensor_scalar(sA[hp][:], psA3[hp][:],
                                            s_b13[:, 8, hc, 0:1], None, AluOpType.add)
                            v.drain()
                            mm = v.scalar_tensor_tensor(
                                sProd[:, slot, :], psB3[hp][:], s_b13[:, 8, hc, 1:2],
                                sA[hp][:], AluOpType.add, AluOpType.mult)
                            v3.inc(mm)
                            ck3[("ab_evB", hp)] = v3.n
                            ck3[("prB", hc)] = v3.n

                        if hc == [g1 - 1 for g0, g1 in SGRP if g0 <= hc < g1][0]:
                            gi = [i for i, (g0, g1) in enumerate(SGRP) if g0 <= hc < g1][0]
                            g0_, g1_ = SGRP[gi]

                            @blk.scalar
                            def _(s, f=f, gi=gi, g0_=g0_, g1_=g1_):
                                v3.wait(s, ck3[("prB", g1_ - 1)])
                                ng = g1_ - g0_
                                if f >= 1 and gi == 0:
                                    t3.wait(s, ck3[("ymmB", f - 1, DC - 1)])
                                elif f == 0 and gi == 0 and ("hgbfree", 0) in ck3:
                                    t3.wait(s, ck3[("hgbfree", 0)])
                                mm = s.activation(
                                    hgb[0][:, g0_:g1_, :].rearrange("p g d -> p (g d)"),
                                    sProd[:, 0:ng, :].rearrange("p g d -> p (g d)"),
                                    AF.Silu)
                                s3.inc(mm)
                                ck3[("sgrpB", gi)] = s3.n
                                ck3[("sgrpB_last",)] = s3.n
                                if g1_ == HC:
                                    ck3[("hrdyB", f)] = s3.n

                    for mc in range(DC):
                        mp = mc % 2

                        @blk.tensor
                        def _(tt, f=f, mc=mc, mp=mp):
                            s3.wait(tt, ck3[("hrdyB", f)])
                            if mc == 0:
                                if f == 0:
                                    cast_wait(tt, "b2")
                                yRwF.wait(tt, ck3[("rwfld", f)])
                            if ("yevB", mp) in ck3:
                                s3.wait(tt, ck3[("yevB", mp)])
                            elif f == 0 and mc < 2 and ("yevck", mp) in ck3:
                                s3.wait(tt, ck3[("yevck", mp)])
                            yW2.wait(tt, ck3[("w2ld", 8)])
                            for hc in range(HC):
                                tt.matmul(psY[mp][:], w2e[:, hc, mc, :], hgb[0][:, hc, :],
                                          start=(hc == 0), stop=False)
                            mm = tt.matmul(psY[mp][:], s_b2[:, mc * 128:(mc + 1) * 128],
                                           rwT_f[:], start=False, stop=True)
                            t3.inc(mm)
                            ck3[("ymmB", f, mc)] = t3.n

                        @blk.scalar
                        def _(s, f=f, mc=mc, mp=mp):
                            t3.wait(s, ck3[("ymmB", f, mc)])
                            s3.inc(s.activation(yf[mp][:], psY[mp][:], AF.Identity,
                                                bias=s_sb2[:, mc:mc + 1]))
                            ck3[("yevB", mp)] = s3.n

                        @blk.tensor
                        def _(tt, f=f, mc=mc, mp=mp):
                            s3.wait(tt, ck3[("yevB", mp)])
                            if ("ytevB",) in ck3:
                                v3.wait(tt, ck3[("ytevB",)])
                            elif ("ytevck",) in ck3 and f == 0 and mc == 0:
                                v3.wait(tt, ck3[("ytevck",)])
                            for sub in range(4):
                                mm = tt.transpose(psYT[:, sub * 128:(sub + 1) * 128],
                                                  yf[mp][:, sub * 128:(sub + 1) * 128],
                                                  s_identf[:])
                            t3.inc(mm)
                            ck3[("ytB", f, mc)] = t3.n

                        @blk.vector
                        def _(v, f=f, mc=mc):
                            t3.wait(v, ck3[("ytB", f, mc)])
                            yX1t.wait(v, 16 * (f + 1))
                            k_out = f * DC + mc
                            if k_out >= 1:
                                yOut.wait(v, 64 * k_out)
                            v3.inc(v.tensor_tensor(
                                ost[:],
                                psYT[:].rearrange("p (s d) -> p s d", s=4),
                                bass.AP(x1t.ap().tensor, x1t.ap().offset + mc * 128,
                                        [x1t.ap().ap[0], [DM, 4], [1, 128]]),
                                AluOpType.add))
                            ck3[("ytevB",)] = v3.n
                            if mc == DC - 1:
                                ck3[("ost", f)] = v3.n

                        @blk.sync
                        def _(sy, f=f, mc=mc):
                            v3.wait(sy, ck3[("ytevB",)])
                            for sub in range(4):
                                yOut.inc(sy.dma_start(
                                    out_d[f * 512 + sub * 128:f * 512 + (sub + 1) * 128,
                                          mc * 128:(mc + 1) * 128],
                                    ost[:, sub, :]), 16)

                @blk.sync
                def _(sy):
                    yOut.wait(sy)
                    for s_ in list(sems.values()) + list(cast_sems.values()):
                        if s_.n:
                            sy.wait_ge(s_.h, s_.n)

    return nc
    return nc



def _prep_inputs(T, ins):
    NT = T // 128
    f32 = np.float32
    w1, w3, w2 = ins["L0_ffn_w1"], ins["L0_ffn_w3"], ins["L0_ffn_w2"]
    sw1, sw3, sw2 = ins["L0_ffn_sw1"], ins["L0_ffn_sw3"], ins["L0_ffn_sw2"]
    b1, b3, b2 = ins["L0_ffn_b1"], ins["L0_ffn_b3"], ins["L0_ffn_b2"]
    sb1, sb3, sb2 = ins["L0_ffn_sb1"], ins["L0_ffn_sb3"], ins["L0_ffn_sb2"]

    w1a = np.concatenate([w1, sw1[None]], 0)
    w3a = np.concatenate([w3, sw3[None]], 0)
    w13 = np.stack([w1a, w3a], 1)
    w13 = w13.reshape(NE, 2, DC, 128, HC, 128)
    w13 = np.ascontiguousarray(w13.transpose(0, 4, 3, 1, 2, 5))

    w2a = np.concatenate([w2, sw2[None]], 0)
    w2a = w2a.reshape(NE, HC, 128, DC, 128)
    w2a = np.ascontiguousarray(w2a.transpose(0, 2, 1, 3, 4))

    b1a = np.concatenate([b1, sb1[None]], 0)
    b3a = np.concatenate([b3, sb3[None]], 0)
    b13 = np.stack([b1a, b3a], -1).reshape(NE, HC, 128, 2)
    b13 = np.ascontiguousarray(b13.transpose(2, 0, 1, 3))

    gate_w = ins["L0_ffn_gate_w"]
    gwt = np.ascontiguousarray(gate_w.T.reshape(DC, 128, 8).transpose(1, 0, 2))
    gbb = np.ascontiguousarray(
        np.broadcast_to(np.asarray(ins["L0_ffn_gate_b"], f32)[None, :], (128, 8)))

    pos = np.arange(T, dtype=np.float64)
    freqs = np.power(10000.0, -np.arange(64, dtype=np.float64) / 64.0)
    rad = pos[:, None] * freqs[None, :]
    cosT = np.cos(rad).astype(ml_dtypes.bfloat16).reshape(NT, 128, 64)
    sinT = np.sin(rad).astype(ml_dtypes.bfloat16).reshape(NT, 128, 64)

    j = np.arange(128)[:, None]
    qq = np.arange(512)[None, :]
    masks = np.stack([(qq >= j + 128 * k) for k in range(4)], 0).astype(ml_dtypes.bfloat16)
    masks = np.ascontiguousarray(masks.transpose(1, 0, 2))

    gzt = np.zeros((4, 128, 1040), ml_dtypes.bfloat16)
    for k in range(4):
        gzt[k, :, 1032] = ml_dtypes.bfloat16(float(NT + k))
    gzt[:, :, 1033] = np.arange(128, dtype=np.float32).astype(
        ml_dtypes.bfloat16)[None, :]
    gzt = gzt.reshape(512, 1040)

    ccols = np.zeros((128, 6), f32)
    ccols[:, 0] = EPS_LN
    ccols[:, 1] = EPS_RMS
    ccols[:, 2] = -1.0
    ccols[:, 3] = NEG_BIG
    ccols[:, 4] = 128.0
    ccols[:, 5] = 511.0

    return {
        "wq": np.ascontiguousarray(ins["L0_attn_wq"].reshape(DM, DM)),
        "wk": np.ascontiguousarray(ins["L0_attn_wk"].reshape(DM, DM)),
        "wv": np.ascontiguousarray(ins["L0_attn_wv"].reshape(DM, DM)),
        "wo": np.ascontiguousarray(ins["L0_attn_wo"]),
        "w13": w13, "w2": w2a, "b13": b13,
        "b2": np.ascontiguousarray(b2),
        "sb2c": np.ascontiguousarray(np.asarray(sb2, f32).reshape(DC, 128).T),
        "gwt": gwt, "gbb": gbb,
        "cosT": cosT, "sinT": sinT,
        "identb": np.eye(128, dtype=ml_dtypes.bfloat16),
        "identf": np.eye(128, dtype=f32),
        "ones1b": np.ones((1, 128), ml_dtypes.bfloat16),
        "onescb": np.ones((128, 1), ml_dtypes.bfloat16),
        "ones1f": np.ones((1, 128), f32),
        "masksb": masks,
        "ccols": ccols,
        "triu": np.triu(np.ones((128, 128))).astype(ml_dtypes.bfloat16),
        "pidx": np.arange(128, dtype=np.float32).astype(ml_dtypes.bfloat16).reshape(128, 1),
        "bigc": np.full((128, 8), 1.0e9, np.float32),
        "eofs": np.ascontiguousarray(
            np.broadcast_to((np.arange(8, dtype=np.float32) * (T + 512))[None, :],
                            (128, 8))),
        "gzt": gzt,
        "zrow": np.zeros((128, DM), f32),
    }


def kernel(**ins):
    x = np.asarray(ins["L0_attn_x"])
    B, T, _ = x.shape
    base = _prep_inputs(T, {k: np.asarray(v) for k, v in ins.items()})
    nc = build_nc(T)
    in_maps = []
    for b in range(B):
        m = dict(base)
        m["x"] = np.ascontiguousarray(x[b])
        in_maps.append(m)
    res = run_bass_kernel_spmd(nc, in_maps, core_ids=list(range(B)))
    return np.stack([np.asarray(r["out"]) for r in res.results], 0)


# version 15
# speedup vs baseline: 1.1137x; 1.1137x over previous
import numpy as np
import ml_dtypes
from contextlib import ExitStack

import concourse.bass as bass
import concourse.mybir as mybir
from concourse.bass_utils import run_bass_kernel_spmd
from concourse.alu_op_type import AluOpType

F32 = mybir.dt.float32
BF16 = mybir.dt.bfloat16
U32 = mybir.dt.uint32
I32 = mybir.dt.int32

def _act_set_id(name):
    from concourse.hw_specs import get_activation_tables
    import concourse.bass as _b
    _nc = _b.Bass("TRN2", target_bir_lowering=False)
    return list(get_activation_tables(_nc.m.arch)).index(name)

LNEXP_SET = None
SILU_SET = None
AF = mybir.ActivationFunctionType
AX = mybir.AxisListType.X

H = 8
DH = 128
DM = 1024
DC = 8
HF = 1408
HC = 11
NE = 9
EPS_LN = 1e-5
EPS_RMS = 1e-6
NEG_BIG = -1.0e30

W13_SL = [(h, min(h + 2, HC)) for h in range(0, HC, 2)]
W2_GR = [(0, 3), (3, 6), (6, 9)]


class Ctr:

    def __init__(self, ctx, nc, name):
        self.h = ctx.enter_context(nc.semaphore(name))
        self.n = 0

    def inc(self, instr, k=1):
        instr.then_inc(self.h, k)
        self.n += k
        return instr

    def wait(self, eng, n=None):
        eng.wait_ge(self.h, self.n if n is None else n)


def build_nc(T, debug=False):
    from concourse.hw_specs import get_activation_tables
    NT = T // 128
    NQ = T // 512
    nc = bass.Bass("TRN2", target_bir_lowering=False, debug=debug)
    tab_names = list(get_activation_tables(nc.m.arch))
    LNEXP_SET = tab_names.index("natural_log_exp_and_others")
    SILU_SET = tab_names.index("silu_and_others")

    x_in = nc.dram_tensor("x", [T, DM], F32, kind="ExternalInput").ap()
    wq_in = nc.dram_tensor("wq", [DM, DM], F32, kind="ExternalInput").ap()
    wk_in = nc.dram_tensor("wk", [DM, DM], F32, kind="ExternalInput").ap()
    wv_in = nc.dram_tensor("wv", [DM, DM], F32, kind="ExternalInput").ap()
    wo_in = nc.dram_tensor("wo", [H, DH, DM], F32, kind="ExternalInput").ap()
    w13_in = nc.dram_tensor("w13", [NE, HC, 128, 2, DC, 128], F32, kind="ExternalInput").ap()
    w2_in = nc.dram_tensor("w2", [NE, 128, HC, DC, 128], F32, kind="ExternalInput").ap()
    b13_in = nc.dram_tensor("b13", [128, NE, HC, 2], F32, kind="ExternalInput").ap()
    b2_in = nc.dram_tensor("b2", [8, DM], F32, kind="ExternalInput").ap()
    sb2_in = nc.dram_tensor("sb2c", [128, DC], F32, kind="ExternalInput").ap()
    gwt_in = nc.dram_tensor("gwt", [128, DC, 8], F32, kind="ExternalInput").ap()
    gbb_in = nc.dram_tensor("gbb", [128, 8], F32, kind="ExternalInput").ap()
    cos_in = nc.dram_tensor("cosT", [NT, 128, 64], BF16, kind="ExternalInput").ap()
    sin_in = nc.dram_tensor("sinT", [NT, 128, 64], BF16, kind="ExternalInput").ap()
    identb_in = nc.dram_tensor("identb", [128, 128], BF16, kind="ExternalInput").ap()
    identf_in = nc.dram_tensor("identf", [128, 128], F32, kind="ExternalInput").ap()
    ones1b_in = nc.dram_tensor("ones1b", [1, 128], BF16, kind="ExternalInput").ap()
    onescb_in = nc.dram_tensor("onescb", [128, 1], BF16, kind="ExternalInput").ap()
    ones1f_in = nc.dram_tensor("ones1f", [1, 128], F32, kind="ExternalInput").ap()
    masks_in = nc.dram_tensor("masksb", [128, 4, 512], BF16, kind="ExternalInput").ap()
    triu_in = nc.dram_tensor("triu", [128, 128], BF16, kind="ExternalInput").ap()
    pidx_in = nc.dram_tensor("pidx", [128, 1], BF16, kind="ExternalInput").ap()
    bigc_in = nc.dram_tensor("bigc", [128, 8], F32, kind="ExternalInput").ap()
    eofs_in = nc.dram_tensor("eofs", [128, 8], F32, kind="ExternalInput").ap()
    ccol_in = nc.dram_tensor("ccols", [128, 6], F32, kind="ExternalInput").ap()
    gzt_in = nc.dram_tensor("gzt", [512, 1040], BF16, kind="ExternalInput").ap()
    zrow_in = nc.dram_tensor("zrow", [128, DM], F32, kind="ExternalInput").ap()

    out_d = nc.dram_tensor("out", [T, DM], F32, kind="ExternalOutput").ap()

    w13s = nc.dram_tensor("w13s", [NE, HC, 128, 2, DC, 128], BF16).ap()
    w2s2 = nc.dram_tensor("w2s", [NE, 128, HC, DC, 128], BF16).ap()
    qs = nc.dram_tensor("qscr", [NT, 128, H, 128], BF16).ap()
    attnscr = nc.dram_tensor("attnscr", [NQ, DC, 128, 512], F32).ap()
    x1s = nc.dram_tensor("x1scr", [T + 512, DM], F32).ap()
    xn2s = nc.dram_tensor("xn2scr", [NQ, 128, DC, 512], BF16).ap()
    rws = nc.dram_tensor("rwscr", [8, T], BF16).ap()
    GR = T + 512
    gathd = nc.dram_tensor("gathd", [8 * (T + 512), 1040], BF16).ap()

    with ExitStack() as top:
        def sbuf(name, shape, dt, stack):
            return stack.enter_context(nc.sbuf_tensor(name, shape, dt))

        def psum(name, shape, dt, stack):
            return stack.enter_context(nc.psum_tensor(name, shape, dt))

        s_identb = sbuf("s_identb", [128, 128], BF16, top)
        s_identf = sbuf("s_identf", [128, 128], F32, top)
        s_ones1b = sbuf("s_ones1b", [1, 128], BF16, top)
        s_onescb = sbuf("s_onescb", [128, 1], BF16, top)
        s_ccol = sbuf("s_ccol", [128, 6], F32, top)
        s_b13 = sbuf("s_b13", [128, NE, HC, 2], F32, top)
        s_sb2 = sbuf("s_sb2", [128, DC], F32, top)
        s_gwt = sbuf("s_gwt", [128, DC, 8], BF16, top)
        s_gbb = sbuf("s_gbb", [128, 8], F32, top)
        s_pidx = sbuf("s_pidx", [128, 1], BF16, top)
        s_triu = sbuf("s_triu", [128, 128], BF16, top)
        s_bigc = sbuf("s_bigc", [128, 8], F32, top)
        s_eofs = sbuf("s_eofs", [128, 8], F32, top)
        s_toti = sbuf("s_toti", [1, 8], I32, top)

        c_epsln = s_ccol[:, 0:1]
        c_epsrms = s_ccol[:, 1:2]
        c_neg1 = s_ccol[:, 2:3]
        c_negbig = s_ccol[:, 3:4]
        c_128 = s_ccol[:, 4:5]

        cast_sems = {}
        for cname in (["wqkv", "wo", "gate", "b2", "gz"] + [f"w13_{e}" for e in range(NE)]
                      + [f"w2_{e}" for e in range(NE)]):
            cast_sems[cname] = Ctr(top, nc, "cast_" + cname)

        def cast_wait(eng, cname):
            s = cast_sems[cname]
            eng.wait_ge(s.h, s.n)

        sems = {}
        for name in ["c0",
                     "v1", "s1", "t1",
                     "v2", "s2", "t2",
                     "v25", "s25", "t25",
                     "v3", "s3", "t3",
                     "yLd0", "yLd1", "yQs",
                     "yQq", "yAt0", "yAt1",
                     "yL25a", "yL25b", "yX1", "yXn2",
                     "yW13a", "yW13b", "yW13c", "yW2a", "yW2b", "yW2c",
                     "yXn2f", "yX1t", "yOut", "yRw", "yRwF", "yGs", "gsc3"]:
            sems[name] = Ctr(top, nc, "sem_" + name)
        c0 = sems["c0"]

        with ExitStack() as p012:
            k_sb = sbuf("k_sb", [128, H, NT, 128], BF16, p012)
            s_ones1f = sbuf("s_ones1f", [1, 128], F32, p012)
            v_sb = sbuf("v_sb", [128, H, NT, 128], BF16, p012)

            v1, s1, t1 = sems["v1"], sems["s1"], sems["t1"]
            yLd = [sems["yLd0"], sems["yLd1"]]
            yQs = sems["yQs"]
            with ExitStack() as p1:
                wq_sb = sbuf("wq_sb", [128, DC, DM], BF16, p1)
                wk_sb = sbuf("wk_sb", [128, DC, DM], BF16, p1)
                wv_sb = sbuf("wv_sb", [128, DC, DM], BF16, p1)
                xb = [sbuf(f"xb{i}", [128, DM], F32, p1) for i in range(2)]
                cb = [sbuf(f"cb{i}", [128, 64], BF16, p1) for i in range(2)]
                sb_ = [sbuf(f"sb{i}", [128, 64], BF16, p1) for i in range(2)]
                mv = sbuf("mv", [128, 14], F32, p1)
                xn_tm = sbuf("xn_tm", [128, DM], BF16, p1)
                xn_fm = sbuf("xn_fm", [128, DM], BF16, p1)
                sqb = sbuf("sqb", [128, DM], F32, p1)
                qkb_q = sbuf("qkb_q", [128, DM], BF16, p1)
                qkb_k = sbuf("qkb_k", [128, DM], BF16, p1)
                rt_q = sbuf("rt_q", [128, DM], BF16, p1)
                rt_k = sbuf("rt_k", [128, DM], BF16, p1)
                q_rope = sbuf("q_rope", [128, DM], BF16, p1)
                k_rope = sbuf("k_rope", [128, DM], BF16, p1)
                qfm_st = rt_q
                msq = sbuf("msq", [128, 8], F32, p1)
                rsq = sbuf("rsq", [128, 8], F32, p1)
                msq2 = sbuf("msq2", [128, 8], F32, p1)
                rsq2 = sbuf("rsq2", [128, 8], F32, p1)
                psQ = psum("psQ", [128, DM], F32, p1)
                psK = psum("psK", [128, DM], F32, p1)
                psV = psum("psV", [128, DM], F32, p1)
                psXP = psum("psXP", [128, DM], BF16, p1)
                psT2 = psum("psT2", [128, DM], BF16, p1)

                ck = {}

                with nc.Block("p1") as blk:
                    @blk.scalar
                    def _(s):
                        s.add_instruction(mybir.InstLoadActFuncSet(
                            name=nc.get_next_instruction_name(),
                            act_func_set_id=LNEXP_SET, ins=[], outs=[]))
                        c0.wait(s, 192)

                    @blk.sync
                    def _(sy):
                        for dst, srcap in [
                            (s_identb[:], identb_in[:]), (s_identf[:], identf_in[:]),
                            (s_ones1b[:], ones1b_in[:]), (s_onescb[:], onescb_in[:]),
                            (s_ccol[:], ccol_in[:]), (s_b13[:], b13_in[:]),
                            (s_sb2[:], sb2_in[:]), (s_gbb[:], gbb_in[:]),
                            (s_pidx[:], pidx_in[:]), (s_triu[:], triu_in[:]),
                            (s_bigc[:], bigc_in[:]), (s_eofs[:], eofs_in[:]),
                        ]:
                            c0.inc(sy.dma_start(dst, srcap), 16)

                    @blk.gpsimd
                    def _(g):
                        cq = cast_sems["wqkv"]
                        cq.inc(g.dma_start(wq_sb[:], wq_in.rearrange("(c p) n -> p c n", p=128)), 16)
                        cq.inc(g.dma_start(wk_sb[:], wk_in.rearrange("(c p) n -> p c n", p=128)), 16)
                        cq.inc(g.dma_start(wv_sb[:], wv_in.rearrange("(c p) n -> p c n", p=128)), 16)
                        cast_sems["gate"].inc(g.dma_start(s_gwt[:], gwt_in[:]), 16)
                        gz = cast_sems["gz"]
                        nblk = 8 * (T + 512) // 512
                        for b_ in range(nblk):
                            gz.inc(g.dma_start(
                                gathd[b_ * 512:(b_ + 1) * 512, :], gzt_in[:]), 16)
                        for sub in range(4):
                            gz.inc(g.dma_start(
                                x1s[T + sub * 128:T + (sub + 1) * 128, :],
                                zrow_in[:]), 16)
                        for e in range(NE):
                            cast_sems[f"w13_{e}"].inc(g.dma_start(w13s[e], w13_in[e]), 16)
                        for e in range(NE):
                            cast_sems[f"w2_{e}"].inc(g.dma_start(w2s2[e], w2_in[e]), 16)

                    @blk.vector
                    def _(v):
                        c0.wait(v, 192)

                    @blk.tensor
                    def _(tt):
                        c0.wait(tt, 192)

                    def emit_qk_T(t):
                        for which, ro in [("q", q_rope), ("k", k_rope)]:
                            @blk.tensor
                            def _(tt, t=t, which=which, ro=ro):
                                v1.wait(tt, ck[("rope", t, which)])
                                if not (t == 0 and which == "q"):
                                    v1.wait(tt, ck[("psT2",)])
                                for h in range(H):
                                    mm = tt.transpose(psT2[:, h * 128:(h + 1) * 128],
                                                      ro[:, h * 128:(h + 1) * 128], s_identb[:])
                                t1.inc(mm)
                                ck[("qT" if which == "q" else "kT", t)] = t1.n

                            @blk.vector
                            def _(v, t=t, which=which):
                                t1.wait(v)
                                if which == "q":
                                    if t >= 1:
                                        yQs.wait(v, 16 * t)
                                    v1.inc(v.tensor_copy(qfm_st[:], psT2[:]))
                                else:
                                    v1.inc(v.tensor_copy(
                                        k_sb[:, :, t, :],
                                        psT2[:].rearrange("p (h d) -> p h d", h=H)))
                                ck[("psT2",)] = v1.n
                                if which == "q":
                                    ck[("qfm", t)] = v1.n

                        @blk.sync
                        def _(sy, t=t):
                            v1.wait(sy, ck[("qfm", t)])
                            yQs.inc(sy.dma_start(qs[t].rearrange("p h d -> p (h d)"), qfm_st[:]), 16)

                    def emit_loads(t):
                        p = t % 2

                        @blk.sync
                        def _(sy, t=t, p=p):
                            if t >= 2:
                                v1.wait(sy, ck[("ropek", t - 2)])
                                s1.wait(sy, ck[("apply", t - 2)])
                            yLd[p].inc(sy.dma_start(xb[p][:], x_in[t * 128:(t + 1) * 128, :]), 16)
                            yLd[p].inc(sy.dma_start(cb[p][:], cos_in[t]), 16)
                            yLd[p].inc(sy.dma_start(sb_[p][:], sin_in[t]), 16)
                            ck[("xld", t)] = yLd[p].n

                    def emit_ln(t):
                        p = t % 2

                        @blk.vector
                        def _(v, t=t, p=p):
                            yLd[p].wait(v, ck[("xld", t)])
                            v.bn_stats(mv[:, 0:6], xb[p][:, 0:512])
                            v.bn_stats(mv[:, 6:12], xb[p][:, 512:1024])
                            v.drain()
                            v.bn_aggr(mv[:, 12:14], mv[:, 0:12])
                            v.drain()
                            v.tensor_scalar(mv[:, 13:14], mv[:, 13:14], c_epsln,
                                            None, AluOpType.add)
                            v.drain()
                            v1.inc(v.tensor_scalar(mv[:, 11:12], mv[:, 12:13], c_neg1,
                                                   None, AluOpType.mult))
                            ck[("stats", t)] = v1.n

                        @blk.scalar
                        def _(s, t=t, p=p):
                            v1.wait(s, ck[("stats", t)])
                            s.activation(mv[:, 13:14], mv[:, 13:14], AF.Ln)
                            s.drain()
                            s.activation(mv[:, 13:14], mv[:, 13:14], AF.Exp, scale=-0.5)
                            s.drain()
                            s.activation(mv[:, 12:13], mv[:, 11:12], AF.Identity,
                                         scale=mv[:, 13:14])
                            s.drain()
                            if t >= 1:
                                t1.wait(s, ck[("xnT", t - 1)])
                            s1.inc(s.activation(xn_tm[:], xb[p][:], AF.Identity,
                                                bias=mv[:, 12:13], scale=mv[:, 13:14]))
                            ck[("apply", t)] = s1.n

                    emit_loads(0)
                    emit_ln(0)

                    def emit_xnT(t):
                        @blk.tensor
                        def _(tt, t=t):
                            s1.wait(tt, ck[("apply", t)])
                            if t >= 1:
                                v1.wait(tt, ck[("xnfm", t - 1)])
                            for i in range(8):
                                mmt = tt.transpose(psXP[:, i * 128:(i + 1) * 128],
                                                   xn_tm[:, i * 128:(i + 1) * 128], s_identb[:])
                            t1.inc(mmt)
                            ck[("xnT", t)] = t1.n

                        @blk.vector
                        def _(v, t=t):
                            t1.wait(v, ck[("xnT", t)])
                            if t >= 1:
                                t1.wait(v, ck[("qkv", t - 1)])
                            v1.inc(v.tensor_copy(xn_fm[:], psXP[:]))
                            ck[("xnfm", t)] = v1.n

                    emit_xnT(0)

                    for t in range(NT):
                        p = t % 2

                        if t + 1 < NT:
                            emit_loads(t + 1)

                        @blk.tensor
                        def _(tt, t=t):
                            if t == 0:
                                cast_wait(tt, "wqkv")
                            v1.wait(tt, ck[("xnfm", t)])
                            if t >= 1:
                                s1.wait(tt, ck[("kscale", t - 1)])
                                v1.wait(tt, ck[("vev", t - 1)])
                            for ps, w in [(psQ, wq_sb), (psK, wk_sb), (psV, wv_sb)]:
                                for hlf in (0, 512):
                                    for dc in range(DC):
                                        mm = tt.matmul(ps[:, hlf:hlf + 512],
                                                       xn_fm[:, dc * 128:(dc + 1) * 128],
                                                       w[:, dc, hlf:hlf + 512],
                                                       start=(dc == 0), stop=(dc == DC - 1))
                            t1.inc(mm)
                            ck[("qkv", t)] = t1.n

                        if t >= 1:
                            emit_qk_T(t - 1)

                        if t + 1 < NT:
                            emit_ln(t + 1)
                            emit_xnT(t + 1)

                        @blk.vector
                        def _(v, t=t):
                            t1.wait(v, ck[("qkv", t)])
                            v1.inc(v.tensor_copy(
                                v_sb[:, :, t, :],
                                psV[:].rearrange("p (h d) -> p h d", h=H)))
                            ck[("vev", t)] = v1.n

                        @blk.scalar
                        def _(s, t=t):
                            t1.wait(s, ck[("qkv", t)])
                            if t >= 1:
                                v1.wait(s, ck[("kred", t - 1)])
                            s1.inc(s.activation(sqb[:], psQ[:], AF.Square))
                            ck[("qsq", t)] = s1.n

                        @blk.vector
                        def _(v, t=t):
                            s1.wait(v, ck[("qsq", t)])
                            v1.inc(v.reduce_sum(
                                msq[:], sqb[:].rearrange("p (h d) -> p h d", h=H), axis=AX))
                            ck[("qred", t)] = v1.n

                        @blk.scalar
                        def _(s, t=t):
                            v1.wait(s, ck[("qred", t)])
                            s1.inc(s.activation(sqb[:], psK[:], AF.Square))
                            ck[("ksq", t)] = s1.n

                        @blk.scalar
                        def _(s, t=t):
                            s.activation(rsq[:], msq[:], AF.Ln, bias=c_epsrms, scale=1.0 / DH)
                            s.drain()
                            s.activation(rsq[:], rsq[:], AF.Exp, scale=-0.5)
                            s.drain()
                            for h in range(H):
                                mm = s.activation(qkb_q[:, h * 128:(h + 1) * 128],
                                                  psQ[:, h * 128:(h + 1) * 128],
                                                  AF.Identity, scale=rsq[:, h:h + 1])
                            s1.inc(mm)
                            ck[("qscale", t)] = s1.n

                        @blk.vector
                        def _(v, t=t):
                            s1.wait(v, ck[("ksq", t)])
                            v1.inc(v.reduce_sum(
                                msq2[:], sqb[:].rearrange("p (h d) -> p h d", h=H), axis=AX))
                            ck[("kred", t)] = v1.n

                        def emit_rope(t, which, ro, qkb, scale_ck):
                            p = t % 2

                            @blk.vector
                            def _(v, t=t, which=which, ro=ro, qkb=qkb):
                                s1.wait(v, ck[(scale_ck, t)])
                                if t >= 1:
                                    t1.wait(v, ck[("qT" if which == "q" else "kT", t - 1)])
                                if which == "q" and t >= 1:
                                    yQs.wait(v, 16 * t)
                                rt = rt_q if which == "q" else rt_k
                                r3 = lambda ap: ap.rearrange("p (h d) -> p h d", h=H)
                                ev, od = r3(qkb[:])[:, :, 0:64], r3(qkb[:])[:, :, 64:128]
                                oe, oo = r3(ro[:])[:, :, 0:64], r3(ro[:])[:, :, 64:128]
                                te, to = r3(rt[:])[:, :, 0:64], r3(rt[:])[:, :, 64:128]
                                cosb = bass.AP(cb[p].ap().tensor, cb[p].ap().offset,
                                               [cb[p].ap().ap[0], [0, H], [1, 64]])
                                sinb = bass.AP(sb_[p].ap().tensor, sb_[p].ap().offset,
                                               [sb_[p].ap().ap[0], [0, H], [1, 64]])
                                v.tensor_tensor(te, ev, cosb, AluOpType.mult)
                                v.tensor_tensor(to, od, sinb, AluOpType.mult)
                                v.drain()
                                v.tensor_tensor(oe, te, to, AluOpType.subtract)
                                v.drain()
                                v.tensor_tensor(te, ev, sinb, AluOpType.mult)
                                v.tensor_tensor(to, od, cosb, AluOpType.mult)
                                v.drain()
                                v1.inc(v.tensor_tensor(oo, te, to, AluOpType.add))
                                ck[("rope", t, which)] = v1.n
                                if which == "k":
                                    ck[("ropek", t)] = v1.n

                        emit_rope(t, "q", q_rope, qkb_q, "qscale")

                        @blk.scalar
                        def _(s, t=t):
                            v1.wait(s, ck[("kred", t)])
                            s.activation(rsq2[:], msq2[:], AF.Ln, bias=c_epsrms, scale=1.0 / DH)
                            s.drain()
                            s.activation(rsq2[:], rsq2[:], AF.Exp, scale=-0.5)
                            s.drain()
                            for h in range(H):
                                mm = s.activation(qkb_k[:, h * 128:(h + 1) * 128],
                                                  psK[:, h * 128:(h + 1) * 128],
                                                  AF.Identity, scale=rsq2[:, h:h + 1])
                            s1.inc(mm)
                            ck[("kscale", t)] = s1.n

                        emit_rope(t, "k", k_rope, qkb_k, "kscale")

                    emit_qk_T(NT - 1)


            v2, s2, t2 = sems["v2"], sems["s2"], sems["t2"]
            yQq = sems["yQq"]
            yAt = [sems["yAt0"], sems["yAt1"]]
            with ExitStack() as p2:
                s_masks = sbuf("s_masks", [128, 4, 512], BF16, p2)
                wo_sb = sbuf("wo_sb", [128, H, DC, 128], BF16, p2)
                qq = sbuf("qq", [128, 4, H, 128], BF16, p2)
                pt_sl = [sbuf(f"pt{i}", [128, 1024], BF16, p2) for i in range(2)]
                ptm = sbuf("ptm", [128, 4, 512], BF16, p2)
                sden = [sbuf(f"sden{i}", [1, 512], F32, p2) for i in range(2)]
                srden = [sbuf(f"srden{i}", [1, 512], F32, p2) for i in range(2)]
                sdb = [sbuf(f"sdb{i}", [128, 512], F32, p2) for i in range(2)]
                o_all = sbuf("o_all", [128, H, 512], BF16, p2)
                sattn = [sbuf(f"sattn{i}", [128, 512], F32, p2) for i in range(2)]
                psS = [psum(f"psS{i}", [128, 1024], F32, p2) for i in range(2)]
                psO = [psum(f"psO{i}", [128, 512], F32, p2) for i in range(2)]
                psDD = [psum(f"psDD{i}", [128, 512], F32, p2) for i in range(2)]
                psA = psS[0][:, 0:512]

                ck2 = {"exp_last": {}, "pv_last": {}, "onorm_p": {}, "wodone": 0}

                with nc.Block("p2") as blk:
                    @blk.gpsimd
                    def _(g):
                        cast_sems["wo"].inc(g.dma_start(
                            wo_sb[:],
                            bass.AP(wo_in.tensor, 0,
                                    [[1024, 128], [131072, H], [128, DC], [1, 128]])), 16)

                    @blk.scalar
                    def _(s):
                        s.add_instruction(mybir.InstLoadActFuncSet(
                            name=nc.get_next_instruction_name(),
                            act_func_set_id=LNEXP_SET, ins=[], outs=[]))

                    @blk.sync
                    def _(sy):
                        yQs.wait(sy)
                        yLd[0].wait(sy)
                        yLd[1].wait(sy)
                        c0.inc(sy.dma_start(s_masks[:], masks_in[:]), 16)
                        c0.inc(sy.dma_start(s_ones1f[:], ones1f_in[:]), 16)

                    for qg in range(NQ):
                        JC = 4 * (qg + 1)
                        nbatch = (JC + 1) // 2

                        @blk.sync
                        def _(sy, qg=qg):
                            if qg >= 1:
                                t2.wait(sy, ck2[("qq_free", qg - 1)])
                            for i in range(4):
                                yQq.inc(sy.dma_start(qq[:, i, :, :], qs[4 * qg + i]), 16)

                        for h in range(H):
                            hp = h % 2
                            exp_ck, pv_ck = {}, {}
                            batches = [list(range(2 * b, min(2 * b + 2, JC)))
                                       for b in range(nbatch)]

                            def emit_st(b, qg=qg, h=h):
                                jcs = batches[b]

                                @blk.tensor
                                def _(tt):
                                    if b == 0 and h == 0:
                                        yQq.wait(tt)
                                    if (b - 2) in exp_ck:
                                        s2.wait(tt, exp_ck[b - 2])
                                    elif b < 2 and (b % 2) in ck2["exp_last"]:
                                        s2.wait(tt, ck2["exp_last"][b % 2])
                                    for i, jc in enumerate(jcs):
                                        mm = tt.matmul(psS[b % 2][:, i * 512:(i + 1) * 512],
                                                       k_sb[:, h, jc, :], qq[:, :, h, :],
                                                       start=True, stop=True)
                                    t2.inc(mm)
                                    ck2[("st", b)] = t2.n

                            def emit_exp(b, qg=qg, h=h):
                                jcs = batches[b]

                                @blk.scalar
                                def _(s):
                                    t2.wait(s, ck2[("st", b)])
                                    if (b - 2) in pv_ck:
                                        t2.wait(s, pv_ck[b - 2])
                                    elif b < 2 and (b % 2) in ck2["pv_last"]:
                                        t2.wait(s, ck2["pv_last"][b % 2])
                                    s2.inc(s.activation(pt_sl[b % 2][:, 0:len(jcs) * 512],
                                                        psS[b % 2][:, 0:len(jcs) * 512],
                                                        AF.Exp, scale=1.0 / DH))
                                    exp_ck[b] = s2.n
                                    ck2["exp_last"][b % 2] = s2.n

                                diag = [i for i, jc in enumerate(jcs) if jc >= 4 * qg]
                                if diag:
                                    @blk.vector
                                    def _(v):
                                        if qg == 0 and h == 0:
                                            c0.wait(v)
                                        s2.wait(v, exp_ck[b])
                                        for i in diag:
                                            kk = jcs[i] - 4 * qg
                                            mm = v.tensor_tensor(
                                                ptm[:, kk, :],
                                                pt_sl[b % 2][:, i * 512:(i + 1) * 512],
                                                s_masks[:, kk, :], AluOpType.mult)
                                        v2.inc(mm)
                                        ck2[("mask", b)] = v2.n

                            def emit_pv(b, qg=qg, h=h, hp=hp):
                                jcs = batches[b]

                                @blk.tensor
                                def _(tt):
                                    s2.wait(tt, exp_ck[b])
                                    if any(jc >= 4 * qg for jc in jcs):
                                        v2.wait(tt, ck2[("mask", b)])
                                    if b == 0 and (hp in ck2["onorm_p"]):
                                        v2.wait(tt, ck2["onorm_p"][hp])
                                    for i, jc in enumerate(jcs):
                                        src = (ptm[:, jc - 4 * qg, :] if jc >= 4 * qg
                                               else pt_sl[b % 2][:, i * 512:(i + 1) * 512])
                                        tt.matmul(psO[hp][:], v_sb[:, h, jc, :], src,
                                                  start=(jc == 0), stop=(jc == JC - 1))
                                        mm = tt.matmul(psDD[hp][0:1, :], s_onescb[:], src,
                                                       start=(jc == 0), stop=(jc == JC - 1))
                                    t2.inc(mm)
                                    pv_ck[b] = t2.n
                                    ck2["pv_last"][b % 2] = t2.n

                            emit_st(0)
                            emit_exp(0)
                            for b in range(1, nbatch):
                                emit_st(b)
                                emit_pv(b - 1)
                                emit_exp(b)
                            emit_pv(nbatch - 1)

                            @blk.vector
                            def _(v, qg=qg, h=h, hp=hp, pv_ck=pv_ck, nbatch=nbatch):
                                t2.wait(v, pv_ck[nbatch - 1])
                                if qg >= 1 and h == 0:
                                    t2.wait(v, ck2["wodone"])
                                v.tensor_copy(sden[hp][:], psDD[hp][0:1, :])
                                v.drain()
                                v2.inc(v.reciprocal(srden[hp][:], sden[hp][:]))

                            @blk.tensor
                            def _(tt, qg=qg, h=h, hp=hp):
                                if qg == 0 and h == 0:
                                    c0.wait(tt)
                                v2.wait(tt)
                                t2.inc(tt.matmul(psDD[hp][:], s_ones1f[:], srden[hp][:],
                                                 start=True, stop=True))

                            @blk.vector
                            def _(v, qg=qg, h=h, hp=hp):
                                t2.wait(v)
                                v.tensor_copy(sdb[hp][:], psDD[hp][:])
                                v.drain()
                                v2.inc(v.tensor_tensor(o_all[:, h, :], psO[hp][:], sdb[hp][:],
                                                       AluOpType.mult))
                                ck2["onorm_p"][hp] = v2.n
                                if h == H - 1:
                                    ck2[("heads_done", qg)] = v2.n

                        ck2[("qq_free", qg)] = t2.n

                        for mc in range(DC):
                            @blk.tensor
                            def _(tt, qg=qg, mc=mc):
                                if qg == 0 and mc == 0:
                                    cast_wait(tt, "wo")
                                v2.wait(tt, ck2[("heads_done", qg)])
                                if ("attn_ev", qg, mc - 1) in ck2:
                                    v2.wait(tt, ck2[("attn_ev", qg, mc - 1)])
                                elif qg >= 1 and mc == 0:
                                    v2.wait(tt, ck2[("attn_ev", qg - 1, DC - 1)])
                                for h in range(H):
                                    mm = tt.matmul(psA, wo_sb[:, h, mc, :], o_all[:, h, :],
                                                   start=(h == 0), stop=(h == H - 1))
                                t2.inc(mm)
                                if mc == DC - 1:
                                    ck2["wodone"] = t2.n

                            @blk.vector
                            def _(v, qg=qg, mc=mc):
                                t2.wait(v)
                                k_st = qg * DC + mc
                                if k_st >= 2:
                                    yAt[mc % 2].wait(v, 16 * (k_st // 2))
                                v2.inc(v.tensor_copy(sattn[mc % 2][:], psA))
                                ck2[("attn_ev", qg, mc)] = v2.n

                            @blk.sync
                            def _(sy, qg=qg, mc=mc):
                                v2.wait(sy, ck2[("attn_ev", qg, mc)])
                                yAt[mc % 2].inc(sy.dma_start(attnscr[qg, mc], sattn[mc % 2][:]), 16)

            v25, s25, t25 = sems["v25"], sems["s25"], sems["t25"]
            yL25 = [sems["yL25a"], sems["yL25b"]]
            yX1 = sems["yX1"]
            yXn2 = sems["yXn2"]
            yRw = sems["yRw"]
            yGs = sems["yGs"]
            with ExitStack() as p25:
                xb2 = [sbuf(f"xb2_{i}", [128, DM], F32, p25) for i in range(2)]
                afm = [sbuf(f"afm{i}", [128, DC, 128], F32, p25) for i in range(2)]
                x1tm = sbuf("x1tm", [128, DM], F32, p25)
                xn2tm = sbuf("xn2tm", [128, DM], BF16, p25)
                xn2st = [sbuf(f"xn2st{i}", [128, DC, 512], BF16, p25) for i in range(2)]
                mv2 = sbuf("mv2", [128, 14], F32, p25)
                sxp = sbuf("sxp", [128, 8], F32, p25)
                sume = sbuf("sume", [128, 1], F32, p25)
                rsum = sbuf("rsum", [128, 1], F32, p25)
                rsc = sbuf("rsc", [128, 8], F32, p25)
                bsd = sbuf("bsd", [128, 8], F32, p25)
                m1c = sbuf("m1c", [128, 1], F32, p25)
                mask1 = sbuf("mask1", [128, 8], F32, p25)
                b2d = sbuf("b2d", [128, 8], F32, p25)
                m2c = sbuf("m2c", [128, 1], F32, p25)
                mask2 = sbuf("mask2", [128, 8], F32, p25)
                mask1i = sbuf("mask1i", [128, 8], I32, p25)
                rwbf = sbuf("rwbf", [128, 8], BF16, p25)
                rw_st = sbuf("rw_st", [8, 128], BF16, p25)
                selb = sbuf("selb", [128, 8], BF16, p25)
                base_b = sbuf("base_b", [128, 8], F32, p25)
                a_s = sbuf("a_s", [128, 8], F32, p25)
                idxf = sbuf("idxf", [128, 8], F32, p25)
                idxu = sbuf("idxu", [128, 8], U32, p25)
                totrow = sbuf("totrow", [1, 8], F32, p25)
                ttile = sbuf("ttile", [1, 8], F32, p25)
                stage = sbuf("stage", [128, 1040], BF16, p25)
                psT25 = psum("psT25", [128, DM], F32, p25)
                psX2 = psum("psX2", [128, DM], BF16, p25)
                psG = psum("psG", [128, 512], F32, p25)
                psRw = psum("psRw", [8, 512], BF16, p25)
                psTot = psum("psTot", [1, 512], F32, p25)

                ck25 = {}

                with nc.Block("p25") as blk:
                    @blk.vector
                    def _(v):
                        v.memset(stage[:], 0.0)
                        v25.inc(v.memset(totrow[:], 0.0))
                        ck25[("tot_snap", -1)] = v25.n

                    @blk.sync
                    def _(sy):
                        yAt[0].wait(sy)
                        yAt[1].wait(sy)
                        yQq.wait(sy)

                    for t in range(NT):
                        p = t % 2
                        qg, off = t // 4, (t % 4) * 128

                        @blk.sync
                        def _(sy, t=t, p=p, qg=qg, off=off):
                            if t >= 2:
                                v25.wait(sy, ck25[("vfin", t - 2)])
                            yL25[p].inc(sy.dma_start(xb2[p][:], x_in[t * 128:(t + 1) * 128, :]), 16)
                            yL25[p].inc(sy.dma_start(
                                afm[p][:],
                                bass.AP(attnscr.tensor, qg * DC * 128 * 512 + off,
                                        [[512, 128], [128 * 512, DC], [1, 128]])), 16)
                            ck25[("ld", t)] = yL25[p].n

                        @blk.tensor
                        def _(tt, t=t, p=p):
                            yL25[p].wait(tt, ck25[("ld", t)])
                            if t >= 1:
                                v25.wait(tt, ck25[("x1add", t - 1)])
                            for mc in range(DC):
                                mm = tt.transpose(psT25[:, mc * 128:(mc + 1) * 128],
                                                  afm[p][:, mc, :], s_identf[:])
                            t25.inc(mm)

                        @blk.vector
                        def _(v, t=t, p=p):
                            t25.wait(v)
                            if t >= 1:
                                yX1.wait(v, 16 * t)
                            v25.inc(v.tensor_tensor(x1tm[:], psT25[:], xb2[p][:], AluOpType.add))
                            ck25[("x1add", t)] = v25.n

                        @blk.sync
                        def _(sy, t=t):
                            v25.wait(sy, ck25[("x1add", t)])
                            yX1.inc(sy.dma_start(x1s[t * 128:(t + 1) * 128, :], x1tm[:]), 16)

                        @blk.vector
                        def _(v, t=t):
                            v.drain()
                            v.bn_stats(mv2[:, 0:6], x1tm[:, 0:512])
                            v.bn_stats(mv2[:, 6:12], x1tm[:, 512:1024])
                            v.drain()
                            v.bn_aggr(mv2[:, 12:14], mv2[:, 0:12])
                            v.drain()
                            v.tensor_scalar(mv2[:, 13:14], mv2[:, 13:14],
                                            c_epsln, None, AluOpType.add)
                            v.drain()
                            v25.inc(v.tensor_scalar(mv2[:, 11:12], mv2[:, 12:13], c_neg1,
                                                    None, AluOpType.mult))

                        @blk.scalar
                        def _(s, t=t):
                            v25.wait(s)
                            s.activation(mv2[:, 13:14], mv2[:, 13:14], AF.Ln)
                            s.drain()
                            s.activation(mv2[:, 13:14], mv2[:, 13:14], AF.Exp, scale=-0.5)
                            s.drain()
                            s.activation(mv2[:, 12:13], mv2[:, 11:12], AF.Identity,
                                         scale=mv2[:, 13:14])
                            s.drain()
                            if t >= 1:
                                t25.wait(s, ck25[("x2t", t - 1)])
                            s25.inc(s.activation(xn2tm[:], x1tm[:], AF.Identity,
                                                 bias=mv2[:, 12:13], scale=mv2[:, 13:14]))
                            ck25[("apply2", t)] = s25.n

                        @blk.tensor
                        def _(tt, t=t):
                            s25.wait(tt, ck25[("apply2", t)])
                            if t >= 1:
                                v25.wait(tt, ck25[("x2ev", t - 1)])
                            for i in range(DC):
                                mm = tt.transpose(psX2[:, i * 128:(i + 1) * 128],
                                                  xn2tm[:, i * 128:(i + 1) * 128], s_identb[:])
                            t25.inc(mm)
                            ck25[("x2t", t)] = t25.n

                        @blk.vector
                        def _(v, t=t, qg=qg, off=off):
                            t25.wait(v, ck25[("x2t", t)])
                            if t % 4 == 0 and qg >= 2:
                                yXn2.wait(v, 16 * qg)
                                t25.wait(v, ck25[("gmm", 4 * (qg - 2) + 3)])
                            v25.inc(v.tensor_copy(
                                xn2st[qg % 2][:, :, off:off + 128],
                                psX2[:].rearrange("p (c d) -> p c d", c=DC)))
                            ck25[("x2ev", t)] = v25.n

                        @blk.sync
                        def _(sy, t=t, qg=qg):
                            if t % 4 == 3:
                                v25.wait(sy, ck25[("x2ev", t)])
                                yXn2.inc(sy.dma_start(xn2s[qg], xn2st[qg % 2][:]), 16)

                        @blk.tensor
                        def _(tt, t=t, qg=qg, off=off):
                            if t == 0:
                                cast_wait(tt, "gate")
                            v25.wait(tt, ck25[("x2ev", t)])
                            if t >= 1:
                                s25.wait(tt, ck25[("gexp", t - 1)])
                            for dc in range(DC):
                                mm = tt.matmul(psG[:, 0:8],
                                               xn2st[qg % 2][:, dc, off:off + 128],
                                               s_gwt[:, dc, :],
                                               start=(dc == 0), stop=(dc == DC - 1))
                            t25.inc(mm)
                            ck25[("gmm", t)] = t25.n

                        @blk.vector
                        def _(v, t=t):
                            t25.wait(v, ck25[("gmm", t)])
                            v25.inc(v.reduce_max(m1c[:], psG[:, 0:8], axis=AX, negate=True))

                        @blk.scalar
                        def _(s, t=t):
                            v25.wait(s)
                            s25.inc(s.activation(sxp[:], psG[:, 0:8], AF.Exp,
                                                 bias=m1c[:], accum_out=sume[:]))
                            ck25[("gexp", t)] = s25.n

                        @blk.vector
                        def _(v, t=t):
                            s25.wait(v, ck25[("gexp", t)])
                            v.reciprocal(rsum[:], sume[:])
                            v.drain()
                            v.tensor_scalar(rsc[:], sxp[:], rsum[:], None, AluOpType.mult)
                            v.drain()
                            v.tensor_tensor(bsd[:], rsc[:], s_gbb[:], AluOpType.add)
                            v.drain()
                            v.reduce_max(m1c[:], bsd[:], axis=AX)
                            v.drain()
                            v.tensor_scalar(mask1[:], bsd[:], m1c[:], None, AluOpType.is_ge)
                            v.drain()
                            v.scalar_tensor_tensor(b2d[:], mask1[:], c_negbig, bsd[:],
                                                   AluOpType.mult, AluOpType.add)
                            v.drain()
                            v.reduce_max(m2c[:], b2d[:], axis=AX)
                            v.drain()
                            v.tensor_scalar(mask2[:], b2d[:], m2c[:], None, AluOpType.is_ge)
                            v.drain()
                            v.tensor_tensor(mask1[:], mask1[:], mask2[:], AluOpType.add)
                            v.drain()
                            v25.inc(v.tensor_tensor(rwbf[:], rsc[:], mask1[:], AluOpType.mult))

                        @blk.tensor
                        def _(tt, t=t):
                            v25.wait(tt)
                            if t >= 1:
                                v25.wait(tt, ck25[("rwev", t - 1)])
                            t25.inc(tt.transpose(psRw[:, 0:128], rwbf[:], s_identb[:]))

                        @blk.vector
                        def _(v, t=t):
                            t25.wait(v)
                            if t >= 1:
                                yRw.wait(v, 16 * t)
                            v25.inc(v.tensor_copy(rw_st[:], psRw[:, 0:128]))
                            ck25[("rwev", t)] = v25.n

                        @blk.sync
                        def _(sy, t=t):
                            v25.wait(sy, ck25[("rwev", t)])
                            yRw.inc(sy.dma_start(
                                bass.AP(rws.tensor, t * 128, [[T, 8], [1, 128]]),
                                rw_st[:]), 16)

                        @blk.vector
                        def _(v, t=t):
                            v.drain()
                            v25.inc(v.tensor_copy(selb[:], mask1[:]))
                            ck25[("selb", t)] = v25.n

                        @blk.tensor
                        def _(tt, t=t):
                            v25.wait(tt, ck25[("selb", t)])
                            v25.wait(tt, ck25[("tot_snap", t - 1)])
                            tt.matmul(psG[:, 16:24], s_ones1f[:], totrow[:],
                                      start=True, stop=True)
                            tt.matmul(psG[:, 8:16], s_triu[:], selb[:],
                                      start=True, stop=True)
                            mmt = tt.matmul(psTot[0:1, 0:8], s_onescb[:], selb[:],
                                            start=True, stop=True)
                            t25.inc(mmt)
                            ck25[("preMM", t)] = t25.n

                        @blk.vector
                        def _(v, t=t):
                            t25.wait(v, ck25[("preMM", t)])
                            if t >= 1:
                                yGs.wait(v, 128 * t)
                            v.tensor_copy(base_b[:], psG[:, 16:24])
                            v.drain()
                            v.tensor_tensor(a_s[:], psG[:, 8:16], base_b[:], AluOpType.add)
                            v.drain()
                            v.tensor_copy(mask1i[:], mask1[:])
                            v.drain()
                            v.select(idxf[:], mask1i[:], a_s[:], s_bigc[:], add_drain=True)
                            v.drain()
                            v.tensor_tensor(idxf[:], idxf[:], mask1[:], AluOpType.subtract)
                            v.drain()
                            v.tensor_tensor(idxf[:], idxf[:], s_eofs[:], AluOpType.add)
                            v.drain()
                            v.tensor_copy(idxu[:], idxf[:])
                            v.drain()
                            v.tensor_copy(ttile[:], psTot[0:1, 0:8])
                            v.drain()
                            v.tensor_tensor(totrow[:], totrow[:], ttile[:], AluOpType.add)
                            v.drain()
                            v.tensor_copy(stage[:, 0:1024], xn2tm[:])
                            v.tensor_copy(stage[:, 1024:1032], rwbf[:])
                            v.memset(stage[:, 1032:1033], float(t))
                            v.tensor_copy(stage[:, 1033:1034], s_pidx[:])
                            v25.inc(v.drain())
                            ck25[("tot_snap", t)] = v25.n
                            ck25[("stage", t)] = v25.n

                        @blk.gpsimd
                        def _(g, t=t):
                            if t == 0:
                                gz_ = cast_sems["gz"]
                                g.wait_ge(gz_.h, gz_.n)
                            v25.wait(g, ck25[("stage", t)])
                            with g.register(f"bnd{t}") as rb:
                                g.reg_mov(rb, 8 * GR - 1)
                                for e in range(8):
                                    yGs.inc(g.indirect_dma_start(
                                        gathd[:],
                                        bass.IndirectOffsetOnAxis(ap=idxu[:, e:e + 1], axis=0),
                                        stage[:], None,
                                        bounds_check=rb, oob_is_err=False), 16)

                        ck25[("vfin", t)] = v25.n

                    @blk.vector
                    def _(v):
                        v.drain()
                        v25.inc(v.tensor_copy(s_toti[:], totrow[:]))

        v3, s3, t3 = sems["v3"], sems["s3"], sems["t3"]
        yW13 = [sems["yW13a"], sems["yW13b"]]
        yW2 = sems["yW2a"]
        yGb = [sems["yW2b"], sems["yW2c"]]
        yXn2f = sems["yXn2f"]
        yX1t = sems["yX1t"]
        yOut = sems["yOut"]
        yRwF = sems["yRwF"]
        gsc3 = sems["gsc3"]
        GR_ = T + 512
        NCH = T // 512
        with ExitStack() as p3:
            w13e = [sbuf(f"w13e{i}", [128, 6 * 2048], BF16, p3) for i in range(2)]
            w2e = sbuf("w2e", [128, HC, DC, 128], BF16, p3)
            gbuf = [sbuf(f"gbuf{i}", [128, 4, 1040], BF16, p3) for i in range(2)]
            gfm = [sbuf(f"gfm{i}", [128, DC, 512], BF16, p3) for i in range(2)]
            hgb = [sbuf(f"hgb{i}", [128, HC, 512], BF16, p3) for i in range(2)]
            ydst = sbuf("ydst", [128, 4, DM], F32, p3)
            rwfc = sbuf("rwfc", [128, 4], F32, p3)
            yidxf = sbuf("yidxf", [128, 4], F32, p3)
            yidxu = sbuf("yidxu", [128, 4], U32, p3)
            ddum = sbuf("ddum", [128, 8 * NCH + 8], BF16, p3)
            ddumg = sbuf("ddumg", [128, 4 * (8 * NCH + 8)], BF16, p3)
            xn2f = sbuf("xn2f", [128, DC, 512], BF16, p3)
            x1t = sbuf("x1t", [128, 4, DM], F32, p3)
            rwT_f = sbuf("rwT_f", [8, 512], BF16, p3)
            s_b2 = sbuf("s_b2", [8, DM], BF16, p3)
            sA = [sbuf(f"sA{i}", [128, 512], F32, p3) for i in range(2)]
            sProd = sbuf("sProd", [128, 4, 512], F32, p3)
            yf = [sbuf(f"yf{i}", [128, 512], F32, p3) for i in range(2)]
            ost = sbuf("ost", [128, 4, 128], F32, p3)
            psA3 = [psum(f"psA3{i}", [128, 512], F32, p3) for i in range(2)]
            psB3 = [psum(f"psB3{i}", [128, 512], F32, p3) for i in range(2)]
            psX3 = psum("psX3", [128, 1024], BF16, p3)
            psY = [psum(f"psY{i}", [128, 512], F32, p3) for i in range(2)]
            psYT = psum("psYT", [128, 512], F32, p3)

            ck3 = {}
            SGRP = [(0, 4), (4, 8), (8, HC)]

            def w13_ap(e, half):
                h0, h1 = (0, 6) if half == 0 else (6, HC)
                base = (e * HC + h0) * (128 * 2048)
                return bass.AP(w13s.tensor, base,
                               [[2048, 128], [128 * 2048, h1 - h0], [1, 2048]])

            with nc.Block("p3") as blk:
                @blk.gpsimd
                def _(g):
                    yGs.wait(g)
                    cast_sems["b2"].inc(g.dma_start(s_b2[:], b2_in[:]), 16)

                @blk.scalar
                def _(s):
                    s.add_instruction(mybir.InstLoadActFuncSet(
                        name=nc.get_next_instruction_name(),
                        act_func_set_id=SILU_SET, ins=[], outs=[]))
                    yGs.wait(s)
                    yX1.wait(s)
                    yXn2.wait(s)
                    yRw.wait(s)
                    yL25[0].wait(s)
                    yL25[1].wait(s)

                @blk.sync
                def _(sy):
                    yX1.wait(sy)
                    yXn2.wait(sy)
                    yRw.wait(sy)
                    yGs.wait(sy)
                    yL25[0].wait(sy)
                    yL25[1].wait(sy)

                _cnd = [0]

                def cond(dec, e, c, fn, fn_else=None):
                    _cnd[0] += 1
                    nm = f"cnd{_cnd[0]}"

                    @dec
                    def _(eng, e=e, c=c, fn=fn, fn_else=fn_else, nm=nm):
                        snap = {s_: s_.n for s_ in sems.values()}
                        with eng.register(nm + "a") as rt, eng.register(nm + "b") as rc:
                            eng.reg_load(rt, s_toti[0:1, e:e + 1])
                            eng.reg_mov(rc, 512 * c)
                            with eng.If_lt(rc, rt):
                                fn(eng)
                            with eng.Else():
                                eng.drain()
                                if fn_else is not None:
                                    fn_else(eng)
                                else:
                                    for s_ in sems.values():
                                        d = s_.n - snap[s_]
                                        for _ in range(d):
                                            eng.sem_inc(s_.h, 1)

                rix = [0]
                for e in range(8):
                    for half in range(2):
                        @blk.sync
                        def _(sy, e=e, half=half):
                            if e >= 1:
                                t3.wait(sy, ck3[("w13use", e - 1, half)])
                            cast_wait(sy, f"w13_{e}")
                            hn = 6 if half == 0 else 5
                            yW13[half].inc(sy.dma_start(
                                w13e[half][:, 0:hn * 2048], w13_ap(e, half)), 16)
                            ck3[("w13ld", e, half)] = yW13[half].n

                    @blk.scalar
                    def _(s, e=e):
                        if e >= 1:
                            t3.wait(s, ck3[("w2use", e - 1)])
                        cast_wait(s, f"w2_{e}")
                        yW2.inc(s.dma_start(w2e[:], w2s2[e]), 16)
                        ck3[("w2ld", e)] = yW2.n

                    for c in range(NCH):
                        r = rix[0]
                        rix[0] += 1
                        cp = c % 2

                        def f_sync(eng, e=e, c=c, cp=cp):
                            if ("gT", e, c - 2) in ck3:
                                t3.wait(eng, ck3[("gT", e, c - 2)])
                                v3.wait(eng, ck3[("gV", e, c - 2)])
                            elif ("gT_last", cp) in ck3:
                                t3.wait(eng, ck3[("gT_last", cp)])
                                v3.wait(eng, ck3[("gV_last", cp)])
                            yGb[cp].inc(eng.dma_start(
                                gbuf[cp][:],
                                bass.AP(gathd.tensor,
                                        (e * GR_ + c * 512) * 1040,
                                        [[1040, 128], [128 * 1040, 4], [1, 1040]])), 16)
                        def f_sync_else(eng, e=e, c=c, cp=cp, r_=r, prev=dict(ck3)):
                            if ("gT", e, c - 2) in prev:
                                t3.wait(eng, prev[("gT", e, c - 2)])
                                v3.wait(eng, prev[("gV", e, c - 2)])
                            elif ("gT_last", cp) in prev:
                                t3.wait(eng, prev[("gT_last", cp)])
                                v3.wait(eng, prev[("gV_last", cp)])
                            eng.dma_start(ddum[0:1, r_:r_ + 1], gathd[0:1, 0:1]
                                          ).then_inc(yGb[cp].h, 16)
                        cond(blk.sync, e, c, f_sync, fn_else=f_sync_else)
                        ck3[("gld", e, c)] = yGb[cp].n

                        for sub in range(4):
                            def f_Tt(eng, e=e, c=c, cp=cp, sub=sub):
                                if sub == 0:
                                    yGb[cp].wait(eng, ck3[("gld", e, c)])
                                if ("x3ev",) in ck3:
                                    v3.wait(eng, ck3[("x3ev",)])
                                for i in range(8):
                                    mm = eng.transpose(
                                        psX3[:, i * 128:(i + 1) * 128],
                                        gbuf[cp][:, sub, i * 128:(i + 1) * 128],
                                        s_identb[:])
                                t3.inc(mm)
                                ck3[("x3t",)] = t3.n
                            cond(blk.tensor, e, c, f_Tt)

                            def f_Vt(eng, e=e, c=c, cp=cp, sub=sub):
                                t3.wait(eng, ck3[("x3t",)])
                                if sub == 0 and ("gfmfree", cp) in ck3:
                                    t3.wait(eng, ck3[("gfmfree", cp)])
                                mm = eng.tensor_copy(
                                    bass.AP(gfm[cp].ap().tensor,
                                            gfm[cp].ap().offset + sub * 128,
                                            [gfm[cp].ap().ap[0], [512, DC], [1, 128]]),
                                    psX3[:].rearrange("p (d t) -> p d t", d=DC))
                                v3.inc(mm)
                                ck3[("x3ev",)] = v3.n
                            cond(blk.vector, e, c, f_Vt)

                        def f_Vidx(eng, e=e, c=c, cp=cp):
                            if ("sca",) in ck3:
                                gsc3.wait(eng, ck3[("sca",)])
                            eng.tensor_copy(rwfc[:], gbuf[cp][:, :, 1024 + e])
                            eng.drain()
                            eng.scalar_tensor_tensor(
                                yidxf[:], gbuf[cp][:, :, 1032], c_128,
                                gbuf[cp][:, :, 1033], AluOpType.mult, AluOpType.add)
                            eng.drain()
                            mm = eng.tensor_copy(yidxu[:], yidxf[:])
                            v3.inc(mm)
                            ck3[("yidx", e, c)] = v3.n
                        cond(blk.vector, e, c, f_Vidx)
                        ck3[("gV", e, c)] = v3.n
                        ck3[("gV_last", cp)] = v3.n

                        def f_T2a(eng, e=e, c=c, cp=cp, vbase=None):
                            vbase = v3.n
                            v3.wait(eng, ck3[("x3ev",)])
                            if c == 0:
                                yW13[0].wait(eng, ck3[("w13ld", e, 0)])
                                yW13[1].wait(eng, ck3[("w13ld", e, 1)])
                            for hc in range(HC):
                                hp = hc % 2
                                half = 0 if hc < 6 else 1
                                h0 = 0 if hc < 6 else 6
                                if hc >= 2:
                                    eng.wait_ge(v3.h, vbase + hc - 1)
                                elif ("ab_ev", hp) in ck3:
                                    v3.wait(eng, ck3[("ab_ev", hp)])
                                base = (hc - h0) * 2048
                                for dc in range(DC):
                                    eng.matmul(psA3[hp][:],
                                               w13e[half][:, base + dc * 128:base + dc * 128 + 128],
                                               gfm[cp][:, dc, :],
                                               start=(dc == 0), stop=(dc == DC - 1))
                                for dc in range(DC):
                                    mm = eng.matmul(psB3[hp][:],
                                                    w13e[half][:, base + 1024 + dc * 128:base + 1024 + dc * 128 + 128],
                                                    gfm[cp][:, dc, :],
                                                    start=(dc == 0), stop=(dc == DC - 1))
                                t3.inc(mm)
                                ck3[("ab", hp)] = t3.n
                        cond(blk.tensor, e, c, f_T2a)
                        ck3[("gfmfree", cp)] = t3.n
                        ck3[("gT", e, c)] = t3.n
                        ck3[("gT_last", cp)] = t3.n

                        def f_V2(eng, e=e, c=c, cp=cp):
                            sbase = s3.n
                            for hc in range(HC):
                                hp = hc % 2
                                slot = hc % 4
                                grp = hc // 4
                                t3.wait(eng, ck3[("abck", e, c, hc)])
                                if grp >= 1:
                                    eng.wait_ge(s3.h, sbase + grp)
                                elif ("sgrp_last",) in ck3:
                                    s3.wait(eng, ck3[("sgrp_last",)])
                                eng.tensor_scalar(sA[hp][:], psA3[hp][:],
                                                  s_b13[:, e, hc, 0:1], None, AluOpType.add)
                                eng.drain()
                                mm = eng.scalar_tensor_tensor(
                                    sProd[:, slot, :], psB3[hp][:], s_b13[:, e, hc, 1:2],
                                    sA[hp][:], AluOpType.add, AluOpType.mult)
                                v3.inc(mm)
                                ck3[("ab_ev", hp)] = v3.n
                                ck3[("pr", hc)] = v3.n
                        base_t = t3.n - HC
                        for hc in range(HC):
                            ck3[("abck", e, c, hc)] = base_t + hc + 1
                        cond(blk.vector, e, c, f_V2)

                        def f_S1(eng, e=e, c=c, cp=cp):
                            for gi, (g0_, g1_) in enumerate(SGRP):
                                v3.wait(eng, ck3[("prck", e, c, g1_ - 1)])
                                if ("hgbfree", cp) in ck3 and gi == 0:
                                    t3.wait(eng, ck3[("hgbfree", cp)])
                                ng = g1_ - g0_
                                mm = eng.activation(
                                    hgb[cp][:, g0_:g1_, :].rearrange("p g d -> p (g d)"),
                                    sProd[:, 0:ng, :].rearrange("p g d -> p (g d)"),
                                    AF.Silu)
                                s3.inc(mm)
                                ck3[("sgrp_p", gi)] = s3.n
                                ck3[("sgrp_last",)] = s3.n
                        base_v = v3.n - HC
                        for hc in range(HC):
                            ck3[("prck", e, c, hc)] = base_v + hc + 1
                        cond(blk.scalar, e, c, f_S1)
                        ck3[("hrdy", e, c)] = s3.n

                        def mk_TB(mcs, first, e=e, c=c, cp=cp):
                            def f(eng):
                                if first:
                                    s3.wait(eng, ck3[("hrdy", e, c)])
                                    yW2.wait(eng, ck3[("w2ld", e)])
                                for kind, mc in mcs:
                                    mp = mc % 2
                                    if kind == "mm":
                                        if ("yevck", mp) in ck3:
                                            s3.wait(eng, ck3[("yevck", mp)])
                                        for hc in range(HC):
                                            mm = eng.matmul(psY[mp][:],
                                                            w2e[:, hc, mc, :],
                                                            hgb[cp][:, hc, :],
                                                            start=(hc == 0), stop=(hc == HC - 1))
                                        t3.inc(mm)
                                        ck3[("ymm", mp)] = t3.n
                                    else:
                                        s3.wait(eng, ck3[("yfck", mc % 2)])
                                        if ("ytevck",) in ck3:
                                            v3.wait(eng, ck3[("ytevck",)])
                                        for sub in range(4):
                                            mm = eng.transpose(
                                                psYT[:, sub * 128:(sub + 1) * 128],
                                                yf[mc % 2][:, sub * 128:(sub + 1) * 128],
                                                s_identf[:])
                                        t3.inc(mm)
                                        ck3[("ytck",)] = t3.n
                            return f

                        def mk_SB(mcs, e=e, c=c):
                            def f(eng):
                                for mc in mcs:
                                    mp = mc % 2
                                    t3.wait(eng, ck3[("ymmck", mp)])
                                    if ("ytck2", mp) in ck3:
                                        t3.wait(eng, ck3[("ytck2", mp)])
                                    mm = eng.copy(yf[mp][:], psY[mp][:])
                                    s3.inc(mm)
                                    ck3[("yevck", mp)] = s3.n
                                    ck3[("yfck", mp)] = s3.n
                            return f

                        def mk_VB(mcs, e=e, c=c):
                            def f(eng):
                                for mc in mcs:
                                    t3.wait(eng, ck3[("ytck3", mc)])
                                    mm = eng.tensor_copy(
                                        bass.AP(ydst.ap().tensor,
                                                ydst.ap().offset + mc * 128,
                                                [ydst.ap().ap[0], [DM, 4], [1, 128]]),
                                        psYT[:].rearrange("p (s d) -> p s d", s=4))
                                    v3.inc(mm)
                                    ck3[("ytevck",)] = v3.n
                            return f

                        for mc in range(DC):
                            mp = mc % 2
                            cond(blk.tensor, e, c, mk_TB([("mm", mc)], first=(mc == 0)))
                            ck3[("ymmck", mp)] = t3.n
                            cond(blk.scalar, e, c, mk_SB([mc]))
                            cond(blk.tensor, e, c, mk_TB([("yt", mc)], first=False))
                            ck3[("ytck2", mp)] = t3.n
                            ck3[("ytck3", mc)] = t3.n
                            cond(blk.vector, e, c, mk_VB([mc]))
                        ck3[("hgbfree", cp)] = t3.n

                        def f_Vsc(eng, e=e, c=c):
                            v3.wait(eng, ck3[("yidx", e, c)])
                            if ("sca",) in ck3:
                                gsc3.wait(eng, ck3[("sca",)])
                            eng.drain()
                            for sub in range(4):
                                mm = eng.tensor_scalar(ydst[:, sub, :], ydst[:, sub, :],
                                                       rwfc[:, sub:sub + 1], None,
                                                       AluOpType.mult)
                            eng.drain()
                            v3.inc(mm)
                            ck3[("ysc", e, c)] = v3.n
                        cond(blk.vector, e, c, f_Vsc)

                        def f_G1(eng, e=e, c=c, r=r):
                            v3.wait(eng, ck3[("ysc", e, c)])
                            gsc3.wait(eng, 64 * r)
                            for sub in range(4):
                                gsc3.inc(eng.indirect_dma_start(
                                    x1s[:],
                                    bass.IndirectOffsetOnAxis(ap=yidxu[:, sub:sub + 1], axis=0),
                                    ydst[:, sub, :], None,
                                    compute_op=AluOpType.add), 16)
                        def f_G1_else(eng, r=r):
                            gsc3.wait(eng, 64 * r)
                            for sub in range(4):
                                eng.dma_start(ddumg[0:1, 4 * r + sub:4 * r + sub + 1],
                                              gathd[0:1, 0:1]).then_inc(gsc3.h, 16)
                        cond(blk.gpsimd, e, c, f_G1, fn_else=f_G1_else)
                        ck3[("sca",)] = gsc3.n

                    ck3[("w13use", e, 0)] = t3.n
                    ck3[("w13use", e, 1)] = t3.n
                    ck3[("w2use", e)] = t3.n


                for half in range(2):
                    @blk.sync
                    def _(sy, half=half):
                        t3.wait(sy, ck3[("w13use", 7, half)])
                        cast_wait(sy, "w13_8")
                        hn = 6 if half == 0 else 5
                        yW13[half].inc(sy.dma_start(
                            w13e[half][:, 0:hn * 2048], w13_ap(8, half)), 16)
                        ck3[("w13ld", 8, half)] = yW13[half].n

                @blk.scalar
                def _(s):
                    t3.wait(s, ck3[("w2use", 7)])
                    cast_wait(s, "w2_8")
                    yW2.inc(s.dma_start(w2e[:], w2s2[8]), 16)
                    ck3[("w2ld", 8)] = yW2.n

                for f in range(NQ):
                    @blk.sync
                    def _(sy, f=f):
                        if f >= 1:
                            t3.wait(sy, ck3[("ymmB", f - 1, DC - 1)])
                        yRwF.inc(sy.dma_start(
                            rwT_f[:], bass.AP(rws.tensor, f * 512, [[T, 8], [1, 512]])), 16)
                        ck3[("rwfld", f)] = yRwF.n
                        if f >= 1:
                            t3.wait(sy, ck3[("sAdoneB", f - 1)])
                        yXn2f.inc(sy.dma_start(xn2f[:], xn2s[f]), 16)
                        if f == 0:
                            gsc3.wait(sy)
                        if f >= 1:
                            v3.wait(sy, ck3[("ost", f - 1)])
                        yX1t.inc(sy.dma_start(
                            x1t[:],
                            bass.AP(x1s.tensor, f * 512 * DM,
                                    [[DM, 128], [128 * DM, 4], [1, DM]])), 16)

                    for hc in range(HC):
                        hp = hc % 2
                        slot = hc % 4
                        grp = hc // 4
                        half = 0 if hc < 6 else 1
                        h0 = 0 if hc < 6 else 6

                        @blk.tensor
                        def _(tt, f=f, hc=hc, hp=hp, half=half, h0=h0):
                            if hc == 0:
                                if f == 0:
                                    yW13[0].wait(tt, ck3[("w13ld", 8, 0)])
                                    yW13[1].wait(tt, ck3[("w13ld", 8, 1)])
                                yXn2f.wait(tt)
                            if ("ab_evB", hp) in ck3:
                                v3.wait(tt, ck3[("ab_evB", hp)])
                            elif f == 0 and hc < 2 and ("ab_ev", hp) in ck3:
                                v3.wait(tt, ck3[("ab_ev", hp)])
                            base = (hc - h0) * 2048
                            for dc in range(DC):
                                tt.matmul(psA3[hp][:],
                                          w13e[half][:, base + dc * 128:base + dc * 128 + 128],
                                          xn2f[:, dc, :],
                                          start=(dc == 0), stop=(dc == DC - 1))
                            for dc in range(DC):
                                mm = tt.matmul(psB3[hp][:],
                                               w13e[half][:, base + 1024 + dc * 128:base + 1024 + dc * 128 + 128],
                                               xn2f[:, dc, :],
                                               start=(dc == 0), stop=(dc == DC - 1))
                            t3.inc(mm)
                            ck3[("abB", hp)] = t3.n
                            if hc == HC - 1:
                                ck3[("sAdoneB", f)] = t3.n

                        @blk.vector
                        def _(v, f=f, hc=hc, hp=hp, slot=slot, grp=grp):
                            t3.wait(v, ck3[("abB", hp)])
                            if ("sgrpB", grp - 1) in ck3:
                                s3.wait(v, ck3[("sgrpB", grp - 1)])
                            elif grp == 0 and ("sgrpB_last",) in ck3:
                                s3.wait(v, ck3[("sgrpB_last",)])
                            elif grp == 0 and ("sgrp_last",) in ck3 and f == 0:
                                s3.wait(v, ck3[("sgrp_last",)])
                            v.tensor_scalar(sA[hp][:], psA3[hp][:],
                                            s_b13[:, 8, hc, 0:1], None, AluOpType.add)
                            v.drain()
                            mm = v.scalar_tensor_tensor(
                                sProd[:, slot, :], psB3[hp][:], s_b13[:, 8, hc, 1:2],
                                sA[hp][:], AluOpType.add, AluOpType.mult)
                            v3.inc(mm)
                            ck3[("ab_evB", hp)] = v3.n
                            ck3[("prB", hc)] = v3.n

                        if hc == [g1 - 1 for g0, g1 in SGRP if g0 <= hc < g1][0]:
                            gi = [i for i, (g0, g1) in enumerate(SGRP) if g0 <= hc < g1][0]
                            g0_, g1_ = SGRP[gi]

                            @blk.scalar
                            def _(s, f=f, gi=gi, g0_=g0_, g1_=g1_):
                                v3.wait(s, ck3[("prB", g1_ - 1)])
                                ng = g1_ - g0_
                                if f >= 1 and gi == 0:
                                    t3.wait(s, ck3[("ymmB", f - 1, DC - 1)])
                                elif f == 0 and gi == 0 and ("hgbfree", 0) in ck3:
                                    t3.wait(s, ck3[("hgbfree", 0)])
                                mm = s.activation(
                                    hgb[0][:, g0_:g1_, :].rearrange("p g d -> p (g d)"),
                                    sProd[:, 0:ng, :].rearrange("p g d -> p (g d)"),
                                    AF.Silu)
                                s3.inc(mm)
                                ck3[("sgrpB", gi)] = s3.n
                                ck3[("sgrpB_last",)] = s3.n
                                if g1_ == HC:
                                    ck3[("hrdyB", f)] = s3.n

                    for mc in range(DC):
                        mp = mc % 2

                        @blk.tensor
                        def _(tt, f=f, mc=mc, mp=mp):
                            s3.wait(tt, ck3[("hrdyB", f)])
                            if mc == 0:
                                if f == 0:
                                    cast_wait(tt, "b2")
                                yRwF.wait(tt, ck3[("rwfld", f)])
                            if ("yevB", mp) in ck3:
                                s3.wait(tt, ck3[("yevB", mp)])
                            elif f == 0 and mc < 2 and ("yevck", mp) in ck3:
                                s3.wait(tt, ck3[("yevck", mp)])
                            yW2.wait(tt, ck3[("w2ld", 8)])
                            for hc in range(HC):
                                tt.matmul(psY[mp][:], w2e[:, hc, mc, :], hgb[0][:, hc, :],
                                          start=(hc == 0), stop=False)
                            mm = tt.matmul(psY[mp][:], s_b2[:, mc * 128:(mc + 1) * 128],
                                           rwT_f[:], start=False, stop=True)
                            t3.inc(mm)
                            ck3[("ymmB", f, mc)] = t3.n

                        @blk.scalar
                        def _(s, f=f, mc=mc, mp=mp):
                            t3.wait(s, ck3[("ymmB", f, mc)])
                            s3.inc(s.activation(yf[mp][:], psY[mp][:], AF.Identity,
                                                bias=s_sb2[:, mc:mc + 1]))
                            ck3[("yevB", mp)] = s3.n

                        @blk.tensor
                        def _(tt, f=f, mc=mc, mp=mp):
                            s3.wait(tt, ck3[("yevB", mp)])
                            if ("ytevB",) in ck3:
                                v3.wait(tt, ck3[("ytevB",)])
                            elif ("ytevck",) in ck3 and f == 0 and mc == 0:
                                v3.wait(tt, ck3[("ytevck",)])
                            for sub in range(4):
                                mm = tt.transpose(psYT[:, sub * 128:(sub + 1) * 128],
                                                  yf[mp][:, sub * 128:(sub + 1) * 128],
                                                  s_identf[:])
                            t3.inc(mm)
                            ck3[("ytB", f, mc)] = t3.n

                        @blk.vector
                        def _(v, f=f, mc=mc):
                            t3.wait(v, ck3[("ytB", f, mc)])
                            yX1t.wait(v, 16 * (f + 1))
                            k_out = f * DC + mc
                            if k_out >= 1:
                                yOut.wait(v, 64 * k_out)
                            v3.inc(v.tensor_tensor(
                                ost[:],
                                psYT[:].rearrange("p (s d) -> p s d", s=4),
                                bass.AP(x1t.ap().tensor, x1t.ap().offset + mc * 128,
                                        [x1t.ap().ap[0], [DM, 4], [1, 128]]),
                                AluOpType.add))
                            ck3[("ytevB",)] = v3.n
                            if mc == DC - 1:
                                ck3[("ost", f)] = v3.n

                        @blk.sync
                        def _(sy, f=f, mc=mc):
                            v3.wait(sy, ck3[("ytevB",)])
                            for sub in range(4):
                                yOut.inc(sy.dma_start(
                                    out_d[f * 512 + sub * 128:f * 512 + (sub + 1) * 128,
                                          mc * 128:(mc + 1) * 128],
                                    ost[:, sub, :]), 16)

                @blk.sync
                def _(sy):
                    yOut.wait(sy)
                    for s_ in list(sems.values()) + list(cast_sems.values()):
                        if s_.n:
                            sy.wait_ge(s_.h, s_.n)

    return nc
    return nc



def _prep_inputs(T, ins):
    NT = T // 128
    f32 = np.float32
    w1, w3, w2 = ins["L0_ffn_w1"], ins["L0_ffn_w3"], ins["L0_ffn_w2"]
    sw1, sw3, sw2 = ins["L0_ffn_sw1"], ins["L0_ffn_sw3"], ins["L0_ffn_sw2"]
    b1, b3, b2 = ins["L0_ffn_b1"], ins["L0_ffn_b3"], ins["L0_ffn_b2"]
    sb1, sb3, sb2 = ins["L0_ffn_sb1"], ins["L0_ffn_sb3"], ins["L0_ffn_sb2"]

    w1a = np.concatenate([w1, sw1[None]], 0)
    w3a = np.concatenate([w3, sw3[None]], 0)
    w13 = np.stack([w1a, w3a], 1)
    w13 = w13.reshape(NE, 2, DC, 128, HC, 128)
    w13 = np.ascontiguousarray(w13.transpose(0, 4, 3, 1, 2, 5))

    w2a = np.concatenate([w2, sw2[None]], 0)
    w2a = w2a.reshape(NE, HC, 128, DC, 128)
    w2a = np.ascontiguousarray(w2a.transpose(0, 2, 1, 3, 4))

    b1a = np.concatenate([b1, sb1[None]], 0)
    b3a = np.concatenate([b3, sb3[None]], 0)
    b13 = np.stack([b1a, b3a], -1).reshape(NE, HC, 128, 2)
    b13 = np.ascontiguousarray(b13.transpose(2, 0, 1, 3))

    gate_w = ins["L0_ffn_gate_w"]
    gwt = np.ascontiguousarray(gate_w.T.reshape(DC, 128, 8).transpose(1, 0, 2))
    gbb = np.ascontiguousarray(
        np.broadcast_to(np.asarray(ins["L0_ffn_gate_b"], f32)[None, :], (128, 8)))

    pos = np.arange(T, dtype=np.float64)
    freqs = np.power(10000.0, -np.arange(64, dtype=np.float64) / 64.0)
    rad = pos[:, None] * freqs[None, :]
    cosT = np.cos(rad).astype(ml_dtypes.bfloat16).reshape(NT, 128, 64)
    sinT = np.sin(rad).astype(ml_dtypes.bfloat16).reshape(NT, 128, 64)

    j = np.arange(128)[:, None]
    qq = np.arange(512)[None, :]
    masks = np.stack([(qq >= j + 128 * k) for k in range(4)], 0).astype(ml_dtypes.bfloat16)
    masks = np.ascontiguousarray(masks.transpose(1, 0, 2))

    gzt = np.zeros((4, 128, 1040), ml_dtypes.bfloat16)
    for k in range(4):
        gzt[k, :, 1032] = ml_dtypes.bfloat16(float(NT + k))
    gzt[:, :, 1033] = np.arange(128, dtype=np.float32).astype(
        ml_dtypes.bfloat16)[None, :]
    gzt = gzt.reshape(512, 1040)

    ccols = np.zeros((128, 6), f32)
    ccols[:, 0] = EPS_LN
    ccols[:, 1] = EPS_RMS
    ccols[:, 2] = -1.0
    ccols[:, 3] = NEG_BIG
    ccols[:, 4] = 128.0
    ccols[:, 5] = 511.0

    return {
        "wq": np.ascontiguousarray(ins["L0_attn_wq"].reshape(DM, DM)),
        "wk": np.ascontiguousarray(ins["L0_attn_wk"].reshape(DM, DM)),
        "wv": np.ascontiguousarray(ins["L0_attn_wv"].reshape(DM, DM)),
        "wo": np.ascontiguousarray(ins["L0_attn_wo"]),
        "w13": w13, "w2": w2a, "b13": b13,
        "b2": np.ascontiguousarray(b2),
        "sb2c": np.ascontiguousarray(np.asarray(sb2, f32).reshape(DC, 128).T),
        "gwt": gwt, "gbb": gbb,
        "cosT": cosT, "sinT": sinT,
        "identb": np.eye(128, dtype=ml_dtypes.bfloat16),
        "identf": np.eye(128, dtype=f32),
        "ones1b": np.ones((1, 128), ml_dtypes.bfloat16),
        "onescb": np.ones((128, 1), ml_dtypes.bfloat16),
        "ones1f": np.ones((1, 128), f32),
        "masksb": masks,
        "ccols": ccols,
        "triu": np.triu(np.ones((128, 128))).astype(ml_dtypes.bfloat16),
        "pidx": np.arange(128, dtype=np.float32).astype(ml_dtypes.bfloat16).reshape(128, 1),
        "bigc": np.full((128, 8), 1.0e9, np.float32),
        "eofs": np.ascontiguousarray(
            np.broadcast_to((np.arange(8, dtype=np.float32) * (T + 512))[None, :],
                            (128, 8))),
        "gzt": gzt,
        "zrow": np.zeros((128, DM), f32),
    }


def kernel(**ins):
    x = np.asarray(ins["L0_attn_x"])
    B, T, _ = x.shape
    base = _prep_inputs(T, {k: np.asarray(v) for k, v in ins.items()})
    nc = build_nc(T)
    in_maps = []
    for b in range(B):
        m = dict(base)
        m["x"] = np.ascontiguousarray(x[b])
        in_maps.append(m)
    res = run_bass_kernel_spmd(nc, in_maps, core_ids=list(range(B)))
    return np.stack([np.asarray(r["out"]) for r in res.results], 0)


# version 16
# speedup vs baseline: 1.1524x; 1.0347x over previous
import numpy as np
import ml_dtypes
from contextlib import ExitStack

import concourse.bass as bass
import concourse.mybir as mybir
from concourse.bass_utils import run_bass_kernel_spmd
from concourse.alu_op_type import AluOpType

F32 = mybir.dt.float32
BF16 = mybir.dt.bfloat16
U32 = mybir.dt.uint32
I32 = mybir.dt.int32

def _act_set_id(name):
    from concourse.hw_specs import get_activation_tables
    import concourse.bass as _b
    _nc = _b.Bass("TRN2", target_bir_lowering=False)
    return list(get_activation_tables(_nc.m.arch)).index(name)

LNEXP_SET = None
SILU_SET = None
AF = mybir.ActivationFunctionType
AX = mybir.AxisListType.X

H = 8
DH = 128
DM = 1024
DC = 8
HF = 1408
HC = 11
NE = 9
EPS_LN = 1e-5
EPS_RMS = 1e-6
NEG_BIG = -1.0e30

W13_SL = [(h, min(h + 2, HC)) for h in range(0, HC, 2)]
W2_GR = [(0, 3), (3, 6), (6, 9)]


class Ctr:

    def __init__(self, ctx, nc, name):
        self.h = ctx.enter_context(nc.semaphore(name))
        self.n = 0

    def inc(self, instr, k=1):
        instr.then_inc(self.h, k)
        self.n += k
        return instr

    def wait(self, eng, n=None):
        eng.wait_ge(self.h, self.n if n is None else n)


def build_nc(T, debug=False):
    from concourse.hw_specs import get_activation_tables
    NT = T // 128
    NQ = T // 512
    nc = bass.Bass("TRN2", target_bir_lowering=False, debug=debug)
    tab_names = list(get_activation_tables(nc.m.arch))
    LNEXP_SET = tab_names.index("natural_log_exp_and_others")
    SILU_SET = tab_names.index("silu_and_others")

    x_in = nc.dram_tensor("x", [T, DM], F32, kind="ExternalInput").ap()
    wq_in = nc.dram_tensor("wq", [DM, DM], F32, kind="ExternalInput").ap()
    wk_in = nc.dram_tensor("wk", [DM, DM], F32, kind="ExternalInput").ap()
    wv_in = nc.dram_tensor("wv", [DM, DM], F32, kind="ExternalInput").ap()
    wo_in = nc.dram_tensor("wo", [H, DH, DM], F32, kind="ExternalInput").ap()
    w13_in = nc.dram_tensor("w13", [NE, HC, 128, 2, DC, 128], F32, kind="ExternalInput").ap()
    w2_in = nc.dram_tensor("w2", [NE, 128, HC, DC, 128], F32, kind="ExternalInput").ap()
    b13_in = nc.dram_tensor("b13", [128, NE, HC, 2], F32, kind="ExternalInput").ap()
    b2_in = nc.dram_tensor("b2", [8, DM], F32, kind="ExternalInput").ap()
    sb2_in = nc.dram_tensor("sb2c", [128, DC], F32, kind="ExternalInput").ap()
    gwt_in = nc.dram_tensor("gwt", [128, DC, 8], F32, kind="ExternalInput").ap()
    gbb_in = nc.dram_tensor("gbb", [128, 8], F32, kind="ExternalInput").ap()
    cos_in = nc.dram_tensor("cosT", [NT, 128, 64], BF16, kind="ExternalInput").ap()
    sin_in = nc.dram_tensor("sinT", [NT, 128, 64], BF16, kind="ExternalInput").ap()
    identb_in = nc.dram_tensor("identb", [128, 128], BF16, kind="ExternalInput").ap()
    identf_in = nc.dram_tensor("identf", [128, 128], F32, kind="ExternalInput").ap()
    ones1b_in = nc.dram_tensor("ones1b", [1, 128], BF16, kind="ExternalInput").ap()
    onescb_in = nc.dram_tensor("onescb", [128, 1], BF16, kind="ExternalInput").ap()
    ones1f_in = nc.dram_tensor("ones1f", [1, 128], F32, kind="ExternalInput").ap()
    masks_in = nc.dram_tensor("masksb", [128, 4, 512], BF16, kind="ExternalInput").ap()
    triu_in = nc.dram_tensor("triu", [128, 128], BF16, kind="ExternalInput").ap()
    pidx_in = nc.dram_tensor("pidx", [128, 1], BF16, kind="ExternalInput").ap()
    bigc_in = nc.dram_tensor("bigc", [128, 8], F32, kind="ExternalInput").ap()
    eofs_in = nc.dram_tensor("eofs", [128, 8], F32, kind="ExternalInput").ap()
    ccol_in = nc.dram_tensor("ccols", [128, 6], F32, kind="ExternalInput").ap()
    gzt_in = nc.dram_tensor("gzt", [512, 1040], BF16, kind="ExternalInput").ap()
    zrow_in = nc.dram_tensor("zrow", [128, DM], F32, kind="ExternalInput").ap()

    out_d = nc.dram_tensor("out", [T, DM], F32, kind="ExternalOutput").ap()

    qs = nc.dram_tensor("qscr", [NT, 128, H, 128], BF16).ap()
    attnscr = nc.dram_tensor("attnscr", [NQ, DC, 128, 512], F32).ap()
    x1s = nc.dram_tensor("x1scr", [T + 512, DM], F32).ap()
    xn2s = nc.dram_tensor("xn2scr", [NQ, 128, DC, 512], BF16).ap()
    rws = nc.dram_tensor("rwscr", [8, T], BF16).ap()
    GR = T + 512
    gathd = nc.dram_tensor("gathd", [8 * (T + 512), 1040], BF16).ap()

    with ExitStack() as top:
        def sbuf(name, shape, dt, stack):
            return stack.enter_context(nc.sbuf_tensor(name, shape, dt))

        def psum(name, shape, dt, stack):
            return stack.enter_context(nc.psum_tensor(name, shape, dt))

        s_identb = sbuf("s_identb", [128, 128], BF16, top)
        s_identf = sbuf("s_identf", [128, 128], F32, top)
        s_ones1b = sbuf("s_ones1b", [1, 128], BF16, top)
        s_onescb = sbuf("s_onescb", [128, 1], BF16, top)
        s_ccol = sbuf("s_ccol", [128, 6], F32, top)
        s_b13 = sbuf("s_b13", [128, NE, HC, 2], F32, top)
        s_sb2 = sbuf("s_sb2", [128, DC], F32, top)
        s_gwt = sbuf("s_gwt", [128, DC, 8], BF16, top)
        s_gbb = sbuf("s_gbb", [128, 8], F32, top)
        s_pidx = sbuf("s_pidx", [128, 1], BF16, top)
        s_triu = sbuf("s_triu", [128, 128], BF16, top)
        s_bigc = sbuf("s_bigc", [128, 8], F32, top)
        s_eofs = sbuf("s_eofs", [128, 8], F32, top)
        s_toti = sbuf("s_toti", [1, 8], I32, top)

        c_epsln = s_ccol[:, 0:1]
        c_epsrms = s_ccol[:, 1:2]
        c_neg1 = s_ccol[:, 2:3]
        c_negbig = s_ccol[:, 3:4]
        c_128 = s_ccol[:, 4:5]

        cast_sems = {}
        for cname in ["wqkv", "wo", "gate", "b2", "gz"]:
            cast_sems[cname] = Ctr(top, nc, "cast_" + cname)

        def cast_wait(eng, cname):
            s = cast_sems[cname]
            eng.wait_ge(s.h, s.n)

        sems = {}
        for name in ["c0",
                     "v1", "s1", "t1",
                     "v2", "s2", "t2",
                     "v25", "s25", "t25",
                     "v3", "s3", "t3",
                     "yLd0", "yLd1", "yQs",
                     "yQq", "yAt0", "yAt1",
                     "yL25a", "yL25b", "yX1", "yXn2",
                     "yW13a", "yW13b", "yW13c", "yW2a", "yW2b", "yW2c",
                     "yXn2f", "yX1t", "yOut", "yRw", "yRwF", "yGs", "gsc3", "gsc3b"]:
            sems[name] = Ctr(top, nc, "sem_" + name)
        c0 = sems["c0"]

        with ExitStack() as p012:
            k_sb = sbuf("k_sb", [128, H, NT, 128], BF16, p012)
            s_ones1f = sbuf("s_ones1f", [1, 128], F32, p012)
            v_sb = sbuf("v_sb", [128, H, NT, 128], BF16, p012)

            v1, s1, t1 = sems["v1"], sems["s1"], sems["t1"]
            yLd = [sems["yLd0"], sems["yLd1"]]
            yQs = sems["yQs"]
            with ExitStack() as p1:
                wq_sb = sbuf("wq_sb", [128, DC, DM], BF16, p1)
                wk_sb = sbuf("wk_sb", [128, DC, DM], BF16, p1)
                wv_sb = sbuf("wv_sb", [128, DC, DM], BF16, p1)
                xb = [sbuf(f"xb{i}", [128, DM], F32, p1) for i in range(2)]
                cb = [sbuf(f"cb{i}", [128, 64], BF16, p1) for i in range(2)]
                sb_ = [sbuf(f"sb{i}", [128, 64], BF16, p1) for i in range(2)]
                mv = sbuf("mv", [128, 14], F32, p1)
                xn_tm = sbuf("xn_tm", [128, DM], BF16, p1)
                xn_fm = sbuf("xn_fm", [128, DM], BF16, p1)
                sqb = sbuf("sqb", [128, DM], F32, p1)
                qkb_q = sbuf("qkb_q", [128, DM], BF16, p1)
                qkb_k = sbuf("qkb_k", [128, DM], BF16, p1)
                rt_q = sbuf("rt_q", [128, DM], BF16, p1)
                rt_k = sbuf("rt_k", [128, DM], BF16, p1)
                q_rope = sbuf("q_rope", [128, DM], BF16, p1)
                k_rope = sbuf("k_rope", [128, DM], BF16, p1)
                qfm_st = rt_q
                msq = sbuf("msq", [128, 8], F32, p1)
                rsq = sbuf("rsq", [128, 8], F32, p1)
                msq2 = sbuf("msq2", [128, 8], F32, p1)
                rsq2 = sbuf("rsq2", [128, 8], F32, p1)
                psQ = psum("psQ", [128, DM], F32, p1)
                psK = psum("psK", [128, DM], F32, p1)
                psV = psum("psV", [128, DM], F32, p1)
                psXP = psum("psXP", [128, DM], BF16, p1)
                psT2 = psum("psT2", [128, DM], BF16, p1)

                ck = {}

                with nc.Block("p1") as blk:
                    @blk.scalar
                    def _(s):
                        s.add_instruction(mybir.InstLoadActFuncSet(
                            name=nc.get_next_instruction_name(),
                            act_func_set_id=LNEXP_SET, ins=[], outs=[]))
                        c0.wait(s, 192)

                    @blk.sync
                    def _(sy):
                        for dst, srcap in [
                            (s_identb[:], identb_in[:]), (s_identf[:], identf_in[:]),
                            (s_ones1b[:], ones1b_in[:]), (s_onescb[:], onescb_in[:]),
                            (s_ccol[:], ccol_in[:]), (s_b13[:], b13_in[:]),
                            (s_sb2[:], sb2_in[:]), (s_gbb[:], gbb_in[:]),
                            (s_pidx[:], pidx_in[:]), (s_triu[:], triu_in[:]),
                            (s_bigc[:], bigc_in[:]), (s_eofs[:], eofs_in[:]),
                        ]:
                            c0.inc(sy.dma_start(dst, srcap), 16)

                    @blk.gpsimd
                    def _(g):
                        cq = cast_sems["wqkv"]
                        cq.inc(g.dma_start(wq_sb[:], wq_in.rearrange("(c p) n -> p c n", p=128)), 16)
                        cq.inc(g.dma_start(wk_sb[:], wk_in.rearrange("(c p) n -> p c n", p=128)), 16)
                        cq.inc(g.dma_start(wv_sb[:], wv_in.rearrange("(c p) n -> p c n", p=128)), 16)
                        cast_sems["gate"].inc(g.dma_start(s_gwt[:], gwt_in[:]), 16)

                    @blk.vector
                    def _(v):
                        c0.wait(v, 192)

                    @blk.tensor
                    def _(tt):
                        c0.wait(tt, 192)

                    def emit_qk_T(t):
                        for which, ro in [("q", q_rope), ("k", k_rope)]:
                            @blk.tensor
                            def _(tt, t=t, which=which, ro=ro):
                                v1.wait(tt, ck[("rope", t, which)])
                                if not (t == 0 and which == "q"):
                                    v1.wait(tt, ck[("psT2",)])
                                for h in range(H):
                                    mm = tt.transpose(psT2[:, h * 128:(h + 1) * 128],
                                                      ro[:, h * 128:(h + 1) * 128], s_identb[:])
                                t1.inc(mm)
                                ck[("qT" if which == "q" else "kT", t)] = t1.n

                            @blk.vector
                            def _(v, t=t, which=which):
                                t1.wait(v)
                                if which == "q":
                                    if t >= 1:
                                        yQs.wait(v, 16 * t)
                                    v1.inc(v.tensor_copy(qfm_st[:], psT2[:]))
                                else:
                                    v1.inc(v.tensor_copy(
                                        k_sb[:, :, t, :],
                                        psT2[:].rearrange("p (h d) -> p h d", h=H)))
                                ck[("psT2",)] = v1.n
                                if which == "q":
                                    ck[("qfm", t)] = v1.n

                        @blk.sync
                        def _(sy, t=t):
                            v1.wait(sy, ck[("qfm", t)])
                            yQs.inc(sy.dma_start(qs[t].rearrange("p h d -> p (h d)"), qfm_st[:]), 16)

                    def emit_loads(t):
                        p = t % 2

                        @blk.sync
                        def _(sy, t=t, p=p):
                            if t >= 2:
                                v1.wait(sy, ck[("ropek", t - 2)])
                                s1.wait(sy, ck[("apply", t - 2)])
                            yLd[p].inc(sy.dma_start(xb[p][:], x_in[t * 128:(t + 1) * 128, :]), 16)
                            yLd[p].inc(sy.dma_start(cb[p][:], cos_in[t]), 16)
                            yLd[p].inc(sy.dma_start(sb_[p][:], sin_in[t]), 16)
                            ck[("xld", t)] = yLd[p].n

                    def emit_ln(t):
                        p = t % 2

                        @blk.vector
                        def _(v, t=t, p=p):
                            yLd[p].wait(v, ck[("xld", t)])
                            v.bn_stats(mv[:, 0:6], xb[p][:, 0:512])
                            v.bn_stats(mv[:, 6:12], xb[p][:, 512:1024])
                            v.drain()
                            v.bn_aggr(mv[:, 12:14], mv[:, 0:12])
                            v.drain()
                            v.tensor_scalar(mv[:, 13:14], mv[:, 13:14], c_epsln,
                                            None, AluOpType.add)
                            v.drain()
                            v1.inc(v.tensor_scalar(mv[:, 11:12], mv[:, 12:13], c_neg1,
                                                   None, AluOpType.mult))
                            ck[("stats", t)] = v1.n

                        @blk.scalar
                        def _(s, t=t, p=p):
                            v1.wait(s, ck[("stats", t)])
                            s.activation(mv[:, 13:14], mv[:, 13:14], AF.Ln)
                            s.drain()
                            s.activation(mv[:, 13:14], mv[:, 13:14], AF.Exp, scale=-0.5)
                            s.drain()
                            s.activation(mv[:, 12:13], mv[:, 11:12], AF.Identity,
                                         scale=mv[:, 13:14])
                            s.drain()
                            if t >= 1:
                                t1.wait(s, ck[("xnT", t - 1)])
                            s1.inc(s.activation(xn_tm[:], xb[p][:], AF.Identity,
                                                bias=mv[:, 12:13], scale=mv[:, 13:14]))
                            ck[("apply", t)] = s1.n

                    emit_loads(0)
                    emit_ln(0)

                    def emit_xnT(t):
                        @blk.tensor
                        def _(tt, t=t):
                            s1.wait(tt, ck[("apply", t)])
                            if t >= 1:
                                v1.wait(tt, ck[("xnfm", t - 1)])
                            for i in range(8):
                                mmt = tt.transpose(psXP[:, i * 128:(i + 1) * 128],
                                                   xn_tm[:, i * 128:(i + 1) * 128], s_identb[:])
                            t1.inc(mmt)
                            ck[("xnT", t)] = t1.n

                        @blk.vector
                        def _(v, t=t):
                            t1.wait(v, ck[("xnT", t)])
                            if t >= 1:
                                t1.wait(v, ck[("qkv", t - 1)])
                            v1.inc(v.tensor_copy(xn_fm[:], psXP[:]))
                            ck[("xnfm", t)] = v1.n

                    emit_xnT(0)

                    for t in range(NT):
                        p = t % 2

                        if t + 1 < NT:
                            emit_loads(t + 1)

                        @blk.tensor
                        def _(tt, t=t):
                            if t == 0:
                                cast_wait(tt, "wqkv")
                            v1.wait(tt, ck[("xnfm", t)])
                            if t >= 1:
                                s1.wait(tt, ck[("kscale", t - 1)])
                                v1.wait(tt, ck[("vev", t - 1)])
                            for ps, w in [(psQ, wq_sb), (psK, wk_sb), (psV, wv_sb)]:
                                for hlf in (0, 512):
                                    for dc in range(DC):
                                        mm = tt.matmul(ps[:, hlf:hlf + 512],
                                                       xn_fm[:, dc * 128:(dc + 1) * 128],
                                                       w[:, dc, hlf:hlf + 512],
                                                       start=(dc == 0), stop=(dc == DC - 1))
                            t1.inc(mm)
                            ck[("qkv", t)] = t1.n

                        if t >= 1:
                            emit_qk_T(t - 1)

                        if t + 1 < NT:
                            emit_ln(t + 1)
                            emit_xnT(t + 1)

                        @blk.vector
                        def _(v, t=t):
                            t1.wait(v, ck[("qkv", t)])
                            v1.inc(v.tensor_copy(
                                v_sb[:, :, t, :],
                                psV[:].rearrange("p (h d) -> p h d", h=H)))
                            ck[("vev", t)] = v1.n

                        @blk.scalar
                        def _(s, t=t):
                            t1.wait(s, ck[("qkv", t)])
                            if t >= 1:
                                v1.wait(s, ck[("kred", t - 1)])
                            s1.inc(s.activation(sqb[:], psQ[:], AF.Square))
                            ck[("qsq", t)] = s1.n

                        @blk.vector
                        def _(v, t=t):
                            s1.wait(v, ck[("qsq", t)])
                            v1.inc(v.reduce_sum(
                                msq[:], sqb[:].rearrange("p (h d) -> p h d", h=H), axis=AX))
                            ck[("qred", t)] = v1.n

                        @blk.scalar
                        def _(s, t=t):
                            v1.wait(s, ck[("qred", t)])
                            s1.inc(s.activation(sqb[:], psK[:], AF.Square))
                            ck[("ksq", t)] = s1.n

                        @blk.scalar
                        def _(s, t=t):
                            s.activation(rsq[:], msq[:], AF.Ln, bias=c_epsrms, scale=1.0 / DH)
                            s.drain()
                            s.activation(rsq[:], rsq[:], AF.Exp, scale=-0.5)
                            s.drain()
                            for h in range(H):
                                mm = s.activation(qkb_q[:, h * 128:(h + 1) * 128],
                                                  psQ[:, h * 128:(h + 1) * 128],
                                                  AF.Identity, scale=rsq[:, h:h + 1])
                            s1.inc(mm)
                            ck[("qscale", t)] = s1.n

                        @blk.vector
                        def _(v, t=t):
                            s1.wait(v, ck[("ksq", t)])
                            v1.inc(v.reduce_sum(
                                msq2[:], sqb[:].rearrange("p (h d) -> p h d", h=H), axis=AX))
                            ck[("kred", t)] = v1.n

                        def emit_rope(t, which, ro, qkb, scale_ck):
                            p = t % 2

                            @blk.vector
                            def _(v, t=t, which=which, ro=ro, qkb=qkb):
                                s1.wait(v, ck[(scale_ck, t)])
                                if t >= 1:
                                    t1.wait(v, ck[("qT" if which == "q" else "kT", t - 1)])
                                if which == "q" and t >= 1:
                                    yQs.wait(v, 16 * t)
                                rt = rt_q if which == "q" else rt_k
                                r3 = lambda ap: ap.rearrange("p (h d) -> p h d", h=H)
                                ev, od = r3(qkb[:])[:, :, 0:64], r3(qkb[:])[:, :, 64:128]
                                oe, oo = r3(ro[:])[:, :, 0:64], r3(ro[:])[:, :, 64:128]
                                te, to = r3(rt[:])[:, :, 0:64], r3(rt[:])[:, :, 64:128]
                                cosb = bass.AP(cb[p].ap().tensor, cb[p].ap().offset,
                                               [cb[p].ap().ap[0], [0, H], [1, 64]])
                                sinb = bass.AP(sb_[p].ap().tensor, sb_[p].ap().offset,
                                               [sb_[p].ap().ap[0], [0, H], [1, 64]])
                                v.tensor_tensor(te, ev, cosb, AluOpType.mult)
                                v.tensor_tensor(to, od, sinb, AluOpType.mult)
                                v.drain()
                                v.tensor_tensor(oe, te, to, AluOpType.subtract)
                                v.drain()
                                v.tensor_tensor(te, ev, sinb, AluOpType.mult)
                                v.tensor_tensor(to, od, cosb, AluOpType.mult)
                                v.drain()
                                v1.inc(v.tensor_tensor(oo, te, to, AluOpType.add))
                                ck[("rope", t, which)] = v1.n
                                if which == "k":
                                    ck[("ropek", t)] = v1.n

                        emit_rope(t, "q", q_rope, qkb_q, "qscale")

                        @blk.scalar
                        def _(s, t=t):
                            v1.wait(s, ck[("kred", t)])
                            s.activation(rsq2[:], msq2[:], AF.Ln, bias=c_epsrms, scale=1.0 / DH)
                            s.drain()
                            s.activation(rsq2[:], rsq2[:], AF.Exp, scale=-0.5)
                            s.drain()
                            for h in range(H):
                                mm = s.activation(qkb_k[:, h * 128:(h + 1) * 128],
                                                  psK[:, h * 128:(h + 1) * 128],
                                                  AF.Identity, scale=rsq2[:, h:h + 1])
                            s1.inc(mm)
                            ck[("kscale", t)] = s1.n

                        emit_rope(t, "k", k_rope, qkb_k, "kscale")

                    emit_qk_T(NT - 1)


            v2, s2, t2 = sems["v2"], sems["s2"], sems["t2"]
            yQq = sems["yQq"]
            yAt = [sems["yAt0"], sems["yAt1"]]
            with ExitStack() as p2:
                s_masks = sbuf("s_masks", [128, 4, 512], BF16, p2)
                wo_sb = sbuf("wo_sb", [128, H, DC, 128], BF16, p2)
                qq = sbuf("qq", [128, 4, H, 128], BF16, p2)
                pt_sl = [sbuf(f"pt{i}", [128, 1024], BF16, p2) for i in range(2)]
                ptm = sbuf("ptm", [128, 4, 512], BF16, p2)
                sden = [sbuf(f"sden{i}", [1, 512], F32, p2) for i in range(2)]
                srden = [sbuf(f"srden{i}", [1, 512], F32, p2) for i in range(2)]
                sdb = [sbuf(f"sdb{i}", [128, 512], F32, p2) for i in range(2)]
                o_all = sbuf("o_all", [128, H, 512], BF16, p2)
                sattn = [sbuf(f"sattn{i}", [128, 512], F32, p2) for i in range(2)]
                psS = [psum(f"psS{i}", [128, 1024], F32, p2) for i in range(2)]
                psO = [psum(f"psO{i}", [128, 512], F32, p2) for i in range(2)]
                psDD = [psum(f"psDD{i}", [128, 512], F32, p2) for i in range(2)]
                psA = psS[0][:, 0:512]

                ck2 = {"exp_last": {}, "pv_last": {}, "onorm_p": {}, "wodone": 0}

                with nc.Block("p2") as blk:
                    @blk.gpsimd
                    def _(g):
                        cast_sems["wo"].inc(g.dma_start(
                            wo_sb[:],
                            bass.AP(wo_in.tensor, 0,
                                    [[1024, 128], [131072, H], [128, DC], [1, 128]])), 16)
                        gz = cast_sems["gz"]
                        nblk = 8 * (T + 512) // 512
                        for b_ in range(nblk):
                            gz.inc(g.dma_start(
                                gathd[b_ * 512:(b_ + 1) * 512, :], gzt_in[:]), 16)
                        for sub in range(4):
                            gz.inc(g.dma_start(
                                x1s[T + sub * 128:T + (sub + 1) * 128, :],
                                zrow_in[:]), 16)

                    @blk.scalar
                    def _(s):
                        s.add_instruction(mybir.InstLoadActFuncSet(
                            name=nc.get_next_instruction_name(),
                            act_func_set_id=LNEXP_SET, ins=[], outs=[]))

                    @blk.sync
                    def _(sy):
                        yQs.wait(sy)
                        yLd[0].wait(sy)
                        yLd[1].wait(sy)
                        c0.inc(sy.dma_start(s_masks[:], masks_in[:]), 16)
                        c0.inc(sy.dma_start(s_ones1f[:], ones1f_in[:]), 16)

                    for qg in range(NQ):
                        JC = 4 * (qg + 1)
                        nbatch = (JC + 1) // 2

                        @blk.sync
                        def _(sy, qg=qg):
                            if qg >= 1:
                                t2.wait(sy, ck2[("qq_free", qg - 1)])
                            for i in range(4):
                                yQq.inc(sy.dma_start(qq[:, i, :, :], qs[4 * qg + i]), 16)

                        for h in range(H):
                            hp = h % 2
                            exp_ck, pv_ck = {}, {}
                            batches = [list(range(2 * b, min(2 * b + 2, JC)))
                                       for b in range(nbatch)]

                            def emit_st(b, qg=qg, h=h):
                                jcs = batches[b]

                                @blk.tensor
                                def _(tt):
                                    if b == 0 and h == 0:
                                        yQq.wait(tt)
                                    if (b - 2) in exp_ck:
                                        s2.wait(tt, exp_ck[b - 2])
                                    elif b < 2 and (b % 2) in ck2["exp_last"]:
                                        s2.wait(tt, ck2["exp_last"][b % 2])
                                    for i, jc in enumerate(jcs):
                                        mm = tt.matmul(psS[b % 2][:, i * 512:(i + 1) * 512],
                                                       k_sb[:, h, jc, :], qq[:, :, h, :],
                                                       start=True, stop=True)
                                    t2.inc(mm)
                                    ck2[("st", b)] = t2.n

                            def emit_exp(b, qg=qg, h=h):
                                jcs = batches[b]

                                @blk.scalar
                                def _(s):
                                    t2.wait(s, ck2[("st", b)])
                                    if (b - 2) in pv_ck:
                                        t2.wait(s, pv_ck[b - 2])
                                    elif b < 2 and (b % 2) in ck2["pv_last"]:
                                        t2.wait(s, ck2["pv_last"][b % 2])
                                    s2.inc(s.activation(pt_sl[b % 2][:, 0:len(jcs) * 512],
                                                        psS[b % 2][:, 0:len(jcs) * 512],
                                                        AF.Exp, scale=1.0 / DH))
                                    exp_ck[b] = s2.n
                                    ck2["exp_last"][b % 2] = s2.n

                                diag = [i for i, jc in enumerate(jcs) if jc >= 4 * qg]
                                if diag:
                                    @blk.vector
                                    def _(v):
                                        if qg == 0 and h == 0:
                                            c0.wait(v)
                                        s2.wait(v, exp_ck[b])
                                        for i in diag:
                                            kk = jcs[i] - 4 * qg
                                            mm = v.tensor_tensor(
                                                ptm[:, kk, :],
                                                pt_sl[b % 2][:, i * 512:(i + 1) * 512],
                                                s_masks[:, kk, :], AluOpType.mult)
                                        v2.inc(mm)
                                        ck2[("mask", b)] = v2.n

                            def emit_pv(b, qg=qg, h=h, hp=hp):
                                jcs = batches[b]

                                @blk.tensor
                                def _(tt):
                                    s2.wait(tt, exp_ck[b])
                                    if any(jc >= 4 * qg for jc in jcs):
                                        v2.wait(tt, ck2[("mask", b)])
                                    if b == 0 and (hp in ck2["onorm_p"]):
                                        v2.wait(tt, ck2["onorm_p"][hp])
                                    for i, jc in enumerate(jcs):
                                        src = (ptm[:, jc - 4 * qg, :] if jc >= 4 * qg
                                               else pt_sl[b % 2][:, i * 512:(i + 1) * 512])
                                        tt.matmul(psO[hp][:], v_sb[:, h, jc, :], src,
                                                  start=(jc == 0), stop=(jc == JC - 1))
                                        mm = tt.matmul(psDD[hp][0:1, :], s_onescb[:], src,
                                                       start=(jc == 0), stop=(jc == JC - 1))
                                    t2.inc(mm)
                                    pv_ck[b] = t2.n
                                    ck2["pv_last"][b % 2] = t2.n

                            emit_st(0)
                            emit_exp(0)
                            for b in range(1, nbatch):
                                emit_st(b)
                                emit_pv(b - 1)
                                emit_exp(b)
                            emit_pv(nbatch - 1)

                            @blk.vector
                            def _(v, qg=qg, h=h, hp=hp, pv_ck=pv_ck, nbatch=nbatch):
                                t2.wait(v, pv_ck[nbatch - 1])
                                if qg >= 1 and h == 0:
                                    t2.wait(v, ck2["wodone"])
                                v.tensor_copy(sden[hp][:], psDD[hp][0:1, :])
                                v.drain()
                                v2.inc(v.reciprocal(srden[hp][:], sden[hp][:]))

                            @blk.tensor
                            def _(tt, qg=qg, h=h, hp=hp):
                                if qg == 0 and h == 0:
                                    c0.wait(tt)
                                v2.wait(tt)
                                t2.inc(tt.matmul(psDD[hp][:], s_ones1f[:], srden[hp][:],
                                                 start=True, stop=True))

                            @blk.vector
                            def _(v, qg=qg, h=h, hp=hp):
                                t2.wait(v)
                                v.tensor_copy(sdb[hp][:], psDD[hp][:])
                                v.drain()
                                v2.inc(v.tensor_tensor(o_all[:, h, :], psO[hp][:], sdb[hp][:],
                                                       AluOpType.mult))
                                ck2["onorm_p"][hp] = v2.n
                                if h == H - 1:
                                    ck2[("heads_done", qg)] = v2.n

                        ck2[("qq_free", qg)] = t2.n

                        for mc in range(DC):
                            @blk.tensor
                            def _(tt, qg=qg, mc=mc):
                                if qg == 0 and mc == 0:
                                    cast_wait(tt, "wo")
                                v2.wait(tt, ck2[("heads_done", qg)])
                                if ("attn_ev", qg, mc - 1) in ck2:
                                    v2.wait(tt, ck2[("attn_ev", qg, mc - 1)])
                                elif qg >= 1 and mc == 0:
                                    v2.wait(tt, ck2[("attn_ev", qg - 1, DC - 1)])
                                for h in range(H):
                                    mm = tt.matmul(psA, wo_sb[:, h, mc, :], o_all[:, h, :],
                                                   start=(h == 0), stop=(h == H - 1))
                                t2.inc(mm)
                                if mc == DC - 1:
                                    ck2["wodone"] = t2.n

                            @blk.vector
                            def _(v, qg=qg, mc=mc):
                                t2.wait(v)
                                k_st = qg * DC + mc
                                if k_st >= 2:
                                    yAt[mc % 2].wait(v, 16 * (k_st // 2))
                                v2.inc(v.tensor_copy(sattn[mc % 2][:], psA))
                                ck2[("attn_ev", qg, mc)] = v2.n

                            @blk.sync
                            def _(sy, qg=qg, mc=mc):
                                v2.wait(sy, ck2[("attn_ev", qg, mc)])
                                yAt[mc % 2].inc(sy.dma_start(attnscr[qg, mc], sattn[mc % 2][:]), 16)

            v25, s25, t25 = sems["v25"], sems["s25"], sems["t25"]
            yL25 = [sems["yL25a"], sems["yL25b"]]
            yX1 = sems["yX1"]
            yXn2 = sems["yXn2"]
            yRw = sems["yRw"]
            yGs = sems["yGs"]
            with ExitStack() as p25:
                xb2 = [sbuf(f"xb2_{i}", [128, DM], F32, p25) for i in range(2)]
                afm = [sbuf(f"afm{i}", [128, DC, 128], F32, p25) for i in range(2)]
                x1tm = sbuf("x1tm", [128, DM], F32, p25)
                xn2tm = sbuf("xn2tm", [128, DM], BF16, p25)
                xn2st = [sbuf(f"xn2st{i}", [128, DC, 512], BF16, p25) for i in range(2)]
                mv2 = sbuf("mv2", [128, 14], F32, p25)
                sxp = sbuf("sxp", [128, 8], F32, p25)
                sume = sbuf("sume", [128, 1], F32, p25)
                rsum = sbuf("rsum", [128, 1], F32, p25)
                rsc = sbuf("rsc", [128, 8], F32, p25)
                bsd = sbuf("bsd", [128, 8], F32, p25)
                m1c = sbuf("m1c", [128, 1], F32, p25)
                mask1 = sbuf("mask1", [128, 8], F32, p25)
                b2d = sbuf("b2d", [128, 8], F32, p25)
                m2c = sbuf("m2c", [128, 1], F32, p25)
                mask2 = sbuf("mask2", [128, 8], F32, p25)
                mask1i = sbuf("mask1i", [128, 8], I32, p25)
                rwbf = sbuf("rwbf", [128, 8], BF16, p25)
                rw_st = sbuf("rw_st", [8, 128], BF16, p25)
                selb = sbuf("selb", [128, 8], BF16, p25)
                base_b = sbuf("base_b", [128, 8], F32, p25)
                a_s = sbuf("a_s", [128, 8], F32, p25)
                idxf = sbuf("idxf", [128, 8], F32, p25)
                idxu = sbuf("idxu", [128, 8], U32, p25)
                totrow = sbuf("totrow", [1, 8], F32, p25)
                ttile = sbuf("ttile", [1, 8], F32, p25)
                stage = sbuf("stage", [128, 1040], BF16, p25)
                psT25 = psum("psT25", [128, DM], F32, p25)
                psX2 = psum("psX2", [128, DM], BF16, p25)
                psG = psum("psG", [128, 512], F32, p25)
                psRw = psum("psRw", [8, 512], BF16, p25)
                psTot = psum("psTot", [1, 512], F32, p25)

                ck25 = {}

                with nc.Block("p25") as blk:
                    @blk.vector
                    def _(v):
                        v.memset(stage[:], 0.0)
                        v25.inc(v.memset(totrow[:], 0.0))
                        ck25[("tot_snap", -1)] = v25.n

                    @blk.sync
                    def _(sy):
                        yAt[0].wait(sy)
                        yAt[1].wait(sy)
                        yQq.wait(sy)

                    for t in range(NT):
                        p = t % 2
                        qg, off = t // 4, (t % 4) * 128

                        @blk.sync
                        def _(sy, t=t, p=p, qg=qg, off=off):
                            if t >= 2:
                                v25.wait(sy, ck25[("vfin", t - 2)])
                            yL25[p].inc(sy.dma_start(xb2[p][:], x_in[t * 128:(t + 1) * 128, :]), 16)
                            yL25[p].inc(sy.dma_start(
                                afm[p][:],
                                bass.AP(attnscr.tensor, qg * DC * 128 * 512 + off,
                                        [[512, 128], [128 * 512, DC], [1, 128]])), 16)
                            ck25[("ld", t)] = yL25[p].n

                        @blk.tensor
                        def _(tt, t=t, p=p):
                            yL25[p].wait(tt, ck25[("ld", t)])
                            if t >= 1:
                                v25.wait(tt, ck25[("x1add", t - 1)])
                            for mc in range(DC):
                                mm = tt.transpose(psT25[:, mc * 128:(mc + 1) * 128],
                                                  afm[p][:, mc, :], s_identf[:])
                            t25.inc(mm)

                        @blk.vector
                        def _(v, t=t, p=p):
                            t25.wait(v)
                            if t >= 1:
                                yX1.wait(v, 16 * t)
                            v25.inc(v.tensor_tensor(x1tm[:], psT25[:], xb2[p][:], AluOpType.add))
                            ck25[("x1add", t)] = v25.n

                        @blk.sync
                        def _(sy, t=t):
                            v25.wait(sy, ck25[("x1add", t)])
                            yX1.inc(sy.dma_start(x1s[t * 128:(t + 1) * 128, :], x1tm[:]), 16)

                        @blk.vector
                        def _(v, t=t):
                            v.drain()
                            v.bn_stats(mv2[:, 0:6], x1tm[:, 0:512])
                            v.bn_stats(mv2[:, 6:12], x1tm[:, 512:1024])
                            v.drain()
                            v.bn_aggr(mv2[:, 12:14], mv2[:, 0:12])
                            v.drain()
                            v.tensor_scalar(mv2[:, 13:14], mv2[:, 13:14],
                                            c_epsln, None, AluOpType.add)
                            v.drain()
                            v25.inc(v.tensor_scalar(mv2[:, 11:12], mv2[:, 12:13], c_neg1,
                                                    None, AluOpType.mult))

                        @blk.scalar
                        def _(s, t=t):
                            v25.wait(s)
                            s.activation(mv2[:, 13:14], mv2[:, 13:14], AF.Ln)
                            s.drain()
                            s.activation(mv2[:, 13:14], mv2[:, 13:14], AF.Exp, scale=-0.5)
                            s.drain()
                            s.activation(mv2[:, 12:13], mv2[:, 11:12], AF.Identity,
                                         scale=mv2[:, 13:14])
                            s.drain()
                            if t >= 1:
                                t25.wait(s, ck25[("x2t", t - 1)])
                            s25.inc(s.activation(xn2tm[:], x1tm[:], AF.Identity,
                                                 bias=mv2[:, 12:13], scale=mv2[:, 13:14]))
                            ck25[("apply2", t)] = s25.n

                        @blk.tensor
                        def _(tt, t=t):
                            s25.wait(tt, ck25[("apply2", t)])
                            if t >= 1:
                                v25.wait(tt, ck25[("x2ev", t - 1)])
                            for i in range(DC):
                                mm = tt.transpose(psX2[:, i * 128:(i + 1) * 128],
                                                  xn2tm[:, i * 128:(i + 1) * 128], s_identb[:])
                            t25.inc(mm)
                            ck25[("x2t", t)] = t25.n

                        @blk.vector
                        def _(v, t=t, qg=qg, off=off):
                            t25.wait(v, ck25[("x2t", t)])
                            if t % 4 == 0 and qg >= 2:
                                yXn2.wait(v, 16 * qg)
                                t25.wait(v, ck25[("gmm", 4 * (qg - 2) + 3)])
                            v25.inc(v.tensor_copy(
                                xn2st[qg % 2][:, :, off:off + 128],
                                psX2[:].rearrange("p (c d) -> p c d", c=DC)))
                            ck25[("x2ev", t)] = v25.n

                        @blk.sync
                        def _(sy, t=t, qg=qg):
                            if t % 4 == 3:
                                v25.wait(sy, ck25[("x2ev", t)])
                                yXn2.inc(sy.dma_start(xn2s[qg], xn2st[qg % 2][:]), 16)

                        @blk.tensor
                        def _(tt, t=t, qg=qg, off=off):
                            if t == 0:
                                cast_wait(tt, "gate")
                            v25.wait(tt, ck25[("x2ev", t)])
                            if t >= 1:
                                s25.wait(tt, ck25[("gexp", t - 1)])
                            for dc in range(DC):
                                mm = tt.matmul(psG[:, 0:8],
                                               xn2st[qg % 2][:, dc, off:off + 128],
                                               s_gwt[:, dc, :],
                                               start=(dc == 0), stop=(dc == DC - 1))
                            t25.inc(mm)
                            ck25[("gmm", t)] = t25.n

                        @blk.vector
                        def _(v, t=t):
                            t25.wait(v, ck25[("gmm", t)])
                            v25.inc(v.reduce_max(m1c[:], psG[:, 0:8], axis=AX, negate=True))

                        @blk.scalar
                        def _(s, t=t):
                            v25.wait(s)
                            s25.inc(s.activation(sxp[:], psG[:, 0:8], AF.Exp,
                                                 bias=m1c[:], accum_out=sume[:]))
                            ck25[("gexp", t)] = s25.n

                        @blk.vector
                        def _(v, t=t):
                            s25.wait(v, ck25[("gexp", t)])
                            v.reciprocal(rsum[:], sume[:])
                            v.drain()
                            v.tensor_scalar(rsc[:], sxp[:], rsum[:], None, AluOpType.mult)
                            v.drain()
                            v.tensor_tensor(bsd[:], rsc[:], s_gbb[:], AluOpType.add)
                            v.drain()
                            v.reduce_max(m1c[:], bsd[:], axis=AX)
                            v.drain()
                            v.tensor_scalar(mask1[:], bsd[:], m1c[:], None, AluOpType.is_ge)
                            v.drain()
                            v.scalar_tensor_tensor(b2d[:], mask1[:], c_negbig, bsd[:],
                                                   AluOpType.mult, AluOpType.add)
                            v.drain()
                            v.reduce_max(m2c[:], b2d[:], axis=AX)
                            v.drain()
                            v.tensor_scalar(mask2[:], b2d[:], m2c[:], None, AluOpType.is_ge)
                            v.drain()
                            v.tensor_tensor(mask1[:], mask1[:], mask2[:], AluOpType.add)
                            v.drain()
                            v25.inc(v.tensor_tensor(rwbf[:], rsc[:], mask1[:], AluOpType.mult))

                        @blk.tensor
                        def _(tt, t=t):
                            v25.wait(tt)
                            if t >= 1:
                                v25.wait(tt, ck25[("rwev", t - 1)])
                            t25.inc(tt.transpose(psRw[:, 0:128], rwbf[:], s_identb[:]))

                        @blk.vector
                        def _(v, t=t):
                            t25.wait(v)
                            if t >= 1:
                                yRw.wait(v, 16 * t)
                            v25.inc(v.tensor_copy(rw_st[:], psRw[:, 0:128]))
                            ck25[("rwev", t)] = v25.n

                        @blk.sync
                        def _(sy, t=t):
                            v25.wait(sy, ck25[("rwev", t)])
                            yRw.inc(sy.dma_start(
                                bass.AP(rws.tensor, t * 128, [[T, 8], [1, 128]]),
                                rw_st[:]), 16)

                        @blk.vector
                        def _(v, t=t):
                            v.drain()
                            v25.inc(v.tensor_copy(selb[:], mask1[:]))
                            ck25[("selb", t)] = v25.n

                        @blk.tensor
                        def _(tt, t=t):
                            v25.wait(tt, ck25[("selb", t)])
                            v25.wait(tt, ck25[("tot_snap", t - 1)])
                            tt.matmul(psG[:, 16:24], s_ones1f[:], totrow[:],
                                      start=True, stop=True)
                            tt.matmul(psG[:, 8:16], s_triu[:], selb[:],
                                      start=True, stop=True)
                            mmt = tt.matmul(psTot[0:1, 0:8], s_onescb[:], selb[:],
                                            start=True, stop=True)
                            t25.inc(mmt)
                            ck25[("preMM", t)] = t25.n

                        @blk.vector
                        def _(v, t=t):
                            t25.wait(v, ck25[("preMM", t)])
                            if t >= 1:
                                yGs.wait(v, 128 * t)
                            v.tensor_copy(base_b[:], psG[:, 16:24])
                            v.drain()
                            v.tensor_tensor(a_s[:], psG[:, 8:16], base_b[:], AluOpType.add)
                            v.drain()
                            v.tensor_copy(mask1i[:], mask1[:])
                            v.drain()
                            v.select(idxf[:], mask1i[:], a_s[:], s_bigc[:], add_drain=True)
                            v.drain()
                            v.tensor_tensor(idxf[:], idxf[:], mask1[:], AluOpType.subtract)
                            v.drain()
                            v.tensor_tensor(idxf[:], idxf[:], s_eofs[:], AluOpType.add)
                            v.drain()
                            v.tensor_copy(idxu[:], idxf[:])
                            v.drain()
                            v.tensor_copy(ttile[:], psTot[0:1, 0:8])
                            v.drain()
                            v.tensor_tensor(totrow[:], totrow[:], ttile[:], AluOpType.add)
                            v.drain()
                            v.tensor_copy(stage[:, 0:1024], xn2tm[:])
                            v.tensor_copy(stage[:, 1024:1032], rwbf[:])
                            v.memset(stage[:, 1032:1033], float(t))
                            v.tensor_copy(stage[:, 1033:1034], s_pidx[:])
                            v25.inc(v.drain())
                            ck25[("tot_snap", t)] = v25.n
                            ck25[("stage", t)] = v25.n

                        @blk.gpsimd
                        def _(g, t=t):
                            if t == 0:
                                gz_ = cast_sems["gz"]
                                g.wait_ge(gz_.h, gz_.n)
                            v25.wait(g, ck25[("stage", t)])
                            with g.register(f"bnd{t}") as rb:
                                g.reg_mov(rb, 8 * GR - 1)
                                for e in range(8):
                                    yGs.inc(g.indirect_dma_start(
                                        gathd[:],
                                        bass.IndirectOffsetOnAxis(ap=idxu[:, e:e + 1], axis=0),
                                        stage[:], None,
                                        bounds_check=rb, oob_is_err=False), 16)

                        ck25[("vfin", t)] = v25.n

                    @blk.vector
                    def _(v):
                        v.drain()
                        v25.inc(v.tensor_copy(s_toti[:], totrow[:]))

        v3, s3, t3 = sems["v3"], sems["s3"], sems["t3"]
        yW13 = [sems["yW13a"], sems["yW13b"]]
        yW2 = sems["yW2a"]
        yGb = [sems["yW2b"], sems["yW2c"]]
        yXn2f = sems["yXn2f"]
        yX1t = sems["yX1t"]
        yOut = sems["yOut"]
        yRwF = sems["yRwF"]
        gsc3p = [sems["gsc3"], sems["gsc3b"]]
        GR_ = T + 512
        NCH = T // 512
        with ExitStack() as p3:
            w13e = [sbuf(f"w13e{i}", [128, 6 * 2048], BF16, p3) for i in range(2)]
            w2e = sbuf("w2e", [128, HC, DC, 128], BF16, p3)
            gbuf = [sbuf(f"gbuf{i}", [128, 4, 1040], BF16, p3) for i in range(2)]
            gfm = [sbuf(f"gfm{i}", [128, DC, 512], BF16, p3) for i in range(2)]
            hgb = [sbuf(f"hgb{i}", [128, HC, 512], BF16, p3) for i in range(2)]
            ydst = [sbuf(f"ydst{i}", [128, 4, DM], F32, p3) for i in range(2)]
            rwfc = [sbuf(f"rwfc{i}", [128, 4], F32, p3) for i in range(2)]
            yidxf = [sbuf(f"yidxf{i}", [128, 4], F32, p3) for i in range(2)]
            yidxu = [sbuf(f"yidxu{i}", [128, 4], U32, p3) for i in range(2)]
            ddum = sbuf("ddum", [128, 8 * NCH + 8], BF16, p3)
            ddumg = sbuf("ddumg", [128, 4 * (8 * NCH + 8)], BF16, p3)
            xn2f = sbuf("xn2f", [128, DC, 512], BF16, p3)
            x1t = sbuf("x1t", [128, 4, DM], F32, p3)
            rwT_f = sbuf("rwT_f", [8, 512], BF16, p3)
            s_b2 = sbuf("s_b2", [8, DM], BF16, p3)
            sA = [sbuf(f"sA{i}", [128, 512], F32, p3) for i in range(2)]
            sProd = sbuf("sProd", [128, 4, 512], F32, p3)
            yf = [sbuf(f"yf{i}", [128, 512], F32, p3) for i in range(2)]
            ost = sbuf("ost", [128, 4, 128], F32, p3)
            psA3 = [psum(f"psA3{i}", [128, 512], F32, p3) for i in range(2)]
            psB3 = [psum(f"psB3{i}", [128, 512], F32, p3) for i in range(2)]
            psX3 = psum("psX3", [128, 1024], BF16, p3)
            psY = [psum(f"psY{i}", [128, 512], F32, p3) for i in range(2)]
            psYT = psum("psYT", [128, 512], F32, p3)

            ck3 = {}
            SGRP = [(0, 4), (4, 8), (8, HC)]

            def w13_ap(e, half):
                h0, h1 = (0, 6) if half == 0 else (6, HC)
                base = (e * HC + h0) * (128 * 2048)
                return bass.AP(w13_in.tensor, base,
                               [[2048, 128], [128 * 2048, h1 - h0], [1, 2048]])

            with nc.Block("p3") as blk:
                @blk.gpsimd
                def _(g):
                    yGs.wait(g)
                    yX1.wait(g)
                    yXn2.wait(g)
                    yRw.wait(g)
                    yL25[0].wait(g)
                    yL25[1].wait(g)
                    cast_sems["b2"].inc(g.dma_start(s_b2[:], b2_in[:]), 16)

                @blk.scalar
                def _(s):
                    s.add_instruction(mybir.InstLoadActFuncSet(
                        name=nc.get_next_instruction_name(),
                        act_func_set_id=SILU_SET, ins=[], outs=[]))
                    yGs.wait(s)
                    yX1.wait(s)
                    yXn2.wait(s)
                    yRw.wait(s)
                    yL25[0].wait(s)
                    yL25[1].wait(s)

                @blk.sync
                def _(sy):
                    yX1.wait(sy)
                    yXn2.wait(sy)
                    yRw.wait(sy)
                    yGs.wait(sy)
                    yL25[0].wait(sy)
                    yL25[1].wait(sy)

                _cnd = [0]

                def cond(dec, e, c, fn, fn_else=None):
                    _cnd[0] += 1
                    nm = f"cnd{_cnd[0]}"

                    @dec
                    def _(eng, e=e, c=c, fn=fn, fn_else=fn_else, nm=nm):
                        snap = {s_: s_.n for s_ in sems.values()}
                        with eng.register(nm + "a") as rt, eng.register(nm + "b") as rc:
                            eng.reg_load(rt, s_toti[0:1, e:e + 1])
                            eng.reg_mov(rc, 512 * c)
                            with eng.If_lt(rc, rt):
                                fn(eng)
                            with eng.Else():
                                eng.drain()
                                if fn_else is not None:
                                    fn_else(eng)
                                else:
                                    for s_ in sems.values():
                                        d = s_.n - snap[s_]
                                        for _ in range(d):
                                            eng.sem_inc(s_.h, 1)

                rix = [0]
                for e in range(8):
                    for half in range(2):
                        @blk.gpsimd
                        def _(g, e=e, half=half):
                            if e >= 1:
                                t3.wait(g, ck3[("w13use", e - 1, half)])
                            hn = 6 if half == 0 else 5
                            yW13[half].inc(g.dma_start(
                                w13e[half][:, 0:hn * 2048], w13_ap(e, half)), 16)
                            ck3[("w13ld", e, half)] = yW13[half].n

                    @blk.gpsimd
                    def _(g, e=e):
                        if e >= 1:
                            t3.wait(g, ck3[("w2use", e - 1)])
                        yW2.inc(g.dma_start(w2e[:], w2_in[e]), 16)
                        ck3[("w2ld", e)] = yW2.n

                    for c in range(NCH):
                        r = rix[0]
                        rix[0] += 1
                        cp = c % 2

                        def f_sync(eng, e=e, c=c, cp=cp):
                            if ("gT", e, c - 2) in ck3:
                                t3.wait(eng, ck3[("gT", e, c - 2)])
                                v3.wait(eng, ck3[("gV", e, c - 2)])
                            elif ("gT_last", cp) in ck3:
                                t3.wait(eng, ck3[("gT_last", cp)])
                                v3.wait(eng, ck3[("gV_last", cp)])
                            yGb[cp].inc(eng.dma_start(
                                gbuf[cp][:],
                                bass.AP(gathd.tensor,
                                        (e * GR_ + c * 512) * 1040,
                                        [[1040, 128], [128 * 1040, 4], [1, 1040]])), 16)
                        def f_sync_else(eng, e=e, c=c, cp=cp, r_=r, prev=dict(ck3)):
                            if ("gT", e, c - 2) in prev:
                                t3.wait(eng, prev[("gT", e, c - 2)])
                                v3.wait(eng, prev[("gV", e, c - 2)])
                            elif ("gT_last", cp) in prev:
                                t3.wait(eng, prev[("gT_last", cp)])
                                v3.wait(eng, prev[("gV_last", cp)])
                            eng.dma_start(ddum[0:1, r_:r_ + 1], gathd[0:1, 0:1]
                                          ).then_inc(yGb[cp].h, 16)
                        cond(blk.sync, e, c, f_sync, fn_else=f_sync_else)
                        ck3[("gld", e, c)] = yGb[cp].n

                        for sub in range(4):
                            def f_Tt(eng, e=e, c=c, cp=cp, sub=sub):
                                if sub == 0:
                                    yGb[cp].wait(eng, ck3[("gld", e, c)])
                                if ("x3ev",) in ck3:
                                    v3.wait(eng, ck3[("x3ev",)])
                                for i in range(8):
                                    mm = eng.transpose(
                                        psX3[:, i * 128:(i + 1) * 128],
                                        gbuf[cp][:, sub, i * 128:(i + 1) * 128],
                                        s_identb[:])
                                t3.inc(mm)
                                ck3[("x3t",)] = t3.n
                            cond(blk.tensor, e, c, f_Tt)

                            def f_Vt(eng, e=e, c=c, cp=cp, sub=sub):
                                t3.wait(eng, ck3[("x3t",)])
                                if sub == 0 and ("gfmfree", cp) in ck3:
                                    t3.wait(eng, ck3[("gfmfree", cp)])
                                mm = eng.tensor_copy(
                                    bass.AP(gfm[cp].ap().tensor,
                                            gfm[cp].ap().offset + sub * 128,
                                            [gfm[cp].ap().ap[0], [512, DC], [1, 128]]),
                                    psX3[:].rearrange("p (d t) -> p d t", d=DC))
                                v3.inc(mm)
                                ck3[("x3ev",)] = v3.n
                            cond(blk.vector, e, c, f_Vt)

                        def f_Vidx(eng, e=e, c=c, cp=cp, r=r):
                            if r >= 2:
                                gsc3p[cp].wait(eng, 64 * (r // 2))
                            eng.tensor_copy(rwfc[cp][:], gbuf[cp][:, :, 1024 + e])
                            eng.drain()
                            eng.scalar_tensor_tensor(
                                yidxf[cp][:], gbuf[cp][:, :, 1032], c_128,
                                gbuf[cp][:, :, 1033], AluOpType.mult, AluOpType.add)
                            eng.drain()
                            mm = eng.tensor_copy(yidxu[cp][:], yidxf[cp][:])
                            v3.inc(mm)
                            ck3[("yidx", e, c)] = v3.n
                        cond(blk.vector, e, c, f_Vidx)
                        ck3[("gV", e, c)] = v3.n
                        ck3[("gV_last", cp)] = v3.n

                        def f_T2a(eng, e=e, c=c, cp=cp, vbase=None):
                            vbase = v3.n
                            v3.wait(eng, ck3[("x3ev",)])
                            if c == 0:
                                yW13[0].wait(eng, ck3[("w13ld", e, 0)])
                            for hc in range(HC):
                                hp = hc % 2
                                half = 0 if hc < 6 else 1
                                h0 = 0 if hc < 6 else 6
                                if c == 0 and hc == 6:
                                    yW13[1].wait(eng, ck3[("w13ld", e, 1)])
                                if hc >= 2:
                                    eng.wait_ge(v3.h, vbase + hc - 1)
                                elif ("ab_ev", hp) in ck3:
                                    v3.wait(eng, ck3[("ab_ev", hp)])
                                base = (hc - h0) * 2048
                                for dc in range(DC):
                                    eng.matmul(psA3[hp][:],
                                               w13e[half][:, base + dc * 128:base + dc * 128 + 128],
                                               gfm[cp][:, dc, :],
                                               start=(dc == 0), stop=(dc == DC - 1))
                                for dc in range(DC):
                                    mm = eng.matmul(psB3[hp][:],
                                                    w13e[half][:, base + 1024 + dc * 128:base + 1024 + dc * 128 + 128],
                                                    gfm[cp][:, dc, :],
                                                    start=(dc == 0), stop=(dc == DC - 1))
                                t3.inc(mm)
                                ck3[("ab", hp)] = t3.n
                        cond(blk.tensor, e, c, f_T2a)
                        ck3[("gfmfree", cp)] = t3.n
                        ck3[("gT", e, c)] = t3.n
                        ck3[("gT_last", cp)] = t3.n

                        def f_V2(eng, e=e, c=c, cp=cp):
                            sbase = s3.n
                            for hc in range(HC):
                                hp = hc % 2
                                slot = hc % 4
                                grp = hc // 4
                                t3.wait(eng, ck3[("abck", e, c, hc)])
                                if grp >= 1:
                                    eng.wait_ge(s3.h, sbase + grp)
                                elif ("sgrp_last",) in ck3:
                                    s3.wait(eng, ck3[("sgrp_last",)])
                                eng.tensor_scalar(sA[hp][:], psA3[hp][:],
                                                  s_b13[:, e, hc, 0:1], None, AluOpType.add)
                                eng.drain()
                                mm = eng.scalar_tensor_tensor(
                                    sProd[:, slot, :], psB3[hp][:], s_b13[:, e, hc, 1:2],
                                    sA[hp][:], AluOpType.add, AluOpType.mult)
                                v3.inc(mm)
                                ck3[("ab_ev", hp)] = v3.n
                                ck3[("pr", hc)] = v3.n
                        base_t = t3.n - HC
                        for hc in range(HC):
                            ck3[("abck", e, c, hc)] = base_t + hc + 1
                        cond(blk.vector, e, c, f_V2)

                        def f_S1(eng, e=e, c=c, cp=cp):
                            for gi, (g0_, g1_) in enumerate(SGRP):
                                v3.wait(eng, ck3[("prck", e, c, g1_ - 1)])
                                if ("hgbfree", cp) in ck3 and gi == 0:
                                    t3.wait(eng, ck3[("hgbfree", cp)])
                                ng = g1_ - g0_
                                mm = eng.activation(
                                    hgb[cp][:, g0_:g1_, :].rearrange("p g d -> p (g d)"),
                                    sProd[:, 0:ng, :].rearrange("p g d -> p (g d)"),
                                    AF.Silu)
                                s3.inc(mm)
                                ck3[("sgrp_p", gi)] = s3.n
                                ck3[("sgrp_last",)] = s3.n
                        base_v = v3.n - HC
                        for hc in range(HC):
                            ck3[("prck", e, c, hc)] = base_v + hc + 1
                        cond(blk.scalar, e, c, f_S1)
                        ck3[("hrdy", e, c)] = s3.n

                        def mk_TB(mcs, first, e=e, c=c, cp=cp):
                            def f(eng):
                                if first:
                                    s3.wait(eng, ck3[("hrdy", e, c)])
                                    yW2.wait(eng, ck3[("w2ld", e)])
                                for kind, mc in mcs:
                                    mp = mc % 2
                                    if kind == "mm":
                                        if ("yevck", mp) in ck3:
                                            s3.wait(eng, ck3[("yevck", mp)])
                                        for hc in range(HC):
                                            mm = eng.matmul(psY[mp][:],
                                                            w2e[:, hc, mc, :],
                                                            hgb[cp][:, hc, :],
                                                            start=(hc == 0), stop=(hc == HC - 1))
                                        t3.inc(mm)
                                        ck3[("ymm", mp)] = t3.n
                                    else:
                                        s3.wait(eng, ck3[("yfck", mc % 2)])
                                        if ("ytevck",) in ck3:
                                            v3.wait(eng, ck3[("ytevck",)])
                                        for sub in range(4):
                                            mm = eng.transpose(
                                                psYT[:, sub * 128:(sub + 1) * 128],
                                                yf[mc % 2][:, sub * 128:(sub + 1) * 128],
                                                s_identf[:])
                                        t3.inc(mm)
                                        ck3[("ytck",)] = t3.n
                            return f

                        def mk_SB(mcs, e=e, c=c):
                            def f(eng):
                                for mc in mcs:
                                    mp = mc % 2
                                    t3.wait(eng, ck3[("ymmck", mp)])
                                    if ("ytck2", mp) in ck3:
                                        t3.wait(eng, ck3[("ytck2", mp)])
                                    mm = eng.copy(yf[mp][:], psY[mp][:])
                                    s3.inc(mm)
                                    ck3[("yevck", mp)] = s3.n
                                    ck3[("yfck", mp)] = s3.n
                            return f

                        def mk_VB(mcs, e=e, c=c, cp=cp):
                            def f(eng):
                                for mc in mcs:
                                    t3.wait(eng, ck3[("ytck3", mc)])
                                    mm = eng.tensor_copy(
                                        bass.AP(ydst[cp].ap().tensor,
                                                ydst[cp].ap().offset + mc * 128,
                                                [ydst[cp].ap().ap[0], [DM, 4], [1, 128]]),
                                        psYT[:].rearrange("p (s d) -> p s d", s=4))
                                    v3.inc(mm)
                                    ck3[("ytevck",)] = v3.n
                            return f

                        for mc in range(DC):
                            mp = mc % 2
                            cond(blk.tensor, e, c, mk_TB([("mm", mc)], first=(mc == 0)))
                            ck3[("ymmck", mp)] = t3.n
                            cond(blk.scalar, e, c, mk_SB([mc]))
                            cond(blk.tensor, e, c, mk_TB([("yt", mc)], first=False))
                            ck3[("ytck2", mp)] = t3.n
                            ck3[("ytck3", mc)] = t3.n
                            cond(blk.vector, e, c, mk_VB([mc]))
                        ck3[("hgbfree", cp)] = t3.n

                        def f_Vsc(eng, e=e, c=c, cp=cp):
                            v3.wait(eng, ck3[("yidx", e, c)])
                            eng.drain()
                            for sub in range(4):
                                mm = eng.tensor_scalar(ydst[cp][:, sub, :],
                                                       ydst[cp][:, sub, :],
                                                       rwfc[cp][:, sub:sub + 1], None,
                                                       AluOpType.mult)
                            eng.drain()
                            v3.inc(mm)
                            ck3[("ysc", e, c)] = v3.n
                        cond(blk.vector, e, c, f_Vsc)

                        def f_G1(eng, e=e, c=c, cp=cp, r=r):
                            v3.wait(eng, ck3[("ysc", e, c)])
                            if e >= 1 and c == 0:
                                gsc3p[0].wait(eng, 64 * (e * NCH // 2))
                                gsc3p[1].wait(eng, 64 * (e * NCH // 2))
                            eng.drain()
                            for sub in range(4):
                                gsc3p[cp].inc(eng.indirect_dma_start(
                                    x1s[:],
                                    bass.IndirectOffsetOnAxis(ap=yidxu[cp][:, sub:sub + 1], axis=0),
                                    ydst[cp][:, sub, :], None,
                                    compute_op=AluOpType.add), 16)
                        def f_G1_else(eng, e=e, c=c, cp=cp, r=r):
                            if e >= 1 and c == 0:
                                gsc3p[0].wait(eng, 64 * (e * NCH // 2))
                                gsc3p[1].wait(eng, 64 * (e * NCH // 2))
                            for sub in range(4):
                                eng.dma_start(ddumg[0:1, 4 * r + sub:4 * r + sub + 1],
                                              gathd[0:1, 0:1]).then_inc(gsc3p[cp].h, 16)
                        cond(blk.gpsimd, e, c, f_G1, fn_else=f_G1_else)

                    ck3[("w13use", e, 0)] = t3.n
                    ck3[("w13use", e, 1)] = t3.n
                    ck3[("w2use", e)] = t3.n


                for half in range(2):
                    @blk.gpsimd
                    def _(g, half=half):
                        t3.wait(g, ck3[("w13use", 7, half)])
                        hn = 6 if half == 0 else 5
                        yW13[half].inc(g.dma_start(
                            w13e[half][:, 0:hn * 2048], w13_ap(8, half)), 16)
                        ck3[("w13ld", 8, half)] = yW13[half].n

                @blk.gpsimd
                def _(g):
                    t3.wait(g, ck3[("w2use", 7)])
                    yW2.inc(g.dma_start(w2e[:], w2_in[8]), 16)
                    ck3[("w2ld", 8)] = yW2.n

                for f in range(NQ):
                    @blk.sync
                    def _(sy, f=f):
                        if f >= 1:
                            t3.wait(sy, ck3[("ymmB", f - 1, DC - 1)])
                        yRwF.inc(sy.dma_start(
                            rwT_f[:], bass.AP(rws.tensor, f * 512, [[T, 8], [1, 512]])), 16)
                        ck3[("rwfld", f)] = yRwF.n
                        if f >= 1:
                            t3.wait(sy, ck3[("sAdoneB", f - 1)])
                        yXn2f.inc(sy.dma_start(xn2f[:], xn2s[f]), 16)
                        if f == 0:
                            gsc3p[0].wait(sy)
                            gsc3p[1].wait(sy)
                        if f >= 1:
                            v3.wait(sy, ck3[("ost", f - 1)])
                        yX1t.inc(sy.dma_start(
                            x1t[:],
                            bass.AP(x1s.tensor, f * 512 * DM,
                                    [[DM, 128], [128 * DM, 4], [1, DM]])), 16)

                    for hc in range(HC):
                        hp = hc % 2
                        slot = hc % 4
                        grp = hc // 4
                        half = 0 if hc < 6 else 1
                        h0 = 0 if hc < 6 else 6

                        @blk.tensor
                        def _(tt, f=f, hc=hc, hp=hp, half=half, h0=h0):
                            if hc == 0:
                                if f == 0:
                                    yW13[0].wait(tt, ck3[("w13ld", 8, 0)])
                                    yW13[1].wait(tt, ck3[("w13ld", 8, 1)])
                                yXn2f.wait(tt)
                            if ("ab_evB", hp) in ck3:
                                v3.wait(tt, ck3[("ab_evB", hp)])
                            elif f == 0 and hc < 2 and ("ab_ev", hp) in ck3:
                                v3.wait(tt, ck3[("ab_ev", hp)])
                            base = (hc - h0) * 2048
                            for dc in range(DC):
                                tt.matmul(psA3[hp][:],
                                          w13e[half][:, base + dc * 128:base + dc * 128 + 128],
                                          xn2f[:, dc, :],
                                          start=(dc == 0), stop=(dc == DC - 1))
                            for dc in range(DC):
                                mm = tt.matmul(psB3[hp][:],
                                               w13e[half][:, base + 1024 + dc * 128:base + 1024 + dc * 128 + 128],
                                               xn2f[:, dc, :],
                                               start=(dc == 0), stop=(dc == DC - 1))
                            t3.inc(mm)
                            ck3[("abB", hp)] = t3.n
                            if hc == HC - 1:
                                ck3[("sAdoneB", f)] = t3.n

                        @blk.vector
                        def _(v, f=f, hc=hc, hp=hp, slot=slot, grp=grp):
                            t3.wait(v, ck3[("abB", hp)])
                            if ("sgrpB", grp - 1) in ck3:
                                s3.wait(v, ck3[("sgrpB", grp - 1)])
                            elif grp == 0 and ("sgrpB_last",) in ck3:
                                s3.wait(v, ck3[("sgrpB_last",)])
                            elif grp == 0 and ("sgrp_last",) in ck3 and f == 0:
                                s3.wait(v, ck3[("sgrp_last",)])
                            v.tensor_scalar(sA[hp][:], psA3[hp][:],
                                            s_b13[:, 8, hc, 0:1], None, AluOpType.add)
                            v.drain()
                            mm = v.scalar_tensor_tensor(
                                sProd[:, slot, :], psB3[hp][:], s_b13[:, 8, hc, 1:2],
                                sA[hp][:], AluOpType.add, AluOpType.mult)
                            v3.inc(mm)
                            ck3[("ab_evB", hp)] = v3.n
                            ck3[("prB", hc)] = v3.n

                        if hc == [g1 - 1 for g0, g1 in SGRP if g0 <= hc < g1][0]:
                            gi = [i for i, (g0, g1) in enumerate(SGRP) if g0 <= hc < g1][0]
                            g0_, g1_ = SGRP[gi]

                            @blk.scalar
                            def _(s, f=f, gi=gi, g0_=g0_, g1_=g1_):
                                v3.wait(s, ck3[("prB", g1_ - 1)])
                                ng = g1_ - g0_
                                if f >= 1 and gi == 0:
                                    t3.wait(s, ck3[("ymmB", f - 1, DC - 1)])
                                elif f == 0 and gi == 0 and ("hgbfree", 0) in ck3:
                                    t3.wait(s, ck3[("hgbfree", 0)])
                                mm = s.activation(
                                    hgb[0][:, g0_:g1_, :].rearrange("p g d -> p (g d)"),
                                    sProd[:, 0:ng, :].rearrange("p g d -> p (g d)"),
                                    AF.Silu)
                                s3.inc(mm)
                                ck3[("sgrpB", gi)] = s3.n
                                ck3[("sgrpB_last",)] = s3.n
                                if g1_ == HC:
                                    ck3[("hrdyB", f)] = s3.n

                    for mc in range(DC):
                        mp = mc % 2

                        @blk.tensor
                        def _(tt, f=f, mc=mc, mp=mp):
                            s3.wait(tt, ck3[("hrdyB", f)])
                            if mc == 0:
                                if f == 0:
                                    cast_wait(tt, "b2")
                                yRwF.wait(tt, ck3[("rwfld", f)])
                            if ("yevB", mp) in ck3:
                                s3.wait(tt, ck3[("yevB", mp)])
                            elif f == 0 and mc < 2 and ("yevck", mp) in ck3:
                                s3.wait(tt, ck3[("yevck", mp)])
                            yW2.wait(tt, ck3[("w2ld", 8)])
                            for hc in range(HC):
                                tt.matmul(psY[mp][:], w2e[:, hc, mc, :], hgb[0][:, hc, :],
                                          start=(hc == 0), stop=False)
                            mm = tt.matmul(psY[mp][:], s_b2[:, mc * 128:(mc + 1) * 128],
                                           rwT_f[:], start=False, stop=True)
                            t3.inc(mm)
                            ck3[("ymmB", f, mc)] = t3.n

                        @blk.scalar
                        def _(s, f=f, mc=mc, mp=mp):
                            t3.wait(s, ck3[("ymmB", f, mc)])
                            s3.inc(s.activation(yf[mp][:], psY[mp][:], AF.Identity,
                                                bias=s_sb2[:, mc:mc + 1]))
                            ck3[("yevB", mp)] = s3.n

                        @blk.tensor
                        def _(tt, f=f, mc=mc, mp=mp):
                            s3.wait(tt, ck3[("yevB", mp)])
                            if ("ytevB",) in ck3:
                                v3.wait(tt, ck3[("ytevB",)])
                            elif ("ytevck",) in ck3 and f == 0 and mc == 0:
                                v3.wait(tt, ck3[("ytevck",)])
                            for sub in range(4):
                                mm = tt.transpose(psYT[:, sub * 128:(sub + 1) * 128],
                                                  yf[mp][:, sub * 128:(sub + 1) * 128],
                                                  s_identf[:])
                            t3.inc(mm)
                            ck3[("ytB", f, mc)] = t3.n

                        @blk.vector
                        def _(v, f=f, mc=mc):
                            t3.wait(v, ck3[("ytB", f, mc)])
                            yX1t.wait(v, 16 * (f + 1))
                            k_out = f * DC + mc
                            if k_out >= 1:
                                yOut.wait(v, 64 * k_out)
                            v3.inc(v.tensor_tensor(
                                ost[:],
                                psYT[:].rearrange("p (s d) -> p s d", s=4),
                                bass.AP(x1t.ap().tensor, x1t.ap().offset + mc * 128,
                                        [x1t.ap().ap[0], [DM, 4], [1, 128]]),
                                AluOpType.add))
                            ck3[("ytevB",)] = v3.n
                            if mc == DC - 1:
                                ck3[("ost", f)] = v3.n

                        @blk.sync
                        def _(sy, f=f, mc=mc):
                            v3.wait(sy, ck3[("ytevB",)])
                            for sub in range(4):
                                yOut.inc(sy.dma_start(
                                    out_d[f * 512 + sub * 128:f * 512 + (sub + 1) * 128,
                                          mc * 128:(mc + 1) * 128],
                                    ost[:, sub, :]), 16)

                @blk.sync
                def _(sy):
                    yOut.wait(sy)
                    for s_ in list(sems.values()) + list(cast_sems.values()):
                        if s_.n:
                            sy.wait_ge(s_.h, s_.n)

    return nc
    return nc



def _prep_inputs(T, ins):
    NT = T // 128
    f32 = np.float32
    w1, w3, w2 = ins["L0_ffn_w1"], ins["L0_ffn_w3"], ins["L0_ffn_w2"]
    sw1, sw3, sw2 = ins["L0_ffn_sw1"], ins["L0_ffn_sw3"], ins["L0_ffn_sw2"]
    b1, b3, b2 = ins["L0_ffn_b1"], ins["L0_ffn_b3"], ins["L0_ffn_b2"]
    sb1, sb3, sb2 = ins["L0_ffn_sb1"], ins["L0_ffn_sb3"], ins["L0_ffn_sb2"]

    w1a = np.concatenate([w1, sw1[None]], 0)
    w3a = np.concatenate([w3, sw3[None]], 0)
    w13 = np.stack([w1a, w3a], 1)
    w13 = w13.reshape(NE, 2, DC, 128, HC, 128)
    w13 = np.ascontiguousarray(w13.transpose(0, 4, 3, 1, 2, 5))

    w2a = np.concatenate([w2, sw2[None]], 0)
    w2a = w2a.reshape(NE, HC, 128, DC, 128)
    w2a = np.ascontiguousarray(w2a.transpose(0, 2, 1, 3, 4))

    b1a = np.concatenate([b1, sb1[None]], 0)
    b3a = np.concatenate([b3, sb3[None]], 0)
    b13 = np.stack([b1a, b3a], -1).reshape(NE, HC, 128, 2)
    b13 = np.ascontiguousarray(b13.transpose(2, 0, 1, 3))

    gate_w = ins["L0_ffn_gate_w"]
    gwt = np.ascontiguousarray(gate_w.T.reshape(DC, 128, 8).transpose(1, 0, 2))
    gbb = np.ascontiguousarray(
        np.broadcast_to(np.asarray(ins["L0_ffn_gate_b"], f32)[None, :], (128, 8)))

    pos = np.arange(T, dtype=np.float64)
    freqs = np.power(10000.0, -np.arange(64, dtype=np.float64) / 64.0)
    rad = pos[:, None] * freqs[None, :]
    cosT = np.cos(rad).astype(ml_dtypes.bfloat16).reshape(NT, 128, 64)
    sinT = np.sin(rad).astype(ml_dtypes.bfloat16).reshape(NT, 128, 64)

    j = np.arange(128)[:, None]
    qq = np.arange(512)[None, :]
    masks = np.stack([(qq >= j + 128 * k) for k in range(4)], 0).astype(ml_dtypes.bfloat16)
    masks = np.ascontiguousarray(masks.transpose(1, 0, 2))

    gzt = np.zeros((4, 128, 1040), ml_dtypes.bfloat16)
    for k in range(4):
        gzt[k, :, 1032] = ml_dtypes.bfloat16(float(NT + k))
    gzt[:, :, 1033] = np.arange(128, dtype=np.float32).astype(
        ml_dtypes.bfloat16)[None, :]
    gzt = gzt.reshape(512, 1040)

    ccols = np.zeros((128, 6), f32)
    ccols[:, 0] = EPS_LN
    ccols[:, 1] = EPS_RMS
    ccols[:, 2] = -1.0
    ccols[:, 3] = NEG_BIG
    ccols[:, 4] = 128.0
    ccols[:, 5] = 511.0

    return {
        "wq": np.ascontiguousarray(ins["L0_attn_wq"].reshape(DM, DM)),
        "wk": np.ascontiguousarray(ins["L0_attn_wk"].reshape(DM, DM)),
        "wv": np.ascontiguousarray(ins["L0_attn_wv"].reshape(DM, DM)),
        "wo": np.ascontiguousarray(ins["L0_attn_wo"]),
        "w13": w13, "w2": w2a, "b13": b13,
        "b2": np.ascontiguousarray(b2),
        "sb2c": np.ascontiguousarray(np.asarray(sb2, f32).reshape(DC, 128).T),
        "gwt": gwt, "gbb": gbb,
        "cosT": cosT, "sinT": sinT,
        "identb": np.eye(128, dtype=ml_dtypes.bfloat16),
        "identf": np.eye(128, dtype=f32),
        "ones1b": np.ones((1, 128), ml_dtypes.bfloat16),
        "onescb": np.ones((128, 1), ml_dtypes.bfloat16),
        "ones1f": np.ones((1, 128), f32),
        "masksb": masks,
        "ccols": ccols,
        "triu": np.triu(np.ones((128, 128))).astype(ml_dtypes.bfloat16),
        "pidx": np.arange(128, dtype=np.float32).astype(ml_dtypes.bfloat16).reshape(128, 1),
        "bigc": np.full((128, 8), 1.0e9, np.float32),
        "eofs": np.ascontiguousarray(
            np.broadcast_to((np.arange(8, dtype=np.float32) * (T + 512))[None, :],
                            (128, 8))),
        "gzt": gzt,
        "zrow": np.zeros((128, DM), f32),
    }


def kernel(**ins):
    x = np.asarray(ins["L0_attn_x"])
    B, T, _ = x.shape
    base = _prep_inputs(T, {k: np.asarray(v) for k, v in ins.items()})
    nc = build_nc(T)
    in_maps = []
    for b in range(B):
        m = dict(base)
        m["x"] = np.ascontiguousarray(x[b])
        in_maps.append(m)
    res = run_bass_kernel_spmd(nc, in_maps, core_ids=list(range(B)))
    return np.stack([np.asarray(r["out"]) for r in res.results], 0)


# version 17
# speedup vs baseline: 1.3956x; 1.2111x over previous
import numpy as np
import ml_dtypes
from contextlib import ExitStack

import concourse.bass as bass
import concourse.mybir as mybir
from concourse.bass_utils import run_bass_kernel_spmd
from concourse.alu_op_type import AluOpType

F32 = mybir.dt.float32
BF16 = mybir.dt.bfloat16
U32 = mybir.dt.uint32
I32 = mybir.dt.int32

def _act_set_id(name):
    from concourse.hw_specs import get_activation_tables
    import concourse.bass as _b
    _nc = _b.Bass("TRN2", target_bir_lowering=False)
    return list(get_activation_tables(_nc.m.arch)).index(name)

LNEXP_SET = None
SILU_SET = None
AF = mybir.ActivationFunctionType
AX = mybir.AxisListType.X

H = 8
DH = 128
DM = 1024
DC = 8
HF = 1408
HC = 11
NE = 9
EPS_LN = 1e-5
EPS_RMS = 1e-6
NEG_BIG = -1.0e30

W13_SL = [(h, min(h + 2, HC)) for h in range(0, HC, 2)]
W2_GR = [(0, 3), (3, 6), (6, 9)]


class Ctr:

    def __init__(self, ctx, nc, name):
        self.h = ctx.enter_context(nc.semaphore(name))
        self.n = 0

    def inc(self, instr, k=1):
        instr.then_inc(self.h, k)
        self.n += k
        return instr

    def wait(self, eng, n=None):
        eng.wait_ge(self.h, self.n if n is None else n)


def build_nc(T, debug=False):
    from concourse.hw_specs import get_activation_tables
    NT = T // 128
    NQ = T // 512
    nc = bass.Bass("TRN2", target_bir_lowering=False, debug=debug)
    tab_names = list(get_activation_tables(nc.m.arch))
    LNEXP_SET = tab_names.index("natural_log_exp_and_others")
    SILU_SET = tab_names.index("silu_and_others")

    x_in = nc.dram_tensor("x", [T, DM], F32, kind="ExternalInput").ap()
    wq_in = nc.dram_tensor("wq", [DM, DM], F32, kind="ExternalInput").ap()
    wk_in = nc.dram_tensor("wk", [DM, DM], F32, kind="ExternalInput").ap()
    wv_in = nc.dram_tensor("wv", [DM, DM], F32, kind="ExternalInput").ap()
    wo_in = nc.dram_tensor("wo", [H, DH, DM], F32, kind="ExternalInput").ap()
    w13_in = nc.dram_tensor("w13", [NE, HC, 128, 2, DC, 128], F32, kind="ExternalInput").ap()
    w2_in = nc.dram_tensor("w2", [NE, 128, HC, DC, 128], F32, kind="ExternalInput").ap()
    b13_in = nc.dram_tensor("b13", [128, NE, HC, 2], F32, kind="ExternalInput").ap()
    b2_in = nc.dram_tensor("b2", [8, DM], F32, kind="ExternalInput").ap()
    sb2_in = nc.dram_tensor("sb2c", [128, DC], F32, kind="ExternalInput").ap()
    gwt_in = nc.dram_tensor("gwt", [128, DC, 8], F32, kind="ExternalInput").ap()
    gbb_in = nc.dram_tensor("gbb", [128, 8], F32, kind="ExternalInput").ap()
    cos_in = nc.dram_tensor("cosT", [NT, 128, 64], BF16, kind="ExternalInput").ap()
    sin_in = nc.dram_tensor("sinT", [NT, 128, 64], BF16, kind="ExternalInput").ap()
    identb_in = nc.dram_tensor("identb", [128, 128], BF16, kind="ExternalInput").ap()
    identf_in = nc.dram_tensor("identf", [128, 128], F32, kind="ExternalInput").ap()
    ones1b_in = nc.dram_tensor("ones1b", [1, 128], BF16, kind="ExternalInput").ap()
    onescb_in = nc.dram_tensor("onescb", [128, 1], BF16, kind="ExternalInput").ap()
    ones1f_in = nc.dram_tensor("ones1f", [1, 128], F32, kind="ExternalInput").ap()
    masks_in = nc.dram_tensor("masksb", [128, 4, 512], BF16, kind="ExternalInput").ap()
    triu_in = nc.dram_tensor("triu", [128, 128], BF16, kind="ExternalInput").ap()
    pidx_in = nc.dram_tensor("pidx", [128, 1], BF16, kind="ExternalInput").ap()
    bigc_in = nc.dram_tensor("bigc", [128, 8], F32, kind="ExternalInput").ap()
    eofs_in = nc.dram_tensor("eofs", [128, 8], F32, kind="ExternalInput").ap()
    ccol_in = nc.dram_tensor("ccols", [128, 6], F32, kind="ExternalInput").ap()
    gzt_in = nc.dram_tensor("gzt", [512, 1040], BF16, kind="ExternalInput").ap()
    zrow_in = nc.dram_tensor("zrow", [128, DM], F32, kind="ExternalInput").ap()

    out_d = nc.dram_tensor("out", [T, DM], F32, kind="ExternalOutput").ap()

    qs = nc.dram_tensor("qscr", [NT, 128, H, 128], BF16).ap()
    attnscr = nc.dram_tensor("attnscr", [NQ, DC, 128, 512], F32).ap()
    x1s = nc.dram_tensor("x1scr", [T + 512, DM], F32).ap()
    xn2s = nc.dram_tensor("xn2scr", [NQ, 128, DC, 512], BF16).ap()
    rws = nc.dram_tensor("rwscr", [8, T], BF16).ap()
    GR = T + 512
    gathd = nc.dram_tensor("gathd", [8 * (T + 512), 1040], BF16).ap()

    with ExitStack() as top:
        def sbuf(name, shape, dt, stack):
            return stack.enter_context(nc.sbuf_tensor(name, shape, dt))

        def psum(name, shape, dt, stack):
            return stack.enter_context(nc.psum_tensor(name, shape, dt))

        s_identb = sbuf("s_identb", [128, 128], BF16, top)
        s_identf = sbuf("s_identf", [128, 128], F32, top)
        s_ones1b = sbuf("s_ones1b", [1, 128], BF16, top)
        s_onescb = sbuf("s_onescb", [128, 1], BF16, top)
        s_ccol = sbuf("s_ccol", [128, 6], F32, top)
        s_b13 = sbuf("s_b13", [128, NE, HC, 2], F32, top)
        s_sb2 = sbuf("s_sb2", [128, DC], F32, top)
        s_gwt = sbuf("s_gwt", [128, DC, 8], BF16, top)
        s_gbb = sbuf("s_gbb", [128, 8], F32, top)
        s_pidx = sbuf("s_pidx", [128, 1], BF16, top)
        s_triu = sbuf("s_triu", [128, 128], BF16, top)
        s_bigc = sbuf("s_bigc", [128, 8], F32, top)
        s_eofs = sbuf("s_eofs", [128, 8], F32, top)
        s_toti = sbuf("s_toti", [1, 8], I32, top)

        c_epsln = s_ccol[:, 0:1]
        c_epsrms = s_ccol[:, 1:2]
        c_neg1 = s_ccol[:, 2:3]
        c_negbig = s_ccol[:, 3:4]
        c_128 = s_ccol[:, 4:5]

        cast_sems = {}
        for cname in ["wqkv", "wo", "gate", "b2", "gz"]:
            cast_sems[cname] = Ctr(top, nc, "cast_" + cname)

        def cast_wait(eng, cname):
            s = cast_sems[cname]
            eng.wait_ge(s.h, s.n)

        sems = {}
        for name in ["c0",
                     "v1", "s1", "t1",
                     "v2", "s2", "t2",
                     "v25", "s25", "t25",
                     "v3", "s3", "t3",
                     "yLd0", "yLd1", "yQs",
                     "yQq", "yAt0", "yAt1",
                     "yL25a", "yL25b", "yX1", "yXn2",
                     "yW13a", "yW13b", "yW13c", "yW2a", "yW2b", "yW2c",
                     "yXn2f", "yX1t", "yOut", "yRw", "yRwF", "yGs", "gsc3", "gsc3b"]:
            sems[name] = Ctr(top, nc, "sem_" + name)
        c0 = sems["c0"]

        with ExitStack() as p012:
            k_sb = sbuf("k_sb", [128, H, NT, 128], BF16, p012)
            s_ones1f = sbuf("s_ones1f", [1, 128], F32, p012)
            v_sb = sbuf("v_sb", [128, H, NT, 128], BF16, p012)

            v1, s1, t1 = sems["v1"], sems["s1"], sems["t1"]
            yLd = [sems["yLd0"], sems["yLd1"]]
            yQs = sems["yQs"]
            with ExitStack() as p1:
                wq_sb = sbuf("wq_sb", [128, DC, DM], BF16, p1)
                wk_sb = sbuf("wk_sb", [128, DC, DM], BF16, p1)
                wv_sb = sbuf("wv_sb", [128, DC, DM], BF16, p1)
                xb = [sbuf(f"xb{i}", [128, DM], F32, p1) for i in range(2)]
                cb = [sbuf(f"cb{i}", [128, 64], BF16, p1) for i in range(2)]
                sb_ = [sbuf(f"sb{i}", [128, 64], BF16, p1) for i in range(2)]
                mv = sbuf("mv", [128, 14], F32, p1)
                xn_tm = sbuf("xn_tm", [128, DM], BF16, p1)
                xn_fm = sbuf("xn_fm", [128, DM], BF16, p1)
                sqb = sbuf("sqb", [128, DM], F32, p1)
                qkb_q = sbuf("qkb_q", [128, DM], BF16, p1)
                qkb_k = sbuf("qkb_k", [128, DM], BF16, p1)
                rt_q = sbuf("rt_q", [128, DM], BF16, p1)
                rt_k = sbuf("rt_k", [128, DM], BF16, p1)
                q_rope = sbuf("q_rope", [128, DM], BF16, p1)
                k_rope = sbuf("k_rope", [128, DM], BF16, p1)
                qfm_st = rt_q
                msq = sbuf("msq", [128, 8], F32, p1)
                rsq = sbuf("rsq", [128, 8], F32, p1)
                msq2 = sbuf("msq2", [128, 8], F32, p1)
                rsq2 = sbuf("rsq2", [128, 8], F32, p1)
                psQ = psum("psQ", [128, DM], F32, p1)
                psK = psum("psK", [128, DM], F32, p1)
                psV = psum("psV", [128, DM], F32, p1)
                psXP = psum("psXP", [128, DM], BF16, p1)
                psT2 = psum("psT2", [128, DM], BF16, p1)

                ck = {}

                with nc.Block("p1") as blk:
                    @blk.scalar
                    def _(s):
                        s.add_instruction(mybir.InstLoadActFuncSet(
                            name=nc.get_next_instruction_name(),
                            act_func_set_id=LNEXP_SET, ins=[], outs=[]))
                        c0.wait(s, 192)

                    @blk.sync
                    def _(sy):
                        for dst, srcap in [
                            (s_identb[:], identb_in[:]), (s_identf[:], identf_in[:]),
                            (s_ones1b[:], ones1b_in[:]), (s_onescb[:], onescb_in[:]),
                            (s_ccol[:], ccol_in[:]), (s_b13[:], b13_in[:]),
                            (s_sb2[:], sb2_in[:]), (s_gbb[:], gbb_in[:]),
                            (s_pidx[:], pidx_in[:]), (s_triu[:], triu_in[:]),
                            (s_bigc[:], bigc_in[:]), (s_eofs[:], eofs_in[:]),
                        ]:
                            c0.inc(sy.dma_start(dst, srcap), 16)

                    @blk.gpsimd
                    def _(g):
                        cq = cast_sems["wqkv"]
                        cq.inc(g.dma_start(wq_sb[:], wq_in.rearrange("(c p) n -> p c n", p=128)), 16)
                        cq.inc(g.dma_start(wk_sb[:], wk_in.rearrange("(c p) n -> p c n", p=128)), 16)
                        cq.inc(g.dma_start(wv_sb[:], wv_in.rearrange("(c p) n -> p c n", p=128)), 16)
                        cast_sems["gate"].inc(g.dma_start(s_gwt[:], gwt_in[:]), 16)

                    @blk.vector
                    def _(v):
                        c0.wait(v, 192)

                    @blk.tensor
                    def _(tt):
                        c0.wait(tt, 192)

                    def emit_qk_T(t):
                        for which, ro in [("q", q_rope), ("k", k_rope)]:
                            @blk.tensor
                            def _(tt, t=t, which=which, ro=ro):
                                v1.wait(tt, ck[("rope", t, which)])
                                if not (t == 0 and which == "q"):
                                    v1.wait(tt, ck[("psT2",)])
                                for h in range(H):
                                    mm = tt.transpose(psT2[:, h * 128:(h + 1) * 128],
                                                      ro[:, h * 128:(h + 1) * 128], s_identb[:])
                                t1.inc(mm)
                                ck[("qT" if which == "q" else "kT", t)] = t1.n

                            @blk.vector
                            def _(v, t=t, which=which):
                                t1.wait(v)
                                if which == "q":
                                    if t >= 1:
                                        yQs.wait(v, 16 * t)
                                    v1.inc(v.tensor_copy(qfm_st[:], psT2[:]))
                                else:
                                    v1.inc(v.tensor_copy(
                                        k_sb[:, :, t, :],
                                        psT2[:].rearrange("p (h d) -> p h d", h=H)))
                                ck[("psT2",)] = v1.n
                                if which == "q":
                                    ck[("qfm", t)] = v1.n

                        @blk.sync
                        def _(sy, t=t):
                            v1.wait(sy, ck[("qfm", t)])
                            yQs.inc(sy.dma_start(qs[t].rearrange("p h d -> p (h d)"), qfm_st[:]), 16)

                    def emit_loads(t):
                        p = t % 2

                        @blk.sync
                        def _(sy, t=t, p=p):
                            if t >= 2:
                                v1.wait(sy, ck[("ropek", t - 2)])
                                s1.wait(sy, ck[("apply", t - 2)])
                            yLd[p].inc(sy.dma_start(xb[p][:], x_in[t * 128:(t + 1) * 128, :]), 16)
                            yLd[p].inc(sy.dma_start(cb[p][:], cos_in[t]), 16)
                            yLd[p].inc(sy.dma_start(sb_[p][:], sin_in[t]), 16)
                            ck[("xld", t)] = yLd[p].n

                    def emit_ln(t):
                        p = t % 2

                        @blk.vector
                        def _(v, t=t, p=p):
                            yLd[p].wait(v, ck[("xld", t)])
                            v.bn_stats(mv[:, 0:6], xb[p][:, 0:512])
                            v.bn_stats(mv[:, 6:12], xb[p][:, 512:1024])
                            v.drain()
                            v.bn_aggr(mv[:, 12:14], mv[:, 0:12])
                            v.drain()
                            v.tensor_scalar(mv[:, 13:14], mv[:, 13:14], c_epsln,
                                            None, AluOpType.add)
                            v.drain()
                            v1.inc(v.tensor_scalar(mv[:, 11:12], mv[:, 12:13], c_neg1,
                                                   None, AluOpType.mult))
                            ck[("stats", t)] = v1.n

                        @blk.scalar
                        def _(s, t=t, p=p):
                            v1.wait(s, ck[("stats", t)])
                            s.activation(mv[:, 13:14], mv[:, 13:14], AF.Ln)
                            s.drain()
                            s.activation(mv[:, 13:14], mv[:, 13:14], AF.Exp, scale=-0.5)
                            s.drain()
                            s.activation(mv[:, 12:13], mv[:, 11:12], AF.Identity,
                                         scale=mv[:, 13:14])
                            s.drain()
                            if t >= 1:
                                t1.wait(s, ck[("xnT", t - 1)])
                            s1.inc(s.activation(xn_tm[:], xb[p][:], AF.Identity,
                                                bias=mv[:, 12:13], scale=mv[:, 13:14]))
                            ck[("apply", t)] = s1.n

                    emit_loads(0)
                    emit_ln(0)

                    def emit_xnT(t):
                        @blk.tensor
                        def _(tt, t=t):
                            s1.wait(tt, ck[("apply", t)])
                            if t >= 1:
                                v1.wait(tt, ck[("xnfm", t - 1)])
                            for i in range(8):
                                mmt = tt.transpose(psXP[:, i * 128:(i + 1) * 128],
                                                   xn_tm[:, i * 128:(i + 1) * 128], s_identb[:])
                            t1.inc(mmt)
                            ck[("xnT", t)] = t1.n

                        @blk.vector
                        def _(v, t=t):
                            t1.wait(v, ck[("xnT", t)])
                            if t >= 1:
                                t1.wait(v, ck[("qkv", t - 1)])
                            v1.inc(v.tensor_copy(xn_fm[:], psXP[:]))
                            ck[("xnfm", t)] = v1.n

                    emit_xnT(0)

                    for t in range(NT):
                        p = t % 2

                        if t + 1 < NT:
                            emit_loads(t + 1)

                        @blk.tensor
                        def _(tt, t=t):
                            if t == 0:
                                cast_wait(tt, "wqkv")
                            v1.wait(tt, ck[("xnfm", t)])
                            if t >= 1:
                                s1.wait(tt, ck[("kscale", t - 1)])
                                v1.wait(tt, ck[("vev", t - 1)])
                            for ps, w in [(psQ, wq_sb), (psK, wk_sb), (psV, wv_sb)]:
                                for hlf in (0, 512):
                                    for dc in range(DC):
                                        mm = tt.matmul(ps[:, hlf:hlf + 512],
                                                       xn_fm[:, dc * 128:(dc + 1) * 128],
                                                       w[:, dc, hlf:hlf + 512],
                                                       start=(dc == 0), stop=(dc == DC - 1))
                            t1.inc(mm)
                            ck[("qkv", t)] = t1.n

                        if t >= 1:
                            emit_qk_T(t - 1)

                        if t + 1 < NT:
                            emit_ln(t + 1)
                            emit_xnT(t + 1)

                        @blk.vector
                        def _(v, t=t):
                            t1.wait(v, ck[("qkv", t)])
                            v1.inc(v.tensor_copy(
                                v_sb[:, :, t, :],
                                psV[:].rearrange("p (h d) -> p h d", h=H)))
                            ck[("vev", t)] = v1.n

                        @blk.scalar
                        def _(s, t=t):
                            t1.wait(s, ck[("qkv", t)])
                            if t >= 1:
                                v1.wait(s, ck[("kred", t - 1)])
                            s1.inc(s.activation(sqb[:], psQ[:], AF.Square))
                            ck[("qsq", t)] = s1.n

                        @blk.vector
                        def _(v, t=t):
                            s1.wait(v, ck[("qsq", t)])
                            v1.inc(v.reduce_sum(
                                msq[:], sqb[:].rearrange("p (h d) -> p h d", h=H), axis=AX))
                            ck[("qred", t)] = v1.n

                        @blk.scalar
                        def _(s, t=t):
                            v1.wait(s, ck[("qred", t)])
                            s1.inc(s.activation(sqb[:], psK[:], AF.Square))
                            ck[("ksq", t)] = s1.n

                        @blk.scalar
                        def _(s, t=t):
                            s.activation(rsq[:], msq[:], AF.Ln, bias=c_epsrms, scale=1.0 / DH)
                            s.drain()
                            s.activation(rsq[:], rsq[:], AF.Exp, scale=-0.5)
                            s.drain()
                            for h in range(H):
                                mm = s.activation(qkb_q[:, h * 128:(h + 1) * 128],
                                                  psQ[:, h * 128:(h + 1) * 128],
                                                  AF.Identity, scale=rsq[:, h:h + 1])
                            s1.inc(mm)
                            ck[("qscale", t)] = s1.n

                        @blk.vector
                        def _(v, t=t):
                            s1.wait(v, ck[("ksq", t)])
                            v1.inc(v.reduce_sum(
                                msq2[:], sqb[:].rearrange("p (h d) -> p h d", h=H), axis=AX))
                            ck[("kred", t)] = v1.n

                        def emit_rope(t, which, ro, qkb, scale_ck):
                            p = t % 2

                            @blk.vector
                            def _(v, t=t, which=which, ro=ro, qkb=qkb):
                                s1.wait(v, ck[(scale_ck, t)])
                                if t >= 1:
                                    t1.wait(v, ck[("qT" if which == "q" else "kT", t - 1)])
                                if which == "q" and t >= 1:
                                    yQs.wait(v, 16 * t)
                                rt = rt_q if which == "q" else rt_k
                                r3 = lambda ap: ap.rearrange("p (h d) -> p h d", h=H)
                                ev, od = r3(qkb[:])[:, :, 0:64], r3(qkb[:])[:, :, 64:128]
                                oe, oo = r3(ro[:])[:, :, 0:64], r3(ro[:])[:, :, 64:128]
                                te, to = r3(rt[:])[:, :, 0:64], r3(rt[:])[:, :, 64:128]
                                cosb = bass.AP(cb[p].ap().tensor, cb[p].ap().offset,
                                               [cb[p].ap().ap[0], [0, H], [1, 64]])
                                sinb = bass.AP(sb_[p].ap().tensor, sb_[p].ap().offset,
                                               [sb_[p].ap().ap[0], [0, H], [1, 64]])
                                v.tensor_tensor(te, ev, cosb, AluOpType.mult)
                                v.tensor_tensor(to, od, sinb, AluOpType.mult)
                                v.drain()
                                v.tensor_tensor(oe, te, to, AluOpType.subtract)
                                v.drain()
                                v.tensor_tensor(te, ev, sinb, AluOpType.mult)
                                v.tensor_tensor(to, od, cosb, AluOpType.mult)
                                v.drain()
                                v1.inc(v.tensor_tensor(oo, te, to, AluOpType.add))
                                ck[("rope", t, which)] = v1.n
                                if which == "k":
                                    ck[("ropek", t)] = v1.n

                        emit_rope(t, "q", q_rope, qkb_q, "qscale")

                        @blk.scalar
                        def _(s, t=t):
                            v1.wait(s, ck[("kred", t)])
                            s.activation(rsq2[:], msq2[:], AF.Ln, bias=c_epsrms, scale=1.0 / DH)
                            s.drain()
                            s.activation(rsq2[:], rsq2[:], AF.Exp, scale=-0.5)
                            s.drain()
                            for h in range(H):
                                mm = s.activation(qkb_k[:, h * 128:(h + 1) * 128],
                                                  psK[:, h * 128:(h + 1) * 128],
                                                  AF.Identity, scale=rsq2[:, h:h + 1])
                            s1.inc(mm)
                            ck[("kscale", t)] = s1.n

                        emit_rope(t, "k", k_rope, qkb_k, "kscale")

                    emit_qk_T(NT - 1)


            v2, s2, t2 = sems["v2"], sems["s2"], sems["t2"]
            yQq = sems["yQq"]
            yAt = [sems["yAt0"], sems["yAt1"]]
            with ExitStack() as p2:
                s_masks = sbuf("s_masks", [128, 4, 512], BF16, p2)
                wo_sb = sbuf("wo_sb", [128, H, DC, 128], BF16, p2)
                qq = sbuf("qq", [128, 4, H, 128], BF16, p2)
                pt_sl = [sbuf(f"pt{i}", [128, 1024], BF16, p2) for i in range(2)]
                ptm = sbuf("ptm", [128, 4, 512], BF16, p2)
                sden = [sbuf(f"sden{i}", [1, 512], F32, p2) for i in range(2)]
                srden = [sbuf(f"srden{i}", [1, 512], F32, p2) for i in range(2)]
                sdb = [sbuf(f"sdb{i}", [128, 512], F32, p2) for i in range(2)]
                o_all = sbuf("o_all", [128, H, 512], BF16, p2)
                sattn = [sbuf(f"sattn{i}", [128, 512], F32, p2) for i in range(2)]
                psS = [psum(f"psS{i}", [128, 1024], F32, p2) for i in range(2)]
                psO = [psum(f"psO{i}", [128, 512], F32, p2) for i in range(2)]
                psDD = [psum(f"psDD{i}", [128, 512], F32, p2) for i in range(2)]
                psA = psS[0][:, 0:512]

                ck2 = {"exp_last": {}, "pv_last": {}, "onorm_p": {}, "wodone": 0}

                with nc.Block("p2") as blk:
                    @blk.gpsimd
                    def _(g):
                        cast_sems["wo"].inc(g.dma_start(
                            wo_sb[:],
                            bass.AP(wo_in.tensor, 0,
                                    [[1024, 128], [131072, H], [128, DC], [1, 128]])), 16)
                        gz = cast_sems["gz"]
                        nblk = 8 * (T + 512) // 512
                        for b_ in range(nblk):
                            gz.inc(g.dma_start(
                                gathd[b_ * 512:(b_ + 1) * 512, :], gzt_in[:]), 16)
                        for sub in range(4):
                            gz.inc(g.dma_start(
                                x1s[T + sub * 128:T + (sub + 1) * 128, :],
                                zrow_in[:]), 16)

                    @blk.scalar
                    def _(s):
                        s.add_instruction(mybir.InstLoadActFuncSet(
                            name=nc.get_next_instruction_name(),
                            act_func_set_id=LNEXP_SET, ins=[], outs=[]))

                    @blk.sync
                    def _(sy):
                        yQs.wait(sy)
                        yLd[0].wait(sy)
                        yLd[1].wait(sy)
                        c0.inc(sy.dma_start(s_masks[:], masks_in[:]), 16)
                        c0.inc(sy.dma_start(s_ones1f[:], ones1f_in[:]), 16)

                    for qg in range(NQ):
                        JC = 4 * (qg + 1)
                        nbatch = (JC + 1) // 2

                        @blk.sync
                        def _(sy, qg=qg):
                            if qg >= 1:
                                t2.wait(sy, ck2[("qq_free", qg - 1)])
                            for i in range(4):
                                yQq.inc(sy.dma_start(qq[:, i, :, :], qs[4 * qg + i]), 16)

                        for h in range(H):
                            hp = h % 2
                            exp_ck, pv_ck = {}, {}
                            batches = [list(range(2 * b, min(2 * b + 2, JC)))
                                       for b in range(nbatch)]

                            def emit_st(b, qg=qg, h=h):
                                jcs = batches[b]

                                @blk.tensor
                                def _(tt):
                                    if b == 0 and h == 0:
                                        yQq.wait(tt)
                                    if (b - 2) in exp_ck:
                                        s2.wait(tt, exp_ck[b - 2])
                                    elif b < 2 and (b % 2) in ck2["exp_last"]:
                                        s2.wait(tt, ck2["exp_last"][b % 2])
                                    for i, jc in enumerate(jcs):
                                        mm = tt.matmul(psS[b % 2][:, i * 512:(i + 1) * 512],
                                                       k_sb[:, h, jc, :], qq[:, :, h, :],
                                                       start=True, stop=True)
                                    t2.inc(mm)
                                    ck2[("st", b)] = t2.n

                            def emit_exp(b, qg=qg, h=h):
                                jcs = batches[b]

                                @blk.scalar
                                def _(s):
                                    t2.wait(s, ck2[("st", b)])
                                    if (b - 2) in pv_ck:
                                        t2.wait(s, pv_ck[b - 2])
                                    elif b < 2 and (b % 2) in ck2["pv_last"]:
                                        t2.wait(s, ck2["pv_last"][b % 2])
                                    s2.inc(s.activation(pt_sl[b % 2][:, 0:len(jcs) * 512],
                                                        psS[b % 2][:, 0:len(jcs) * 512],
                                                        AF.Exp, scale=1.0 / DH))
                                    exp_ck[b] = s2.n
                                    ck2["exp_last"][b % 2] = s2.n

                                diag = [i for i, jc in enumerate(jcs) if jc >= 4 * qg]
                                if diag:
                                    @blk.vector
                                    def _(v):
                                        if qg == 0 and h == 0:
                                            c0.wait(v)
                                        s2.wait(v, exp_ck[b])
                                        for i in diag:
                                            kk = jcs[i] - 4 * qg
                                            mm = v.tensor_tensor(
                                                ptm[:, kk, :],
                                                pt_sl[b % 2][:, i * 512:(i + 1) * 512],
                                                s_masks[:, kk, :], AluOpType.mult)
                                        v2.inc(mm)
                                        ck2[("mask", b)] = v2.n

                            def emit_pv(b, qg=qg, h=h, hp=hp):
                                jcs = batches[b]

                                @blk.tensor
                                def _(tt):
                                    s2.wait(tt, exp_ck[b])
                                    if any(jc >= 4 * qg for jc in jcs):
                                        v2.wait(tt, ck2[("mask", b)])
                                    if b == 0 and (hp in ck2["onorm_p"]):
                                        v2.wait(tt, ck2["onorm_p"][hp])
                                    for i, jc in enumerate(jcs):
                                        src = (ptm[:, jc - 4 * qg, :] if jc >= 4 * qg
                                               else pt_sl[b % 2][:, i * 512:(i + 1) * 512])
                                        tt.matmul(psO[hp][:], v_sb[:, h, jc, :], src,
                                                  start=(jc == 0), stop=(jc == JC - 1))
                                        mm = tt.matmul(psDD[hp][0:1, :], s_onescb[:], src,
                                                       start=(jc == 0), stop=(jc == JC - 1))
                                    t2.inc(mm)
                                    pv_ck[b] = t2.n
                                    ck2["pv_last"][b % 2] = t2.n

                            emit_st(0)
                            emit_exp(0)
                            for b in range(1, nbatch):
                                emit_st(b)
                                emit_pv(b - 1)
                                emit_exp(b)
                            emit_pv(nbatch - 1)

                            @blk.vector
                            def _(v, qg=qg, h=h, hp=hp, pv_ck=pv_ck, nbatch=nbatch):
                                t2.wait(v, pv_ck[nbatch - 1])
                                if qg >= 1 and h == 0:
                                    t2.wait(v, ck2["wodone"])
                                v.tensor_copy(sden[hp][:], psDD[hp][0:1, :])
                                v.drain()
                                v2.inc(v.reciprocal(srden[hp][:], sden[hp][:]))

                            @blk.tensor
                            def _(tt, qg=qg, h=h, hp=hp):
                                if qg == 0 and h == 0:
                                    c0.wait(tt)
                                v2.wait(tt)
                                t2.inc(tt.matmul(psDD[hp][:], s_ones1f[:], srden[hp][:],
                                                 start=True, stop=True))

                            @blk.vector
                            def _(v, qg=qg, h=h, hp=hp):
                                t2.wait(v)
                                v.tensor_copy(sdb[hp][:], psDD[hp][:])
                                v.drain()
                                v2.inc(v.tensor_tensor(o_all[:, h, :], psO[hp][:], sdb[hp][:],
                                                       AluOpType.mult))
                                ck2["onorm_p"][hp] = v2.n
                                if h == H - 1:
                                    ck2[("heads_done", qg)] = v2.n

                        ck2[("qq_free", qg)] = t2.n

                        for mc in range(DC):
                            @blk.tensor
                            def _(tt, qg=qg, mc=mc):
                                if qg == 0 and mc == 0:
                                    cast_wait(tt, "wo")
                                v2.wait(tt, ck2[("heads_done", qg)])
                                if ("attn_ev", qg, mc - 1) in ck2:
                                    v2.wait(tt, ck2[("attn_ev", qg, mc - 1)])
                                elif qg >= 1 and mc == 0:
                                    v2.wait(tt, ck2[("attn_ev", qg - 1, DC - 1)])
                                for h in range(H):
                                    mm = tt.matmul(psA, wo_sb[:, h, mc, :], o_all[:, h, :],
                                                   start=(h == 0), stop=(h == H - 1))
                                t2.inc(mm)
                                if mc == DC - 1:
                                    ck2["wodone"] = t2.n

                            @blk.vector
                            def _(v, qg=qg, mc=mc):
                                t2.wait(v)
                                k_st = qg * DC + mc
                                if k_st >= 2:
                                    yAt[mc % 2].wait(v, 16 * (k_st // 2))
                                v2.inc(v.tensor_copy(sattn[mc % 2][:], psA))
                                ck2[("attn_ev", qg, mc)] = v2.n

                            @blk.sync
                            def _(sy, qg=qg, mc=mc):
                                v2.wait(sy, ck2[("attn_ev", qg, mc)])
                                yAt[mc % 2].inc(sy.dma_start(attnscr[qg, mc], sattn[mc % 2][:]), 16)

            v25, s25, t25 = sems["v25"], sems["s25"], sems["t25"]
            yL25 = [sems["yL25a"], sems["yL25b"]]
            yX1 = sems["yX1"]
            yXn2 = sems["yXn2"]
            yRw = sems["yRw"]
            yGs = sems["yGs"]
            with ExitStack() as p25:
                xb2 = [sbuf(f"xb2_{i}", [128, DM], F32, p25) for i in range(2)]
                afm = [sbuf(f"afm{i}", [128, DC, 128], F32, p25) for i in range(2)]
                x1tm = sbuf("x1tm", [128, DM], F32, p25)
                xn2tm = sbuf("xn2tm", [128, DM], BF16, p25)
                xn2st = [sbuf(f"xn2st{i}", [128, DC, 512], BF16, p25) for i in range(2)]
                mv2 = sbuf("mv2", [128, 14], F32, p25)
                sxp = sbuf("sxp", [128, 8], F32, p25)
                sume = sbuf("sume", [128, 1], F32, p25)
                rsum = sbuf("rsum", [128, 1], F32, p25)
                rsc = sbuf("rsc", [128, 8], F32, p25)
                bsd = sbuf("bsd", [128, 8], F32, p25)
                m1c = sbuf("m1c", [128, 1], F32, p25)
                mask1 = sbuf("mask1", [128, 8], F32, p25)
                b2d = sbuf("b2d", [128, 8], F32, p25)
                m2c = sbuf("m2c", [128, 1], F32, p25)
                mask2 = sbuf("mask2", [128, 8], F32, p25)
                mask1i = sbuf("mask1i", [128, 8], I32, p25)
                rwbf = sbuf("rwbf", [128, 8], BF16, p25)
                rw_st = sbuf("rw_st", [8, 128], BF16, p25)
                selb = sbuf("selb", [128, 8], BF16, p25)
                base_b = sbuf("base_b", [128, 8], F32, p25)
                a_s = sbuf("a_s", [128, 8], F32, p25)
                idxf = sbuf("idxf", [128, 8], F32, p25)
                idxu = sbuf("idxu", [128, 8], U32, p25)
                totrow = sbuf("totrow", [1, 8], F32, p25)
                ttile = sbuf("ttile", [1, 8], F32, p25)
                stage = sbuf("stage", [128, 1040], BF16, p25)
                psT25 = psum("psT25", [128, DM], F32, p25)
                psX2 = psum("psX2", [128, DM], BF16, p25)
                psG = psum("psG", [128, 512], F32, p25)
                psRw = psum("psRw", [8, 512], BF16, p25)
                psTot = psum("psTot", [1, 512], F32, p25)

                ck25 = {}

                with nc.Block("p25") as blk:
                    @blk.vector
                    def _(v):
                        v.memset(stage[:], 0.0)
                        v25.inc(v.memset(totrow[:], 0.0))
                        ck25[("tot_snap", -1)] = v25.n

                    @blk.sync
                    def _(sy):
                        yAt[0].wait(sy)
                        yAt[1].wait(sy)
                        yQq.wait(sy)

                    for t in range(NT):
                        p = t % 2
                        qg, off = t // 4, (t % 4) * 128

                        @blk.sync
                        def _(sy, t=t, p=p, qg=qg, off=off):
                            if t >= 2:
                                v25.wait(sy, ck25[("vfin", t - 2)])
                            yL25[p].inc(sy.dma_start(xb2[p][:], x_in[t * 128:(t + 1) * 128, :]), 16)
                            yL25[p].inc(sy.dma_start(
                                afm[p][:],
                                bass.AP(attnscr.tensor, qg * DC * 128 * 512 + off,
                                        [[512, 128], [128 * 512, DC], [1, 128]])), 16)
                            ck25[("ld", t)] = yL25[p].n

                        @blk.tensor
                        def _(tt, t=t, p=p):
                            yL25[p].wait(tt, ck25[("ld", t)])
                            if t >= 1:
                                v25.wait(tt, ck25[("x1add", t - 1)])
                            for mc in range(DC):
                                mm = tt.transpose(psT25[:, mc * 128:(mc + 1) * 128],
                                                  afm[p][:, mc, :], s_identf[:])
                            t25.inc(mm)

                        @blk.vector
                        def _(v, t=t, p=p):
                            t25.wait(v)
                            if t >= 1:
                                yX1.wait(v, 16 * t)
                            v25.inc(v.tensor_tensor(x1tm[:], psT25[:], xb2[p][:], AluOpType.add))
                            ck25[("x1add", t)] = v25.n

                        @blk.sync
                        def _(sy, t=t):
                            v25.wait(sy, ck25[("x1add", t)])
                            yX1.inc(sy.dma_start(x1s[t * 128:(t + 1) * 128, :], x1tm[:]), 16)

                        @blk.vector
                        def _(v, t=t):
                            v.drain()
                            v.bn_stats(mv2[:, 0:6], x1tm[:, 0:512])
                            v.bn_stats(mv2[:, 6:12], x1tm[:, 512:1024])
                            v.drain()
                            v.bn_aggr(mv2[:, 12:14], mv2[:, 0:12])
                            v.drain()
                            v.tensor_scalar(mv2[:, 13:14], mv2[:, 13:14],
                                            c_epsln, None, AluOpType.add)
                            v.drain()
                            v25.inc(v.tensor_scalar(mv2[:, 11:12], mv2[:, 12:13], c_neg1,
                                                    None, AluOpType.mult))

                        @blk.scalar
                        def _(s, t=t):
                            v25.wait(s)
                            s.activation(mv2[:, 13:14], mv2[:, 13:14], AF.Ln)
                            s.drain()
                            s.activation(mv2[:, 13:14], mv2[:, 13:14], AF.Exp, scale=-0.5)
                            s.drain()
                            s.activation(mv2[:, 12:13], mv2[:, 11:12], AF.Identity,
                                         scale=mv2[:, 13:14])
                            s.drain()
                            if t >= 1:
                                t25.wait(s, ck25[("x2t", t - 1)])
                            s25.inc(s.activation(xn2tm[:], x1tm[:], AF.Identity,
                                                 bias=mv2[:, 12:13], scale=mv2[:, 13:14]))
                            ck25[("apply2", t)] = s25.n

                        @blk.tensor
                        def _(tt, t=t):
                            s25.wait(tt, ck25[("apply2", t)])
                            if t >= 1:
                                v25.wait(tt, ck25[("x2ev", t - 1)])
                            for i in range(DC):
                                mm = tt.transpose(psX2[:, i * 128:(i + 1) * 128],
                                                  xn2tm[:, i * 128:(i + 1) * 128], s_identb[:])
                            t25.inc(mm)
                            ck25[("x2t", t)] = t25.n

                        @blk.vector
                        def _(v, t=t, qg=qg, off=off):
                            t25.wait(v, ck25[("x2t", t)])
                            if t % 4 == 0 and qg >= 2:
                                yXn2.wait(v, 16 * qg)
                                t25.wait(v, ck25[("gmm", 4 * (qg - 2) + 3)])
                            v25.inc(v.tensor_copy(
                                xn2st[qg % 2][:, :, off:off + 128],
                                psX2[:].rearrange("p (c d) -> p c d", c=DC)))
                            ck25[("x2ev", t)] = v25.n

                        @blk.sync
                        def _(sy, t=t, qg=qg):
                            if t % 4 == 3:
                                v25.wait(sy, ck25[("x2ev", t)])
                                yXn2.inc(sy.dma_start(xn2s[qg], xn2st[qg % 2][:]), 16)

                        @blk.tensor
                        def _(tt, t=t, qg=qg, off=off):
                            if t == 0:
                                cast_wait(tt, "gate")
                            v25.wait(tt, ck25[("x2ev", t)])
                            if t >= 1:
                                s25.wait(tt, ck25[("gexp", t - 1)])
                            for dc in range(DC):
                                mm = tt.matmul(psG[:, 0:8],
                                               xn2st[qg % 2][:, dc, off:off + 128],
                                               s_gwt[:, dc, :],
                                               start=(dc == 0), stop=(dc == DC - 1))
                            t25.inc(mm)
                            ck25[("gmm", t)] = t25.n

                        @blk.vector
                        def _(v, t=t):
                            t25.wait(v, ck25[("gmm", t)])
                            v25.inc(v.reduce_max(m1c[:], psG[:, 0:8], axis=AX, negate=True))

                        @blk.scalar
                        def _(s, t=t):
                            v25.wait(s)
                            s25.inc(s.activation(sxp[:], psG[:, 0:8], AF.Exp,
                                                 bias=m1c[:], accum_out=sume[:]))
                            ck25[("gexp", t)] = s25.n

                        @blk.vector
                        def _(v, t=t):
                            s25.wait(v, ck25[("gexp", t)])
                            v.reciprocal(rsum[:], sume[:])
                            v.drain()
                            v.tensor_scalar(rsc[:], sxp[:], rsum[:], None, AluOpType.mult)
                            v.drain()
                            v.tensor_tensor(bsd[:], rsc[:], s_gbb[:], AluOpType.add)
                            v.drain()
                            v.reduce_max(m1c[:], bsd[:], axis=AX)
                            v.drain()
                            v.tensor_scalar(mask1[:], bsd[:], m1c[:], None, AluOpType.is_ge)
                            v.drain()
                            v.scalar_tensor_tensor(b2d[:], mask1[:], c_negbig, bsd[:],
                                                   AluOpType.mult, AluOpType.add)
                            v.drain()
                            v.reduce_max(m2c[:], b2d[:], axis=AX)
                            v.drain()
                            v.tensor_scalar(mask2[:], b2d[:], m2c[:], None, AluOpType.is_ge)
                            v.drain()
                            v.tensor_tensor(mask1[:], mask1[:], mask2[:], AluOpType.add)
                            v.drain()
                            v25.inc(v.tensor_tensor(rwbf[:], rsc[:], mask1[:], AluOpType.mult))

                        @blk.tensor
                        def _(tt, t=t):
                            v25.wait(tt)
                            if t >= 1:
                                v25.wait(tt, ck25[("rwev", t - 1)])
                            t25.inc(tt.transpose(psRw[:, 0:128], rwbf[:], s_identb[:]))

                        @blk.vector
                        def _(v, t=t):
                            t25.wait(v)
                            if t >= 1:
                                yRw.wait(v, 16 * t)
                            v25.inc(v.tensor_copy(rw_st[:], psRw[:, 0:128]))
                            ck25[("rwev", t)] = v25.n

                        @blk.sync
                        def _(sy, t=t):
                            v25.wait(sy, ck25[("rwev", t)])
                            yRw.inc(sy.dma_start(
                                bass.AP(rws.tensor, t * 128, [[T, 8], [1, 128]]),
                                rw_st[:]), 16)

                        @blk.vector
                        def _(v, t=t):
                            v.drain()
                            v25.inc(v.tensor_copy(selb[:], mask1[:]))
                            ck25[("selb", t)] = v25.n

                        @blk.tensor
                        def _(tt, t=t):
                            v25.wait(tt, ck25[("selb", t)])
                            v25.wait(tt, ck25[("tot_snap", t - 1)])
                            tt.matmul(psG[:, 16:24], s_ones1f[:], totrow[:],
                                      start=True, stop=True)
                            tt.matmul(psG[:, 8:16], s_triu[:], selb[:],
                                      start=True, stop=True)
                            mmt = tt.matmul(psTot[0:1, 0:8], s_onescb[:], selb[:],
                                            start=True, stop=True)
                            t25.inc(mmt)
                            ck25[("preMM", t)] = t25.n

                        @blk.vector
                        def _(v, t=t):
                            t25.wait(v, ck25[("preMM", t)])
                            if t >= 1:
                                yGs.wait(v, 128 * t)
                            v.tensor_copy(base_b[:], psG[:, 16:24])
                            v.drain()
                            v.tensor_tensor(a_s[:], psG[:, 8:16], base_b[:], AluOpType.add)
                            v.drain()
                            v.tensor_copy(mask1i[:], mask1[:])
                            v.drain()
                            v.select(idxf[:], mask1i[:], a_s[:], s_bigc[:], add_drain=True)
                            v.drain()
                            v.tensor_tensor(idxf[:], idxf[:], mask1[:], AluOpType.subtract)
                            v.drain()
                            v.tensor_tensor(idxf[:], idxf[:], s_eofs[:], AluOpType.add)
                            v.drain()
                            v.tensor_copy(idxu[:], idxf[:])
                            v.drain()
                            v.tensor_copy(ttile[:], psTot[0:1, 0:8])
                            v.drain()
                            v.tensor_tensor(totrow[:], totrow[:], ttile[:], AluOpType.add)
                            v.drain()
                            v.tensor_copy(stage[:, 0:1024], xn2tm[:])
                            v.tensor_copy(stage[:, 1024:1032], rwbf[:])
                            v.memset(stage[:, 1032:1033], float(t))
                            v.tensor_copy(stage[:, 1033:1034], s_pidx[:])
                            v25.inc(v.drain())
                            ck25[("tot_snap", t)] = v25.n
                            ck25[("stage", t)] = v25.n

                        @blk.gpsimd
                        def _(g, t=t):
                            if t == 0:
                                gz_ = cast_sems["gz"]
                                g.wait_ge(gz_.h, gz_.n)
                            v25.wait(g, ck25[("stage", t)])
                            with g.register(f"bnd{t}") as rb:
                                g.reg_mov(rb, 8 * GR - 1)
                                for e in range(8):
                                    yGs.inc(g.indirect_dma_start(
                                        gathd[:],
                                        bass.IndirectOffsetOnAxis(ap=idxu[:, e:e + 1], axis=0),
                                        stage[:], None,
                                        bounds_check=rb, oob_is_err=False), 16)

                        ck25[("vfin", t)] = v25.n

                    @blk.vector
                    def _(v):
                        v.drain()
                        v25.inc(v.tensor_copy(s_toti[:], totrow[:]))

        v3, s3, t3 = sems["v3"], sems["s3"], sems["t3"]
        yW13 = [sems["yW13a"], sems["yW13b"]]
        yW2 = sems["yW2a"]
        yGb = [sems["yW2b"], sems["yW2c"]]
        yXn2f = sems["yXn2f"]
        yX1t = sems["yX1t"]
        yOut = sems["yOut"]
        yRwF = sems["yRwF"]
        gsc3p = [sems["gsc3"], sems["gsc3b"]]
        GR_ = T + 512
        NCH = T // 512
        with ExitStack() as p3:
            w13e = [sbuf(f"w13e{i}", [128, 6 * 2048], BF16, p3) for i in range(2)]
            w2e = sbuf("w2e", [128, HC, DC, 128], BF16, p3)
            gbuf = [sbuf(f"gbuf{i}", [128, 4, 1040], BF16, p3) for i in range(2)]
            gfm = [sbuf(f"gfm{i}", [128, DC, 512], BF16, p3) for i in range(2)]
            hgb = [sbuf(f"hgb{i}", [128, HC, 512], BF16, p3) for i in range(2)]
            ydst = [sbuf(f"ydst{i}", [128, 4, DM], F32, p3) for i in range(2)]
            rwfc = [sbuf(f"rwfc{i}", [128, 4], F32, p3) for i in range(2)]
            yidxf = [sbuf(f"yidxf{i}", [128, 4], F32, p3) for i in range(2)]
            yidxu = [sbuf(f"yidxu{i}", [128, 4], U32, p3) for i in range(2)]
            ddum = sbuf("ddum", [128, 8 * NCH + 8], BF16, p3)
            ddumg = sbuf("ddumg", [128, 4 * (8 * NCH + 8)], BF16, p3)
            xn2f = sbuf("xn2f", [128, DC, 512], BF16, p3)
            x1t = sbuf("x1t", [128, 4, DM], F32, p3)
            rwT_f = sbuf("rwT_f", [8, 512], BF16, p3)
            s_b2 = sbuf("s_b2", [8, DM], BF16, p3)
            sA = [sbuf(f"sA{i}", [128, 512], F32, p3) for i in range(2)]
            sProd = sbuf("sProd", [128, 4, 512], F32, p3)
            yf = [sbuf(f"yf{i}", [128, 512], F32, p3) for i in range(2)]
            ost = sbuf("ost", [128, 4, 128], F32, p3)
            psA3 = [psum(f"psA3{i}", [128, 512], F32, p3) for i in range(2)]
            psB3 = [psum(f"psB3{i}", [128, 512], F32, p3) for i in range(2)]
            psX3 = psum("psX3", [128, 1024], BF16, p3)
            psY = [psum(f"psY{i}", [128, 512], F32, p3) for i in range(2)]
            psYT = psum("psYT", [128, 512], F32, p3)

            ck3 = {}
            SGRP = [(0, 4), (4, 8), (8, HC)]

            def w13_ap(e, half):
                h0, h1 = (0, 6) if half == 0 else (6, HC)
                base = (e * HC + h0) * (128 * 2048)
                return bass.AP(w13_in.tensor, base,
                               [[2048, 128], [128 * 2048, h1 - h0], [1, 2048]])

            with nc.Block("p3") as blk:
                @blk.gpsimd
                def _(g):
                    yGs.wait(g)
                    yX1.wait(g)
                    yXn2.wait(g)
                    yRw.wait(g)
                    yL25[0].wait(g)
                    yL25[1].wait(g)
                    cast_sems["b2"].inc(g.dma_start(s_b2[:], b2_in[:]), 16)

                @blk.scalar
                def _(s):
                    s.add_instruction(mybir.InstLoadActFuncSet(
                        name=nc.get_next_instruction_name(),
                        act_func_set_id=SILU_SET, ins=[], outs=[]))
                    yGs.wait(s)
                    yX1.wait(s)
                    yXn2.wait(s)
                    yRw.wait(s)
                    yL25[0].wait(s)
                    yL25[1].wait(s)

                @blk.sync
                def _(sy):
                    yX1.wait(sy)
                    yXn2.wait(sy)
                    yRw.wait(sy)
                    yGs.wait(sy)
                    yL25[0].wait(sy)
                    yL25[1].wait(sy)

                TI, VI, SI = 31, 25, 11
                creg, treg = {}, {}
                for di, dec in enumerate([blk.sync, blk.scalar, blk.vector,
                                          blk.tensor, blk.gpsimd]):
                    @dec
                    def _(eng, di=di):
                        v25 = sems["v25"]
                        eng.wait_ge(v25.h, v25.n)
                        regs = [eng.alloc_register(name=f"cnt{di}_{k}")
                                for k in range(8)]
                        eng.reg_load(regs, s_toti[0:1, 0:8])
                        creg[id(eng)] = regs
                        treg[id(eng)] = eng.alloc_register(name=f"tmp{di}")

                def cond(dec, e, c, fn, fn_else=None):
                    @dec
                    def _(eng, e=e, c=c, fn=fn, fn_else=fn_else):
                        snap = {s_: s_.n for s_ in sems.values()}
                        rc = treg[id(eng)]
                        eng.reg_mov(rc, 512 * c)
                        with eng.If_lt(rc, creg[id(eng)][e]):
                            fn(eng)
                        with eng.Else():
                            if fn_else is not None:
                                fn_else(eng)
                            else:
                                eng.drain()
                                for s_ in sems.values():
                                    d = s_.n - snap[s_]
                                    if d:
                                        eng.sem_inc(s_.h, d)

                rix = [0]
                for e in range(8):
                    for half in range(2):
                        @blk.gpsimd
                        def _(g, e=e, half=half):
                            if e >= 1:
                                t3.wait(g, ck3[("w13use", e - 1, half)])
                            hn = 6 if half == 0 else 5
                            yW13[half].inc(g.dma_start(
                                w13e[half][:, 0:hn * 2048], w13_ap(e, half)), 16)
                            ck3[("w13ld", e, half)] = yW13[half].n

                    @blk.gpsimd
                    def _(g, e=e):
                        if e >= 1:
                            t3.wait(g, ck3[("w2use", e - 1)])
                        yW2.inc(g.dma_start(w2e[:], w2_in[e]), 16)
                        ck3[("w2ld", e)] = yW2.n

                    for c in range(NCH):
                        r = rix[0]
                        rix[0] += 1
                        cp = c % 2
                        T0, V0, S0 = t3.n, v3.n, s3.n

                        def f_sync(eng, e=e, c=c, cp=cp, r=r, T0=T0, V0=V0):
                            if r >= 2:
                                eng.wait_ge(t3.h, T0 - 2 * TI + 4)
                                eng.wait_ge(v3.h, V0 - 2 * VI + 5)
                            yGb[cp].inc(eng.dma_start(
                                gbuf[cp][:],
                                bass.AP(gathd.tensor,
                                        (e * GR_ + c * 512) * 1040,
                                        [[1040, 128], [128 * 1040, 4], [1, 1040]])), 16)
                        def f_sync_else(eng, cp=cp, r=r):
                            eng.dma_start(ddum[0:1, r:r + 1], gathd[0:1, 0:1]
                                          ).then_inc(yGb[cp].h, 16)
                        cond(blk.sync, e, c, f_sync, fn_else=f_sync_else)
                        gld = yGb[cp].n

                        def f_T(eng, e=e, c=c, cp=cp, r=r, T0=T0, V0=V0, S0=S0, gld=gld):
                            for sub in range(4):
                                if sub == 0:
                                    eng.wait_ge(yGb[cp].h, gld)
                                if sub >= 1:
                                    eng.wait_ge(v3.h, V0 + sub)
                                elif r >= 1:
                                    eng.wait_ge(v3.h, V0 - VI + 4)
                                for i in range(8):
                                    mm = eng.transpose(
                                        psX3[:, i * 128:(i + 1) * 128],
                                        gbuf[cp][:, sub, i * 128:(i + 1) * 128],
                                        s_identb[:])
                                t3.inc(mm)
                            eng.wait_ge(v3.h, V0 + 4)
                            if c == 0:
                                eng.wait_ge(yW13[0].h, ck3[("w13ld", e, 0)])
                            for hc in range(HC):
                                hp = hc % 2
                                half = 0 if hc < 6 else 1
                                h0 = 0 if hc < 6 else 6
                                if c == 0 and hc == 6:
                                    eng.wait_ge(yW13[1].h, ck3[("w13ld", e, 1)])
                                if hc >= 2:
                                    eng.wait_ge(v3.h, V0 + 4 + hc)
                                elif r >= 1:
                                    eng.wait_ge(v3.h, V0 - 9 if hp == 0 else V0 - 10)
                                base = (hc - h0) * 2048
                                for dc in range(DC):
                                    eng.matmul(psA3[hp][:],
                                               w13e[half][:, base + dc * 128:base + dc * 128 + 128],
                                               gfm[cp][:, dc, :],
                                               start=(dc == 0), stop=(dc == DC - 1))
                                for dc in range(DC):
                                    mm = eng.matmul(psB3[hp][:],
                                                    w13e[half][:, base + 1024 + dc * 128:base + 1024 + dc * 128 + 128],
                                                    gfm[cp][:, dc, :],
                                                    start=(dc == 0), stop=(dc == DC - 1))
                                t3.inc(mm)
                            for mc in range(DC):
                                mp = mc % 2
                                if mc == 0:
                                    eng.wait_ge(s3.h, S0 + 3)
                                    eng.wait_ge(yW2.h, ck3[("w2ld", e)])
                                if mc >= 2:
                                    eng.wait_ge(s3.h, S0 + 2 + mc)
                                elif r >= 1:
                                    eng.wait_ge(s3.h, S0 - 1 if mp == 0 else S0)
                                for hc in range(HC):
                                    mm = eng.matmul(psY[mp][:], w2e[:, hc, mc, :],
                                                    hgb[cp][:, hc, :],
                                                    start=(hc == 0), stop=(hc == HC - 1))
                                t3.inc(mm)
                                eng.wait_ge(s3.h, S0 + 4 + mc)
                                if mc >= 1:
                                    eng.wait_ge(v3.h, V0 + 16 + mc)
                                elif r >= 1:
                                    eng.wait_ge(v3.h, V0 - 1)
                                for sub in range(4):
                                    mm = eng.transpose(
                                        psYT[:, sub * 128:(sub + 1) * 128],
                                        yf[mp][:, sub * 128:(sub + 1) * 128],
                                        s_identf[:])
                                t3.inc(mm)
                        cond(blk.tensor, e, c, f_T)

                        def f_V(eng, e=e, c=c, cp=cp, r=r, T0=T0, V0=V0, S0=S0):
                            for sub in range(4):
                                eng.wait_ge(t3.h, T0 + 1 + sub)
                                mm = eng.tensor_copy(
                                    bass.AP(gfm[cp].ap().tensor,
                                            gfm[cp].ap().offset + sub * 128,
                                            [gfm[cp].ap().ap[0], [512, DC], [1, 128]]),
                                    psX3[:].rearrange("p (d t) -> p d t", d=DC))
                                v3.inc(mm)
                            if r >= 2:
                                eng.wait_ge(gsc3p[cp].h, 64 * (r // 2))
                            eng.tensor_copy(rwfc[cp][:], gbuf[cp][:, :, 1024 + e])
                            eng.drain()
                            eng.scalar_tensor_tensor(
                                yidxf[cp][:], gbuf[cp][:, :, 1032], c_128,
                                gbuf[cp][:, :, 1033], AluOpType.mult, AluOpType.add)
                            eng.drain()
                            mm = eng.tensor_copy(yidxu[cp][:], yidxf[cp][:])
                            v3.inc(mm)
                            for hc in range(HC):
                                hp = hc % 2
                                slot = hc % 4
                                grp = hc // 4
                                eng.wait_ge(t3.h, T0 + 5 + hc)
                                if grp >= 1:
                                    eng.wait_ge(s3.h, S0 + grp)
                                elif r >= 1:
                                    eng.wait_ge(s3.h, S0 - 8)
                                eng.tensor_scalar(sA[hp][:], psA3[hp][:],
                                                  s_b13[:, e, hc, 0:1], None, AluOpType.add)
                                eng.drain()
                                mm = eng.scalar_tensor_tensor(
                                    sProd[:, slot, :], psB3[hp][:], s_b13[:, e, hc, 1:2],
                                    sA[hp][:], AluOpType.add, AluOpType.mult)
                                v3.inc(mm)
                            for mc in range(DC):
                                eng.wait_ge(t3.h, T0 + 17 + 2 * mc)
                                mm = eng.tensor_copy(
                                    bass.AP(ydst[cp].ap().tensor,
                                            ydst[cp].ap().offset + mc * 128,
                                            [ydst[cp].ap().ap[0], [DM, 4], [1, 128]]),
                                    psYT[:].rearrange("p (s d) -> p s d", s=4))
                                v3.inc(mm)
                            eng.drain()
                            for sub in range(4):
                                mm = eng.tensor_scalar(ydst[cp][:, sub, :],
                                                       ydst[cp][:, sub, :],
                                                       rwfc[cp][:, sub:sub + 1], None,
                                                       AluOpType.mult)
                            eng.drain()
                            v3.inc(mm)
                        cond(blk.vector, e, c, f_V)

                        def f_S(eng, e=e, c=c, cp=cp, r=r, T0=T0, V0=V0, S0=S0):
                            for gi, (g0_, g1_) in enumerate(SGRP):
                                eng.wait_ge(v3.h, V0 + 6 + (g1_ - 1))
                                ng = g1_ - g0_
                                mm = eng.activation(
                                    hgb[cp][:, g0_:g1_, :].rearrange("p g d -> p (g d)"),
                                    sProd[:, 0:ng, :].rearrange("p g d -> p (g d)"),
                                    AF.Silu)
                                s3.inc(mm)
                            for mc in range(DC):
                                mp = mc % 2
                                eng.wait_ge(t3.h, T0 + 16 + 2 * mc)
                                mm = eng.copy(yf[mp][:], psY[mp][:])
                                s3.inc(mm)
                        cond(blk.scalar, e, c, f_S)

                        def f_G1(eng, e=e, c=c, cp=cp, r=r, V0=V0):
                            eng.wait_ge(v3.h, V0 + 25)
                            if e >= 1 and c == 0:
                                eng.wait_ge(gsc3p[0].h, 64 * (e * NCH // 2))
                                eng.wait_ge(gsc3p[1].h, 64 * (e * NCH // 2))
                            for sub in range(4):
                                gsc3p[cp].inc(eng.indirect_dma_start(
                                    x1s[:],
                                    bass.IndirectOffsetOnAxis(ap=yidxu[cp][:, sub:sub + 1], axis=0),
                                    ydst[cp][:, sub, :], None,
                                    compute_op=AluOpType.add), 16)
                        def f_G1_else(eng, e=e, c=c, cp=cp, r=r):
                            if e >= 1 and c == 0:
                                eng.wait_ge(gsc3p[0].h, 64 * (e * NCH // 2))
                                eng.wait_ge(gsc3p[1].h, 64 * (e * NCH // 2))
                            for sub in range(4):
                                eng.dma_start(ddumg[0:1, 4 * r + sub:4 * r + sub + 1],
                                              gathd[0:1, 0:1]).then_inc(gsc3p[cp].h, 16)
                        cond(blk.gpsimd, e, c, f_G1, fn_else=f_G1_else)

                        assert t3.n == T0 + TI and v3.n == V0 + VI and s3.n == S0 + SI

                    ck3[("w13use", e, 0)] = t3.n
                    ck3[("w13use", e, 1)] = t3.n
                    ck3[("w2use", e)] = t3.n

                V0L, S0L = v3.n - VI, s3.n - SI
                ck3[("ab_ev", 0)] = V0L + 16
                ck3[("ab_ev", 1)] = V0L + 15
                ck3[("sgrp_last",)] = S0L + 3
                ck3[("hgbfree", 0)] = t3.n - TI - 1
                ck3[("yevck", 0)] = S0L + 10
                ck3[("yevck", 1)] = S0L + 11
                ck3[("ytevck",)] = V0L + 24

                for half in range(2):
                    @blk.gpsimd
                    def _(g, half=half):
                        t3.wait(g, ck3[("w13use", 7, half)])
                        hn = 6 if half == 0 else 5
                        yW13[half].inc(g.dma_start(
                            w13e[half][:, 0:hn * 2048], w13_ap(8, half)), 16)
                        ck3[("w13ld", 8, half)] = yW13[half].n

                @blk.gpsimd
                def _(g):
                    t3.wait(g, ck3[("w2use", 7)])
                    yW2.inc(g.dma_start(w2e[:], w2_in[8]), 16)
                    ck3[("w2ld", 8)] = yW2.n

                for f in range(NQ):
                    @blk.sync
                    def _(sy, f=f):
                        if f >= 1:
                            t3.wait(sy, ck3[("ymmB", f - 1, DC - 1)])
                        yRwF.inc(sy.dma_start(
                            rwT_f[:], bass.AP(rws.tensor, f * 512, [[T, 8], [1, 512]])), 16)
                        ck3[("rwfld", f)] = yRwF.n
                        if f >= 1:
                            t3.wait(sy, ck3[("sAdoneB", f - 1)])
                        yXn2f.inc(sy.dma_start(xn2f[:], xn2s[f]), 16)
                        if f == 0:
                            gsc3p[0].wait(sy)
                            gsc3p[1].wait(sy)
                        if f >= 1:
                            v3.wait(sy, ck3[("ost", f - 1)])
                        yX1t.inc(sy.dma_start(
                            x1t[:],
                            bass.AP(x1s.tensor, f * 512 * DM,
                                    [[DM, 128], [128 * DM, 4], [1, DM]])), 16)

                    for hc in range(HC):
                        hp = hc % 2
                        slot = hc % 4
                        grp = hc // 4
                        half = 0 if hc < 6 else 1
                        h0 = 0 if hc < 6 else 6

                        @blk.tensor
                        def _(tt, f=f, hc=hc, hp=hp, half=half, h0=h0):
                            if hc == 0:
                                if f == 0:
                                    yW13[0].wait(tt, ck3[("w13ld", 8, 0)])
                                    yW13[1].wait(tt, ck3[("w13ld", 8, 1)])
                                yXn2f.wait(tt)
                            if ("ab_evB", hp) in ck3:
                                v3.wait(tt, ck3[("ab_evB", hp)])
                            elif f == 0 and hc < 2 and ("ab_ev", hp) in ck3:
                                v3.wait(tt, ck3[("ab_ev", hp)])
                            base = (hc - h0) * 2048
                            for dc in range(DC):
                                tt.matmul(psA3[hp][:],
                                          w13e[half][:, base + dc * 128:base + dc * 128 + 128],
                                          xn2f[:, dc, :],
                                          start=(dc == 0), stop=(dc == DC - 1))
                            for dc in range(DC):
                                mm = tt.matmul(psB3[hp][:],
                                               w13e[half][:, base + 1024 + dc * 128:base + 1024 + dc * 128 + 128],
                                               xn2f[:, dc, :],
                                               start=(dc == 0), stop=(dc == DC - 1))
                            t3.inc(mm)
                            ck3[("abB", hp)] = t3.n
                            if hc == HC - 1:
                                ck3[("sAdoneB", f)] = t3.n

                        @blk.vector
                        def _(v, f=f, hc=hc, hp=hp, slot=slot, grp=grp):
                            t3.wait(v, ck3[("abB", hp)])
                            if ("sgrpB", grp - 1) in ck3:
                                s3.wait(v, ck3[("sgrpB", grp - 1)])
                            elif grp == 0 and ("sgrpB_last",) in ck3:
                                s3.wait(v, ck3[("sgrpB_last",)])
                            elif grp == 0 and ("sgrp_last",) in ck3 and f == 0:
                                s3.wait(v, ck3[("sgrp_last",)])
                            v.tensor_scalar(sA[hp][:], psA3[hp][:],
                                            s_b13[:, 8, hc, 0:1], None, AluOpType.add)
                            v.drain()
                            mm = v.scalar_tensor_tensor(
                                sProd[:, slot, :], psB3[hp][:], s_b13[:, 8, hc, 1:2],
                                sA[hp][:], AluOpType.add, AluOpType.mult)
                            v3.inc(mm)
                            ck3[("ab_evB", hp)] = v3.n
                            ck3[("prB", hc)] = v3.n

                        if hc == [g1 - 1 for g0, g1 in SGRP if g0 <= hc < g1][0]:
                            gi = [i for i, (g0, g1) in enumerate(SGRP) if g0 <= hc < g1][0]
                            g0_, g1_ = SGRP[gi]

                            @blk.scalar
                            def _(s, f=f, gi=gi, g0_=g0_, g1_=g1_):
                                v3.wait(s, ck3[("prB", g1_ - 1)])
                                ng = g1_ - g0_
                                if f >= 1 and gi == 0:
                                    t3.wait(s, ck3[("ymmB", f - 1, DC - 1)])
                                elif f == 0 and gi == 0 and ("hgbfree", 0) in ck3:
                                    t3.wait(s, ck3[("hgbfree", 0)])
                                mm = s.activation(
                                    hgb[0][:, g0_:g1_, :].rearrange("p g d -> p (g d)"),
                                    sProd[:, 0:ng, :].rearrange("p g d -> p (g d)"),
                                    AF.Silu)
                                s3.inc(mm)
                                ck3[("sgrpB", gi)] = s3.n
                                ck3[("sgrpB_last",)] = s3.n
                                if g1_ == HC:
                                    ck3[("hrdyB", f)] = s3.n

                    for mc in range(DC):
                        mp = mc % 2

                        @blk.tensor
                        def _(tt, f=f, mc=mc, mp=mp):
                            s3.wait(tt, ck3[("hrdyB", f)])
                            if mc == 0:
                                if f == 0:
                                    cast_wait(tt, "b2")
                                yRwF.wait(tt, ck3[("rwfld", f)])
                            if ("yevB", mp) in ck3:
                                s3.wait(tt, ck3[("yevB", mp)])
                            elif f == 0 and mc < 2 and ("yevck", mp) in ck3:
                                s3.wait(tt, ck3[("yevck", mp)])
                            yW2.wait(tt, ck3[("w2ld", 8)])
                            for hc in range(HC):
                                tt.matmul(psY[mp][:], w2e[:, hc, mc, :], hgb[0][:, hc, :],
                                          start=(hc == 0), stop=False)
                            mm = tt.matmul(psY[mp][:], s_b2[:, mc * 128:(mc + 1) * 128],
                                           rwT_f[:], start=False, stop=True)
                            t3.inc(mm)
                            ck3[("ymmB", f, mc)] = t3.n

                        @blk.scalar
                        def _(s, f=f, mc=mc, mp=mp):
                            t3.wait(s, ck3[("ymmB", f, mc)])
                            s3.inc(s.activation(yf[mp][:], psY[mp][:], AF.Identity,
                                                bias=s_sb2[:, mc:mc + 1]))
                            ck3[("yevB", mp)] = s3.n

                        @blk.tensor
                        def _(tt, f=f, mc=mc, mp=mp):
                            s3.wait(tt, ck3[("yevB", mp)])
                            if ("ytevB",) in ck3:
                                v3.wait(tt, ck3[("ytevB",)])
                            elif ("ytevck",) in ck3 and f == 0 and mc == 0:
                                v3.wait(tt, ck3[("ytevck",)])
                            for sub in range(4):
                                mm = tt.transpose(psYT[:, sub * 128:(sub + 1) * 128],
                                                  yf[mp][:, sub * 128:(sub + 1) * 128],
                                                  s_identf[:])
                            t3.inc(mm)
                            ck3[("ytB", f, mc)] = t3.n

                        @blk.vector
                        def _(v, f=f, mc=mc):
                            t3.wait(v, ck3[("ytB", f, mc)])
                            yX1t.wait(v, 16 * (f + 1))
                            k_out = f * DC + mc
                            if k_out >= 1:
                                yOut.wait(v, 64 * k_out)
                            v3.inc(v.tensor_tensor(
                                ost[:],
                                psYT[:].rearrange("p (s d) -> p s d", s=4),
                                bass.AP(x1t.ap().tensor, x1t.ap().offset + mc * 128,
                                        [x1t.ap().ap[0], [DM, 4], [1, 128]]),
                                AluOpType.add))
                            ck3[("ytevB",)] = v3.n
                            if mc == DC - 1:
                                ck3[("ost", f)] = v3.n

                        @blk.sync
                        def _(sy, f=f, mc=mc):
                            v3.wait(sy, ck3[("ytevB",)])
                            for sub in range(4):
                                yOut.inc(sy.dma_start(
                                    out_d[f * 512 + sub * 128:f * 512 + (sub + 1) * 128,
                                          mc * 128:(mc + 1) * 128],
                                    ost[:, sub, :]), 16)

                @blk.sync
                def _(sy):
                    yOut.wait(sy)
                    for s_ in list(sems.values()) + list(cast_sems.values()):
                        if s_.n:
                            sy.wait_ge(s_.h, s_.n)

    return nc
    return nc



def _prep_inputs(T, ins):
    NT = T // 128
    f32 = np.float32
    w1, w3, w2 = ins["L0_ffn_w1"], ins["L0_ffn_w3"], ins["L0_ffn_w2"]
    sw1, sw3, sw2 = ins["L0_ffn_sw1"], ins["L0_ffn_sw3"], ins["L0_ffn_sw2"]
    b1, b3, b2 = ins["L0_ffn_b1"], ins["L0_ffn_b3"], ins["L0_ffn_b2"]
    sb1, sb3, sb2 = ins["L0_ffn_sb1"], ins["L0_ffn_sb3"], ins["L0_ffn_sb2"]

    w1a = np.concatenate([w1, sw1[None]], 0)
    w3a = np.concatenate([w3, sw3[None]], 0)
    w13 = np.stack([w1a, w3a], 1)
    w13 = w13.reshape(NE, 2, DC, 128, HC, 128)
    w13 = np.ascontiguousarray(w13.transpose(0, 4, 3, 1, 2, 5))

    w2a = np.concatenate([w2, sw2[None]], 0)
    w2a = w2a.reshape(NE, HC, 128, DC, 128)
    w2a = np.ascontiguousarray(w2a.transpose(0, 2, 1, 3, 4))

    b1a = np.concatenate([b1, sb1[None]], 0)
    b3a = np.concatenate([b3, sb3[None]], 0)
    b13 = np.stack([b1a, b3a], -1).reshape(NE, HC, 128, 2)
    b13 = np.ascontiguousarray(b13.transpose(2, 0, 1, 3))

    gate_w = ins["L0_ffn_gate_w"]
    gwt = np.ascontiguousarray(gate_w.T.reshape(DC, 128, 8).transpose(1, 0, 2))
    gbb = np.ascontiguousarray(
        np.broadcast_to(np.asarray(ins["L0_ffn_gate_b"], f32)[None, :], (128, 8)))

    pos = np.arange(T, dtype=np.float64)
    freqs = np.power(10000.0, -np.arange(64, dtype=np.float64) / 64.0)
    rad = pos[:, None] * freqs[None, :]
    cosT = np.cos(rad).astype(ml_dtypes.bfloat16).reshape(NT, 128, 64)
    sinT = np.sin(rad).astype(ml_dtypes.bfloat16).reshape(NT, 128, 64)

    j = np.arange(128)[:, None]
    qq = np.arange(512)[None, :]
    masks = np.stack([(qq >= j + 128 * k) for k in range(4)], 0).astype(ml_dtypes.bfloat16)
    masks = np.ascontiguousarray(masks.transpose(1, 0, 2))

    gzt = np.zeros((4, 128, 1040), ml_dtypes.bfloat16)
    for k in range(4):
        gzt[k, :, 1032] = ml_dtypes.bfloat16(float(NT + k))
    gzt[:, :, 1033] = np.arange(128, dtype=np.float32).astype(
        ml_dtypes.bfloat16)[None, :]
    gzt = gzt.reshape(512, 1040)

    ccols = np.zeros((128, 6), f32)
    ccols[:, 0] = EPS_LN
    ccols[:, 1] = EPS_RMS
    ccols[:, 2] = -1.0
    ccols[:, 3] = NEG_BIG
    ccols[:, 4] = 128.0
    ccols[:, 5] = 511.0

    return {
        "wq": np.ascontiguousarray(ins["L0_attn_wq"].reshape(DM, DM)),
        "wk": np.ascontiguousarray(ins["L0_attn_wk"].reshape(DM, DM)),
        "wv": np.ascontiguousarray(ins["L0_attn_wv"].reshape(DM, DM)),
        "wo": np.ascontiguousarray(ins["L0_attn_wo"]),
        "w13": w13, "w2": w2a, "b13": b13,
        "b2": np.ascontiguousarray(b2),
        "sb2c": np.ascontiguousarray(np.asarray(sb2, f32).reshape(DC, 128).T),
        "gwt": gwt, "gbb": gbb,
        "cosT": cosT, "sinT": sinT,
        "identb": np.eye(128, dtype=ml_dtypes.bfloat16),
        "identf": np.eye(128, dtype=f32),
        "ones1b": np.ones((1, 128), ml_dtypes.bfloat16),
        "onescb": np.ones((128, 1), ml_dtypes.bfloat16),
        "ones1f": np.ones((1, 128), f32),
        "masksb": masks,
        "ccols": ccols,
        "triu": np.triu(np.ones((128, 128))).astype(ml_dtypes.bfloat16),
        "pidx": np.arange(128, dtype=np.float32).astype(ml_dtypes.bfloat16).reshape(128, 1),
        "bigc": np.full((128, 8), 1.0e9, np.float32),
        "eofs": np.ascontiguousarray(
            np.broadcast_to((np.arange(8, dtype=np.float32) * (T + 512))[None, :],
                            (128, 8))),
        "gzt": gzt,
        "zrow": np.zeros((128, DM), f32),
    }


def kernel(**ins):
    x = np.asarray(ins["L0_attn_x"])
    B, T, _ = x.shape
    base = _prep_inputs(T, {k: np.asarray(v) for k, v in ins.items()})
    nc = build_nc(T)
    in_maps = []
    for b in range(B):
        m = dict(base)
        m["x"] = np.ascontiguousarray(x[b])
        in_maps.append(m)
    res = run_bass_kernel_spmd(nc, in_maps, core_ids=list(range(B)))
    return np.stack([np.asarray(r["out"]) for r in res.results], 0)
